# version 37
# baseline (speedup 1.0000x reference)
import contextlib
import os
import numpy as np
import ml_dtypes
import concourse.bass as bass
import concourse.mybir as mybir
from concourse.bass_utils import run_bass_kernel_spmd

F32 = mybir.dt.float32
F16 = mybir.dt.float16
BF16 = mybir.dt.bfloat16
AF = mybir.ActivationFunctionType
ALU = mybir.AluOpType
AXX = mybir.AxisListType.X

D = 1024
NH = 16
NE = 8
EPS = 1e-5

ENGINES = ("tensor", "vector", "scalar", "gpsimd", "sync")


class Mark:
    __slots__ = ("sem_name", "value")
    def __init__(self, sem_name, value):
        self.sem_name = sem_name
        self.value = value


class Stream:
    def __init__(self, prog, name, queue, nsems):
        self.prog = prog
        self.name = name
        self.queue = queue
        self.nsems = nsems
        self.counts = [0] * nsems
        self.i = 0
        for k in range(nsems):
            prog._ensure_sem(f"{name}{k}")

    def dma(self, fn, waits=()):
        k = self.i % self.nsems
        self.counts[k] += 16
        sem_name = f"{self.name}{k}"
        self.i += 1
        sems = self.prog.sems
        def op(e, fn=fn, sem_name=sem_name):
            return fn(e).then_inc(sems[sem_name], 16)
        self.prog._emit(self.queue, op, waits=waits)
        return Mark(sem_name, self.counts[k])

    def final_marks(self):
        return [Mark(f"{self.name}{k}", self.counts[k])
                for k in range(self.nsems) if self.counts[k] > 0]


class Prog:
    def __init__(self, nc):
        self.nc = nc
        self.ops = {e: [] for e in ENGINES}
        self.sems = {}
        self._stack = contextlib.ExitStack()
        self.counts = {e: 0 for e in ENGINES}
        self.waited = {e: {} for e in ENGINES}
        for e in ENGINES:
            self._ensure_sem("m_" + e)

    def _ensure_sem(self, name):
        if name not in self.sems:
            self.sems[name] = self._stack.enter_context(self.nc.semaphore(name))
        return self.sems[name]

    def stream(self, name, queue, nsems):
        return Stream(self, name, queue, nsems)

    def _emit(self, eng, op, waits=()):
        needed = []
        own_dep = False
        w = self.waited[eng]
        for mk in waits:
            if mk is None:
                continue
            if mk.sem_name == "m_" + eng:
                own_dep = True
                continue
            if w.get(mk.sem_name, -1) >= mk.value:
                continue
            w[mk.sem_name] = mk.value
            needed.append((mk.sem_name, mk.value))
        if needed or own_dep:
            sems = self.sems
            def full(e, needed=needed, op=op, own_dep=own_dep):
                for sem_name, value in needed:
                    e.wait_ge(sems[sem_name], value)
                if own_dep:
                    e.drain()
                return op(e)
            self.ops[eng].append(full)
        else:
            self.ops[eng].append(op)

    def do(self, eng, fn, waits=(), mark=False):
        mk = None
        if mark:
            self.counts[eng] += 1
            if eng == "gpsimd":
                name = f"mg{self.counts[eng]}"
                sem = self._ensure_sem(name)
                mk = Mark(name, 1)
            else:
                name = "m_" + eng
                sem = self.sems[name]
                mk = Mark(name, self.counts[eng])
            def op(e, fn=fn, sem=sem):
                return fn(e).then_inc(sem, 1)
        else:
            op = fn
        self._emit(eng, op, waits=waits)
        return mk

    _bar_n = 0

    def barrier(self, engines=ENGINES, extra=()):
        Prog._bar_n += 1
        marks = []
        for e in engines:
            name = f"bar{Prog._bar_n}_{e}"
            sem = self._ensure_sem(name)
            self._emit(e, lambda en, sem=sem: en.nop().then_inc(sem, 1))
            marks.append(Mark(name, 1))
        marks = marks + list(extra)
        for e in engines:
            self.do(e, lambda en: en.nop(), waits=marks)

    def finalize(self, block):
        for eng in ENGINES:
            ops = self.ops[eng]
            if not ops:
                continue
            def make(ops):
                def body(e):
                    for op in ops:
                        op(e)
                return body
            getattr(block, eng)(make(ops))

    def close(self):
        self._stack.close()


def _gammas():
    lin = np.linspace(np.log(1.0 / 512.0), np.log(1.0 / 32.0), NH)
    return 1.0 - np.exp(lin)


def build(nj=4):
    STAGE = int(os.environ.get("KSTAGE", "99"))
    nc = bass.Bass("TRN2", target_bir_lowering=False)
    ST = nj * 512
    nt = nj * 4

    dx = nc.dram_tensor("x", [D, ST], F32, kind="ExternalInput").ap()
    dwq = nc.dram_tensor("wq", [128, 512], F32, kind="ExternalInput").ap()
    dwk = nc.dram_tensor("wk", [128, 512], F32, kind="ExternalInput").ap()
    dwv = nc.dram_tensor("wv", [128, 512], F32, kind="ExternalInput").ap()
    dw1a = nc.dram_tensor("w1a", [D, D], F32, kind="ExternalInput").ap()
    dw2a = nc.dram_tensor("w2a", [D, D], F32, kind="ExternalInput").ap()
    dgns = nc.dram_tensor("gns", [128, 8], F32, kind="ExternalInput").ap()
    dgnb = nc.dram_tensor("gnb", [128, 8], F32, kind="ExternalInput").ap()
    dlng = nc.dram_tensor("lng", [128, 8], F32, kind="ExternalInput").ap()
    drout = nc.dram_tensor("rout", [D, 9], F32, kind="ExternalInput").ap()
    dwe1 = nc.dram_tensor("we1", [NE, D, D], F32, kind="ExternalInput").ap()
    dwe2 = nc.dram_tensor("we2", [NE, D, D], F32, kind="ExternalInput").ap()
    dwgu = nc.dram_tensor("wgu", [D, 2 * D], F32, kind="ExternalInput").ap()
    dwdn = nc.dram_tensor("wdn", [D, D], F32, kind="ExternalInput").ap()
    dsel16 = nc.dram_tensor("sel16", [128, 128], BF16, kind="ExternalInput").ap()
    dselb = nc.dram_tensor("selb", [16, 1024], BF16, kind="ExternalInput").ap()
    dones1 = nc.dram_tensor("ones1", [1, 128], BF16, kind="ExternalInput").ap()
    donesc = nc.dram_tensor("onesc", [128, 1], BF16, kind="ExternalInput").ap()
    dmasks = nc.dram_tensor("masks", [128, 1280], BF16, kind="ExternalInput").ap()
    dgmn4 = nc.dram_tensor("gmn4", [128, 64], F32, kind="ExternalInput").ap()
    deye = nc.dram_tensor("eye128", [128, 128], BF16, kind="ExternalInput").ap()
    dout = nc.dram_tensor("out", [D, ST], F32, kind="ExternalOutput").ap()
    dx1s = nc.dram_tensor("x1s", [D, ST], BF16).ap()

    gam = _gammas()

    st = contextlib.ExitStack()
    sb = lambda name, shape, dt: st.enter_context(nc.sbuf_tensor(name, shape, dt))
    ps = lambda name, shape: st.enter_context(nc.psum_tensor(name, shape, F32))

    x2st = sb("x2st", [128, 8 * ST], BF16)
    wqkv = sb("wqkv", [128, 1536], BF16)
    gns_sb = sb("gns_sb", [128, 8], F32)
    gnb_sb = sb("gnb_sb", [128, 8], F32)
    lng_sb = sb("lng_sb", [128, 8], F32)
    sel16_sb = sb("sel16_sb", [128, 128], BF16)
    selb_sb = sb("selb_sb", [16, 1024], BF16)
    ones1_sb = sb("ones1_sb", [1, 128], BF16)
    onesc_sb = sb("onesc_sb", [128, 1], BF16)
    eps_sb = sb("eps_sb", [1, 1], F32)
    masks_sb = sb("masks_sb", [128, 1280], BF16)
    gmn4_sb = sb("gmn4_sb", [128, 64], F32)
    lgb = sb("lgb", [128, 320], F32)
    bcmu = sb("bcmu", [128, 512], BF16)
    bcrs = sb("bcrs", [128, 512], BF16)
    combT = sb("combT", [32, ST], BF16)
    eye_sb = sb("eye_sb", [128, 128], BF16)
    rout_sb = sb("rout_sb", [128, 72], BF16)
    iotaq = sb("iotaq", [128, 512], F16)
    iotak = sb("iotak", [128, 512], F16)
    stat_f = sb("stat_f", [16, 1024], F32)
    scrF = sb("scrF", [16, 1024], F32)
    stat_b = sb("stat_b", [16, 1024], BF16)
    scr16 = sb("scr16", [128, 4 * 512], BF16)
    bcomb = sb("bcomb", [128, 512], BF16)
    comb_scr = sb("comb_scr", [128, 96], F32)
    comb_b = sb("comb_b", [128, 32], BF16)
    combR = sb("combR", [1, 1024], BF16)

    P = [ps(f"P{k}", [128, 512]) for k in range(8)]

    p = Prog(nc)
    ldx = p.stream("ldx", "sync", 2)
    ldc = p.stream("ldc", "sync", 10)
    wst = p.stream("wst", "gpsimd", 8)
    spl = p.stream("spl", "sync", 2)
    ost = p.stream("ost", "sync", 2)
    cbr = p.stream("cbr", "sync", 2)
    ldg = p.stream("ldg", "gpsimd", 2)

    with nc.Block() as block:
        m_gns = ldc.dma(lambda e: e.dma_start(out=gns_sb[:], in_=dgns[:, :]))
        m_gnb = ldc.dma(lambda e: e.dma_start(out=gnb_sb[:], in_=dgnb[:, :]))
        m_lng = ldc.dma(lambda e: e.dma_start(out=lng_sb[:], in_=dlng[:, :]))
        m_sel16 = ldc.dma(lambda e: e.dma_start(out=sel16_sb[:], in_=dsel16[:, :]))
        m_selb = ldc.dma(lambda e: e.dma_start(out=selb_sb[:], in_=dselb[:, :]))
        m_ones1 = ldc.dma(lambda e: e.dma_start(out=ones1_sb[:], in_=dones1[:, :]))
        m_onesc = ldc.dma(lambda e: e.dma_start(out=onesc_sb[:], in_=donesc[:, :]))
        m_eps = p.do("vector", lambda e: e.memset(eps_sb[0:1, :], EPS), mark=True)
        m_masks = ldc.dma(lambda e: e.dma_start(out=masks_sb[:], in_=dmasks[:, :]))
        m_gmn4 = ldc.dma(lambda e: e.dma_start(out=gmn4_sb[:], in_=dgmn4[:, :]))
        m_eye = ldc.dma(lambda e: e.dma_start(out=eye_sb[:], in_=deye[:, :]))
        m_wq = wst.dma(lambda e: e.dma_start(out=wqkv[:, 0:512], in_=dwq[:, :]))
        m_wk = wst.dma(lambda e: e.dma_start(out=wqkv[:, 512:1024], in_=dwk[:, :]))
        m_wv = wst.dma(lambda e: e.dma_start(out=wqkv[:, 1024:1536], in_=dwv[:, :]))
        m_rout = wst.dma(lambda e: e.dma_start(
            out=rout_sb[:, 0:72].rearrange("p (k f) -> p k f", k=8),
            in_=drout[:, :].rearrange("(k p) f -> p k f", p=128)))
        m_ioq = p.do("gpsimd", lambda e: e.iota(
            iotaq[:, :], pattern=[[1, 512]], base=0, channel_multiplier=0,
            allow_small_or_imprecise_dtypes=True), mark=True)
        m_iok = p.do("gpsimd", lambda e: e.iota(
            iotak[:, :], pattern=[[0, 4], [-1, 128]], base=0, channel_multiplier=0,
            allow_small_or_imprecise_dtypes=True), mark=True)
        with contextlib.ExitStack() as ascope:
            asb = lambda name, shape, dt: ascope.enter_context(
                nc.sbuf_tensor(name, shape, dt))
            xj = asb("xj", [128, 8192], BF16)
            bgall = asb("bgall", [128, 16 * 512], BF16)
            ln1_f = asb("ln1_f", [1, 1024], F32)
            ln1scr = asb("ln1scr", [1, 1024], F32)
            ln1_b = asb("ln1_b", [1, 1024], BF16)
            xlnj = asb("xlnj", [128, 8192], BF16)
            qt = asb("qt", [128, 1024], BF16)
            kst = asb("kst", [128, 4096], BF16)
            vst = asb("vst", [128, 4096], BF16)
            ktok = asb("ktok", [128, 1024], BF16)
            Ms_f = asb("Ms_f", [128, 512], F32)
            Ms_b = asb("Ms_b", [128, 512], BF16)
            gatej = asb("gatej", [128, 4096], BF16)
            retj = asb("retj", [128, 4096], BF16)
            s_sb = asb("s_sb", [128, 4096], BF16)
            wscope1 = contextlib.ExitStack()
            w1a_sb = wscope1.enter_context(
                nc.sbuf_tensor("w1a_sb", [128, 8192], BF16, side="right"))
            w2a_sb = wscope1.enter_context(
                nc.sbuf_tensor("w2a_sb", [128, 8192], BF16, side="right"))

            def gen_bg(e):
                last = None
                for c in range(8):
                    e.activation(bgall[:, 1024 * c:1024 * c + 512], iotaq[:, :],
                                 AF.Exp, scale=lng_sb[:, c:c + 1])
                    last = e.activation(bgall[:, 1024 * c + 512:1024 * c + 1024],
                                        iotak[:, :], AF.Exp, scale=lng_sb[:, c:c + 1])
                return last
            m_bgall = p.do("scalar", gen_bg, waits=[m_ioq, m_iok, m_lng], mark=True)

            m_w1a = wst.dma(lambda e: e.dma_start(
                out=w1a_sb[:, :].rearrange("p (k f) -> p k f", k=8),
                in_=dw1a[:, :].rearrange("(k p) f -> p k f", p=128)))
            m_w2a = wst.dma(lambda e: e.dma_start(
                out=w2a_sb[:, :].rearrange("p (k f) -> p k f", k=8),
                in_=dw2a[:, :].rearrange("(k p) f -> p k f", p=128)))

            def load_x(j, waits):
                xo = 4096 * (j % 2)
                return ldg.dma(lambda e, j=j, xo=xo: e.dma_start(
                    out=xj[:, xo:xo + 4096].rearrange("p (c s) -> p c s", c=8),
                    in_=dx[:, :].rearrange("(c p) s -> p c s", p=128)
                        [:, :, 512 * j:512 * j + 512]), waits=waits)

            m_x = load_x(0, [])
            m_x_next = load_x(1, []) if nj > 1 else None

            def ln_partA(m_src, stA, stB, xoff=0, bufs=None, first_waits=()):
                st_f, sc_F, st_b = bufs if bufs is not None else (stat_f, scrF, stat_b)
                prev_mm = [None, None]
                mm_stat = None
                for c in range(8):
                    sl = (c % 2) * 512
                    srcw = m_src[c] if isinstance(m_src, list) else m_src
                    msq = p.do("scalar", lambda e, c=c, sl=sl: e.activation(
                        scr16[:, sl:sl + 512],
                        xj[:, xoff + 512 * c:xoff + 512 * c + 512],
                        AF.Square), waits=[srcw, prev_mm[c % 2]], mark=True)
                    def statmm(e, c=c, sl=sl):
                        e.matmul(stA[0:1, :], onesc_sb[:, 0:1],
                                 xj[:, xoff + 512 * c:xoff + 512 * c + 512],
                                 start=(c == 0), stop=(c == 7))
                        return e.matmul(stB[0:1, :], onesc_sb[:, 0:1],
                                        scr16[:, sl:sl + 512],
                                        start=(c == 0), stop=(c == 7))
                    mm_stat = p.do("tensor", statmm,
                                   waits=[srcw, msq, m_onesc] +
                                         (list(first_waits) if c == 0 else []),
                                   mark=True)
                    prev_mm[c % 2] = mm_stat
                return ln_tail(stA, stB, xoff, (st_f, sc_F, st_b), mm_stat)

            def ln_tail(stA, stB, xoff, bufs, mm_stat):
                st_f, sc_F, st_b = bufs
                m_m2 = p.do("scalar", lambda e: e.activation(
                    sc_F[0:1, 0:512], stA[0:1, :], AF.Square),
                    waits=[mm_stat], mark=True)
                m_var = p.do("vector", lambda e: e.scalar_tensor_tensor(
                    st_f[0:1, 512:1024], sc_F[0:1, 0:512], -1.0, stB[0:1, :],
                    ALU.mult, ALU.add), waits=[m_m2], mark=True)
                def fin(e):
                    e.activation(sc_F[0:1, 512:1024], st_f[0:1, 512:1024],
                                 AF.Sqrt, bias=eps_sb[0:1, 0:1])
                    e.drain()
                    e.copy(st_b[0:1, 0:512], stA[0:1, :])
                    return e.copy(st_b[0:1, 512:1024], sc_F[0:1, 512:1024])
                m_fin = p.do("scalar", fin, waits=[m_var, m_eps], mark=True)
                return (st_b, m_fin, xoff, Mark("m_vector", m_var.value))

            def ln_partB(ctx, bcA, bcB, dst, bc_waits=()):
                nonlocal prev_lnapply
                st_b, m_fin, xoff = ctx[:3]
                def bcast(e):
                    e.matmul(bcA[:, :], ones1_sb[0:1, :], st_b[0:1, 0:512],
                             start=True, stop=True)
                    return e.matmul(bcB[:, :], ones1_sb[0:1, :], st_b[0:1, 512:1024],
                                    start=True, stop=True)
                m_bc = p.do("tensor", bcast,
                            waits=[m_fin, m_ones1] + list(bc_waits), mark=True)
                m_evmu = p.do("scalar", lambda e: e.copy(bcmu[:, :], bcA[:, :]),
                              waits=[m_bc, prev_lnapply], mark=True)
                def rec(e):
                    with nc.allow_low_precision(reason="bf16 rstd tile"):
                        return e.reciprocal(bcrs[:, :], bcB[:, :])
                m_rec = p.do("vector", rec, waits=[m_bc], mark=True)
                mu_b = bcmu[:, :].unsqueeze(1).broadcast_to([128, 4, 512])
                rs_b = bcrs[:, :].unsqueeze(1).broadcast_to([128, 4, 512])
                if dst[0] == "ln1":
                    xlo = dst[1]
                    mid = xlnj[:, xlo:xlo + 4096].rearrange(
                        "p (c t) -> p c t", t=512)
                    out3 = mid
                    guard = None
                else:
                    pj, pxo, guard = dst[1], dst[2], dst[3]
                    xoff = pxo
                    mid = xlnj[:, pxo:pxo + 4096].rearrange(
                        "p (c t) -> p c t", t=512)
                    out3 = x2st[:, :].rearrange("p (c t) -> p c t", c=8)[
                        :, :, 512 * pj:512 * pj + 512]
                in3 = xj[:, xoff:xoff + 4096].rearrange("p (c t) -> p c t", t=512)
                def a1(e, h):
                    with nc.allow_low_precision(reason="bf16 LN apply"):
                        return e.tensor_tensor(mid[:, 4 * h:4 * h + 4, :],
                                               in3[:, 4 * h:4 * h + 4, :],
                                               mu_b, ALU.subtract)
                def a2(e, h):
                    with nc.allow_low_precision(reason="bf16 LN apply"):
                        return e.tensor_tensor(out3[:, 4 * h:4 * h + 4, :],
                                               mid[:, 4 * h:4 * h + 4, :],
                                               rs_b, ALU.mult)
                m_a1h = [p.do("gpsimd", lambda e, h=h: a1(e, h),
                              waits=[m_evmu, m_rec, guard] if h == 0 else [],
                              mark=True) for h in (0, 1)]
                marks = [p.do("gpsimd", lambda e, h=h: a2(e, h),
                              waits=[m_a1h[h]], mark=True) for h in (0, 1)]
                prev_lnapply = marks[1]
                return [marks[0]] * 4 + [marks[1]] * 4

            def layernorm(m_src, stA, stB, bcA, bcB, dst, xoff=0,
                          bufs=None, first_waits=()):
                ctx = ln_partA(m_src, stA, stB, xoff=xoff, bufs=bufs,
                               first_waits=first_waits)
                return ln_partB(ctx, bcA, bcB, dst)

            m_spill = None
            prev_lnapply = None
            m_Minit = p.do("vector", lambda e: e.memset(Ms_f[:, :], 0.0), mark=True)
            m_Mcast = [m_Minit] * 8
            m_Mmm = [None] * 8
            m_vst_guard = [None] * 8
            m_kst_guard = [None] * 8
            ln1 = layernorm(m_x, P[6], P[7], P[6], P[7], ("ln1", 0), xoff=0,
                            bufs=(ln1_f, ln1scr, ln1_b))
            m_x_cur = m_x
            m_ln2_prev = []
            m_ln2_fin = None
            pending_ln2B = None
            def emit_ln2B():
                nonlocal pending_ln2B, m_ln2_prev, m_spill, m_x, m_x_next, \
                    m_ln2_fin
                ctx2, pj, pxo, pm_x1 = pending_ln2B
                pending_ln2B = None
                ln2 = ln_partB(ctx2, P[2], P[3],
                               ("ln2", pj, pxo, pm_x1[7]))
                m_ln2_fin = ctx2[1]
                m_ln2_prev = ln2
                m_spill = spl.dma(lambda e, pj=pj, pxo=pxo: e.dma_start(
                    out=dx1s[:, :].rearrange("(c p) s -> p c s", p=128)
                        [:, :, 512 * pj:512 * pj + 512],
                    in_=xj[:, pxo:pxo + 4096].rearrange("p (c s) -> p c s", c=8)),
                    waits=ln2 + pm_x1)
                m_x = m_x_next
                if pj + 2 < nj:
                    m_x_next = load_x(pj + 2, [m_spill] + ln2)
            for j in range(nj):
                xo = 4096 * (j % 2)
                xlo = 4096 * (j % 2)

                if STAGE < 2:
                    break
                m_gate = [None] * 8
                prev_gate_ev = [None, None]
                for m in range(8):
                    bank = P[4] if m % 2 == 0 else P[5]
                    m_g = None
                    for k in range(8):
                        def gmk(e, m=m, k=k, bank=bank, xlo=xlo):
                            return e.matmul(
                                bank[:, :],
                                w1a_sb[:, 1024 * k + 128 * m:1024 * k + 128 * m + 128],
                                xlnj[:, xlo + 512 * k:xlo + 512 * k + 512],
                                start=(k == 0), stop=(k == 7))
                        m_g = p.do("tensor", gmk,
                                   waits=[m_w1a, ln1[k]] +
                                         ([prev_gate_ev[m % 2]] if k == 0 else []),
                                   mark=(k == 7))
                    def gsil(e, m=m, bank=bank):
                        e.activation(gatej[:, 512 * m:512 * m + 512], bank[:, :],
                                     AF.Silu)
                        e.drain()
                        return e.add(gatej[:, 512 * m:512 * m + 512],
                                     gatej[:, 512 * m:512 * m + 512],
                                     gnb_sb[:, m:m + 1])
                    m_gate[m] = p.do("scalar", gsil, waits=[m_g, m_gnb], mark=True)
                    prev_gate_ev[m % 2] = m_gate[m]
                    if m == 1 and pending_ln2B is not None:
                        emit_ln2B()

                if STAGE < 3:
                    break

                m_qkev = [None] * 8
                m_scores_last = [None] * 8
                m_retj_ev = [None] * 8
                m_vev_prev = []
                pending_gnstat = None
                m_ktok_prev = None
                m_ktevA_last = None
                m_slot_prev = [None] * 4
                m_ret_bank_prev = [None, None]
                prev_sev = [(), ()]
                m_sq_prev = [None, None]
                m_gn_stat = None
                for c in range(8):
                    qsl = (c % 2) * 512
                    nA, nB = 2 * c, 2 * c + 1
                    gA, gB = gam[nA], gam[nB]
                    def qkmm(e, c=c, xlo=xlo):
                        x_ap = xlnj[:, xlo + 512 * c:xlo + 512 * c + 512]
                        e.matmul(P[0][0:64, :], wqkv[0:64, 64 * c:64 * c + 64],
                                 x_ap[0:64, :], start=True, stop=True,
                                 tile_position=(0, 0))
                        e.matmul(P[0][64:128, :], wqkv[64:128, 64 * c:64 * c + 64],
                                 x_ap[64:128, :], start=True, stop=True,
                                 tile_position=(64, 64))
                        e.matmul(P[1][0:64, :], wqkv[0:64, 512 + 64 * c:576 + 64 * c],
                                 x_ap[0:64, :], start=True, stop=True,
                                 tile_position=(0, 0))
                        return e.matmul(P[1][64:128, :], wqkv[64:128, 512 + 64 * c:576 + 64 * c],
                                        x_ap[64:128, :], start=True, stop=True,
                                        tile_position=(64, 64))
                    m_qk = p.do("tensor", qkmm,
                                waits=[ln1[c], m_wq, m_wk] +
                                      list(prev_sev[0] or ()) +
                                      list(prev_sev[1] or ()), mark=True)
                    if pending_gnstat is not None:
                        pgf, pgw = pending_gnstat
                        m_gn_stat = p.do("tensor", pgf, waits=pgw, mark=True)
                        m_sq_prev[(c - 1) % 2] = m_gn_stat
                        pending_gnstat = None
                    slot_guard = m_scores_last[c - 2] if c >= 2 else None
                    m_qe = p.do("vector", lambda e, qsl=qsl, c=c: e.tensor_tensor(
                        qt[:, qsl:qsl + 512], P[0][:, :],
                        bgall[:, 1024 * c:1024 * c + 512], ALU.mult),
                        waits=[m_qk, m_bgall, slot_guard], mark=True)
                    m_ke = p.do("vector", lambda e, c=c: e.tensor_copy(
                        kst[:, 512 * c:512 * c + 512], P[1][:, :]),
                        waits=[m_qk, m_kst_guard[c]], mark=True)
                    m_qkev[c] = [m_qe, m_ke]

                    def vmm(e, c=c, xlo=xlo):
                        last = None
                        for u in range(4):
                            x_ap = xlnj[:, xlo + 512 * c + 128 * u:xlo + 512 * c + 128 * u + 128]
                            e.matmul(P[6][:, 128 * u:128 * u + 64], x_ap[0:64, :],
                                     wqkv[0:64, 1024 + 64 * c:1088 + 64 * c],
                                     start=True, stop=True, tile_position=(0, 0))
                            last = e.matmul(P[3][:, 128 * u:128 * u + 64],
                                            x_ap[64:128, :],
                                            wqkv[64:128, 1024 + 64 * c:1088 + 64 * c],
                                            start=True, stop=True,
                                            tile_position=(64, 0))
                        return last
                    m_v = p.do("tensor", vmm,
                               waits=[ln1[c], m_wv] + (m_vev_prev or []) +
                                     list(prev_sev[1] or ()) + m_ln2_prev +
                                     ([m_ln2_fin] if c == 0 else []),
                               mark=True)
                    def vevA(e, c=c):
                        dst = vst[:, :].rearrange("p (u f) -> p u f", f=1024)
                        return e.copy(
                            dst[:, :, 128 * c:128 * c + 64],
                            P[6][:, 0:512].rearrange("p (u f) -> p u f", f=128)[:, :, 0:64])
                    def vevB(e, c=c):
                        dst = vst[:, :].rearrange("p (u f) -> p u f", f=1024)
                        return e.tensor_copy(
                            dst[:, :, 128 * c + 64:128 * c + 128],
                            P[3][:, 0:512].rearrange("p (u f) -> p u f", f=128)[:, :, 0:64])
                    m_vevA = p.do("scalar", vevA,
                                  waits=[m_v, m_vst_guard[c]], mark=True)
                    m_vevB = p.do("vector", vevB,
                                  waits=[m_v, m_vst_guard[c]], mark=True)
                    m_vevc = [m_vevA, m_vevB]
                    m_vev_prev = m_vevc

                    last_j = (j == nj - 1)
                    ksl = (c % 2) * 512
                    if not last_j:
                        def ktmmA(e, c=c, xlo=xlo):
                            last = None
                            for u in range(4):
                                x_ap = xlnj[:, xlo + 512 * c + 128 * u:xlo + 512 * c + 128 * u + 128]
                                last = e.matmul(P[6][:, 128 * u:128 * u + 64],
                                                x_ap[0:64, :],
                                                wqkv[0:64, 512 + 64 * c:576 + 64 * c],
                                                start=True, stop=True,
                                                tile_position=(0, 0))
                            return last
                        def ktmmB(e, c=c, xlo=xlo):
                            last = None
                            for u in range(4):
                                x_ap = xlnj[:, xlo + 512 * c + 128 * u:xlo + 512 * c + 128 * u + 128]
                                last = e.matmul(P[3][:, 128 * u:128 * u + 64],
                                                x_ap[64:128, :],
                                                wqkv[64:128, 512 + 64 * c:576 + 64 * c],
                                                start=True, stop=True,
                                                tile_position=(64, 0))
                            return last
                        m_ktA = p.do("tensor", ktmmA,
                                     waits=[m_vevA, m_ktok_prev], mark=True)
                        m_kt = p.do("tensor", ktmmB, waits=[m_vevB], mark=True)
                        def ktevA(e, c=c, ksl=ksl):
                            last = None
                            for u in range(4):
                                n = 2 * c
                                last = e.mul(
                                    ktok[:, ksl + 128 * u:ksl + 128 * u + 64],
                                    P[6][:, 128 * u:128 * u + 64],
                                    gmn4_sb[:, 4 * n + u:4 * n + u + 1])
                            return last
                        def ktevB(e, c=c, ksl=ksl):
                            last = None
                            for u in range(4):
                                n = 2 * c + 1
                                last = e.tensor_scalar(
                                    ktok[:, ksl + 128 * u + 64:ksl + 128 * u + 128],
                                    P[3][:, 128 * u:128 * u + 64],
                                    gmn4_sb[:, 4 * n + u:4 * n + u + 1],
                                    None, ALU.mult)
                            return last
                        m_ktevA = p.do("scalar", ktevA,
                                       waits=[m_ktA, m_gmn4, m_Mmm[c]], mark=True)
                        m_ktevB = p.do("vector", ktevB,
                                       waits=[m_kt, m_gmn4, m_Mmm[c]], mark=True)
                        m_ktev = m_ktevB
                        m_ktevA_last = m_ktevA
                        m_ktok_prev = m_ktevA

                    if STAGE < 4:
                        continue
                    ret_free = [m_gate[6], m_gate[7], m_ret_bank_prev[0],
                                m_ret_bank_prev[1]]
                    m_sev = [None] * 4
                    m_ret_i = [None] * 4
                    m_s = None
                    pending_rmm = None
                    if j > 0:
                        def crossmm(e, c=c, qsl=qsl):
                            e.matmul(P[4][0:64, :], Ms_b[0:64, 64 * c:64 * c + 64],
                                     qt[0:64, qsl:qsl + 512], start=True, stop=False,
                                     tile_position=(0, 0), skip_group_check=True)
                            return e.matmul(P[5][64:128, :],
                                            Ms_b[64:128, 64 * c:64 * c + 64],
                                            qt[64:128, qsl:qsl + 512], start=True,
                                            stop=False, tile_position=(64, 64),
                                            skip_group_check=True)
                        m_cross = p.do("tensor", crossmm,
                                       waits=m_qkev[c] + [m_Mcast[c]] + ret_free,
                                       mark=True)
                    if False:
                        def mupd(e, c=c, gA=gA, gB=gB):
                            e.tensor_add(Ms_f[0:64, 64 * c:64 * c + 64],
                                         Ms_f[0:64, 64 * c:64 * c + 64],
                                         P[5][0:64, 0:64])
                            e.tensor_add(Ms_f[64:128, 64 * c:64 * c + 64],
                                         Ms_f[64:128, 64 * c:64 * c + 64],
                                         P[4][64:128, 0:64])
                            e.drain()
                            e.tensor_scalar(Ms_f[0:64, 64 * c:64 * c + 64],
                                            Ms_f[0:64, 64 * c:64 * c + 64],
                                            float(gA ** 512), None, ALU.mult)
                            e.tensor_scalar(Ms_f[64:128, 64 * c:64 * c + 64],
                                            Ms_f[64:128, 64 * c:64 * c + 64],
                                            float(gB ** 512), None, ALU.mult)
                            e.drain()
                            return e.tensor_copy(Ms_b[:, 64 * c:64 * c + 64],
                                                 Ms_f[:, 64 * c:64 * c + 64])
                        m_Mcast[c] = p.do("vector", mupd,
                                          waits=[m_kv, m_Minit,
                                                 m_cross if j > 0 else None],
                                          mark=True)
                    for kx in range(4):
                        pa = P[0] if kx % 2 == 0 else P[2]
                        pb = P[1] if kx % 2 == 0 else P[3]
                        if kx >= 2:
                            bank_free = list(m_sev[kx - 2])
                        elif kx == 0:
                            bank_free = m_qkev[c] + list(prev_sev[0] or ())
                        else:
                            bank_free = ([m_ktevB] if not last_j else [m_vevB]) + \
                                        list(prev_sev[1] or ())
                        def smmA(e, c=c, kx=kx, pa=pa, qsl=qsl):
                            return e.matmul(
                                pa[:, :],
                                kst[0:64, 512 * c + 128 * kx:512 * c + 128 * kx + 128],
                                qt[0:64, qsl:qsl + 512], start=True, stop=True,
                                tile_position=(0, 0))
                        def smmB(e, c=c, kx=kx, pb=pb, qsl=qsl):
                            return e.matmul(
                                pb[:, :],
                                kst[64:128, 512 * c + 128 * kx:512 * c + 128 * kx + 128],
                                qt[64:128, qsl:qsl + 512], start=True,
                                stop=True, tile_position=(64, 0))
                        m_sA = p.do("tensor", smmA,
                                    waits=m_qkev[c] +
                                          [m_ret_i[kx - 2] if kx >= 2 else None] +
                                          bank_free,
                                    mark=True)
                        m_s = p.do("tensor", smmB, mark=True)
                        ssl = kx * 1024
                        slot_free = m_slot_prev[kx]
                        col0 = 128 * kx
                        moff = (0, 512, 896, 1152)[kx]
                        w = 512 - col0
                        def sevA(e, pa=pa, ssl=ssl, kx=kx, nA=nA, col0=col0,
                                 moff=moff, w=w):
                            return e.scalar_tensor_tensor(
                                s_sb[:, ssl + col0:ssl + 512],
                                pa[:, col0:512],
                                gmn4_sb[:, 4 * nA + kx:4 * nA + kx + 1],
                                masks_sb[:, moff:moff + w], ALU.mult, ALU.mult)
                        def sevB(e, pb=pb, ssl=ssl, kx=kx, nB=nB, col0=col0,
                                 moff=moff, w=w):
                            return e.scalar_tensor_tensor(
                                s_sb[:, ssl + 512 + col0:ssl + 1024],
                                pb[:, col0:512],
                                gmn4_sb[:, 4 * nB + kx:4 * nB + kx + 1],
                                masks_sb[:, moff:moff + w], ALU.mult, ALU.mult)
                        marks = [p.do("vector", sevA,
                                      waits=[m_sA, slot_free, m_masks, m_gmn4],
                                      mark=True),
                                 p.do("vector", sevB,
                                      waits=[m_s, slot_free, m_masks, m_gmn4],
                                      mark=True)]
                        m_sev[kx] = tuple(marks)
                        is_start = (j == 0 and kx == 0)
                        is_stop = (kx == 3)
                        def rmm(e, c=c, kx=kx, ssl=ssl, nA=nA, nB=nB,
                                is_start=is_start, is_stop=is_stop, col0=col0):
                            e.matmul(P[4][0:64, col0:512],
                                     vst[:, 1024 * kx + 64 * nA:1024 * kx + 64 * nA + 64],
                                     s_sb[:, ssl + col0:ssl + 512],
                                     start=is_start, stop=is_stop,
                                     tile_position=(0, 0), skip_group_check=True)
                            return e.matmul(
                                P[5][64:128, col0:512],
                                vst[:, 1024 * kx + 64 * nB:1024 * kx + 64 * nB + 64],
                                s_sb[:, ssl + 512 + col0:ssl + 1024],
                                start=is_start, stop=is_stop,
                                tile_position=(0, 64), skip_group_check=True)
                        if pending_rmm is not None:
                            pi, pf = pending_rmm
                            m_ret_i[pi] = p.do("tensor", pf,
                                               waits=list(m_sev[pi]) + m_vevc +
                                                     (ret_free if (j == 0 and pi == 0)
                                                      else []),
                                               mark=True)
                        pending_rmm = (kx, rmm)
                    pi, pf = pending_rmm
                    m_ret_i[pi] = p.do("tensor", pf,
                                       waits=list(m_sev[pi]) + m_vevc,
                                       mark=True)
                    prev_sev = [m_sev[2], m_sev[3]]
                    m_slot_prev = list(m_ret_i)
                    m_scores_last[c] = m_s
                    m_vst_guard[c] = m_ret_i[3]
                    m_kst_guard[c] = m_s

                    if j > 0:
                        m_Mmm[c] = m_cross
                    m_revA = p.do("scalar", lambda e, c=c: e.copy(
                        retj[0:64, 512 * c:512 * c + 512], P[4][0:64, :]),
                        waits=[m_ret_i[3], m_sq_prev[c % 2]], mark=True)
                    m_revB = p.do("vector", lambda e, c=c: e.tensor_copy(
                        retj[64:128, 512 * c:512 * c + 512], P[5][64:128, :]),
                        waits=[m_ret_i[3], m_sq_prev[c % 2]], mark=True)
                    m_rev = m_revB
                    m_retj_ev[c] = Mark("m_vector", m_revB.value)
                    m_ret_bank_prev[0] = m_revB
                    m_ret_bank_prev[1] = Mark("m_scalar", m_revA.value)
                    sqs = 1024 + (c % 2) * 512
                    m_sq = p.do("scalar", lambda e, c=c, sqs=sqs: e.activation(
                        scr16[:, sqs:sqs + 512], retj[:, 512 * c:512 * c + 512],
                        AF.Square), waits=[m_rev, m_revA], mark=True)
                    def gnstat(e, c=c, sqs=sqs):
                        e.matmul(P[7][0:16, :], sel16_sb[:, 16 * c:16 * c + 16],
                                 retj[:, 512 * c:512 * c + 512],
                                 start=(c == 0), stop=(c == 7))
                        return e.matmul(P[7][32:48, :],
                                        sel16_sb[:, 16 * c:16 * c + 16],
                                        scr16[:, sqs:sqs + 512],
                                        start=(c == 0), stop=(c == 7),
                                        tile_position=(0, 32),
                                        skip_group_check=True)
                    if c < 7:
                        pending_gnstat = (gnstat, [m_rev, m_revA, m_sq, m_sel16])
                        m_sq_prev[c % 2] = Mark("m_scalar", m_sq.value)
                    else:
                        m_gn_stat = p.do("tensor", gnstat,
                                         waits=[m_rev, m_revA, m_sq, m_sel16],
                                         mark=True)
                        m_sq_prev[c % 2] = m_gn_stat
                    if not last_j:
                        def kvmm(e, c=c, ksl=ksl, nA=nA, nB=nB):
                            last = None
                            for u in range(4):
                                e.matmul(P[5][0:64, 0:64],
                                         ktok[:, ksl + 128 * u:ksl + 128 * u + 64],
                                         vst[:, 1024 * u + 64 * nA:1024 * u + 64 * nA + 64],
                                         start=(u == 0), stop=(u == 3),
                                         tile_position=(0, 0), skip_group_check=True)
                                last = e.matmul(
                                    P[4][64:128, 0:64],
                                    ktok[:, ksl + 128 * u + 64:ksl + 128 * u + 128],
                                    vst[:, 1024 * u + 64 * nB:1024 * u + 64 * nB + 64],
                                    start=(u == 0), stop=(u == 3),
                                    tile_position=(0, 64), skip_group_check=True)
                            return last
                        m_kv = p.do("tensor", kvmm,
                                    waits=[m_ktevA, m_ktevB, m_Mcast[c]] + m_vevc,
                                    mark=True)
                        m_vst_guard[c] = m_kv
                        def mupd(e, c=c, gA=gA, gB=gB):
                            e.tensor_add(Ms_f[0:64, 64 * c:64 * c + 64],
                                         Ms_f[0:64, 64 * c:64 * c + 64],
                                         P[5][0:64, 0:64])
                            e.tensor_add(Ms_f[64:128, 64 * c:64 * c + 64],
                                         Ms_f[64:128, 64 * c:64 * c + 64],
                                         P[4][64:128, 0:64])
                            e.drain()
                            e.tensor_scalar(Ms_f[0:64, 64 * c:64 * c + 64],
                                            Ms_f[0:64, 64 * c:64 * c + 64],
                                            float(gA ** 512), None, ALU.mult)
                            e.tensor_scalar(Ms_f[64:128, 64 * c:64 * c + 64],
                                            Ms_f[64:128, 64 * c:64 * c + 64],
                                            float(gB ** 512), None, ALU.mult)
                            e.drain()
                            return e.tensor_copy(Ms_b[:, 64 * c:64 * c + 64],
                                                 Ms_f[:, 64 * c:64 * c + 64])
                        m_Mcast[c] = p.do("vector", mupd,
                                          waits=[m_kv, m_Minit,
                                                 m_cross if j > 0 else None],
                                          mark=True)
                        m_ret_bank_prev[0] = m_revB
                        m_ret_bank_prev[1] = Mark("m_scalar", m_revA.value)

                if STAGE < 5:
                    break
                def gnmath(e):
                    e.tensor_scalar(stat_f[0:16, 0:512], P[7][0:16, :], 1.0 / 64, None, ALU.mult)
                    e.tensor_scalar(stat_f[0:16, 512:1024], P[7][32:48, :], 1.0 / 64, None, ALU.mult)
                    e.drain()
                    e.tensor_mul(scrF[0:16, 0:512], stat_f[0:16, 0:512], stat_f[0:16, 0:512])
                    e.drain()
                    e.tensor_sub(stat_f[0:16, 512:1024], stat_f[0:16, 512:1024],
                                 scrF[0:16, 0:512])
                    e.drain()
                    return e.tensor_scalar(stat_f[0:16, 512:1024],
                                           stat_f[0:16, 512:1024], EPS, None, ALU.add)
                m_gmath = p.do("vector", gnmath, waits=[m_gn_stat], mark=True)
                m_gstd = p.do("scalar", lambda e: e.activation(
                    scrF[0:16, 512:1024], stat_f[0:16, 512:1024], AF.Sqrt),
                    waits=[m_gmath], mark=True)
                def gfin(e):
                    e.reciprocal(stat_f[0:16, 512:1024], scrF[0:16, 512:1024])
                    e.drain()
                    e.tensor_copy(stat_b[0:16, 0:512], stat_f[0:16, 0:512])
                    return e.tensor_copy(stat_b[0:16, 512:1024], stat_f[0:16, 512:1024])
                m_gfin = p.do("vector", gfin, waits=[m_gstd], mark=True)

                hoist_fin = None
                if j + 1 < nj:
                    ln1_ctx = ln_partA(m_x_next, P[4], P[5],
                                       xoff=4096 * ((j + 1) % 2),
                                       bufs=(ln1_f, ln1scr, ln1_b),
                                       first_waits=[m_retj_ev[7]])
                    hoist_fin = ln1_ctx[1]

                m_om = [None] * 8
                prev_gnb = [None, None, None]
                prev_op = [m_retj_ev[7], m_retj_ev[7]]
                def omk01(m, k, extra=()):
                    def omk(e, m=m, k=k, xlo=xlo):
                        return e.matmul(
                            (P[4] if m == 0 else P[5])[:, :],
                            w2a_sb[:, 1024 * k + 128 * m:1024 * k + 128 * m + 128],
                            xlnj[:, xlo + 512 * k:xlo + 512 * k + 512],
                            start=(k == 0), stop=(k == 7))
                    return p.do("tensor", omk,
                                waits=[m_w2a, m_om[k]] + list(extra),
                                mark=(k == 7))
                for c in range(8):
                    bA = (P[0], P[2], P[6])[c % 3]
                    bB = (P[1], P[3], P[7])[c % 3]
                    def gnb(e, c=c, bA=bA, bB=bB):
                        e.matmul(bA[:, :], selb_sb[0:16, 128 * c:128 * c + 128],
                                 stat_b[0:16, 0:512], start=True, stop=True)
                        return e.matmul(bB[:, :], selb_sb[0:16, 128 * c:128 * c + 128],
                                        stat_b[0:16, 512:1024], start=True, stop=True)
                    m_b = p.do("tensor", gnb,
                               waits=[m_gfin, m_selb, prev_gnb[c % 3]] +
                                     ([m_ktevA_last] if c == 2 else []) +
                                     list(prev_sev[0] or ()) +
                                     list(prev_sev[1] or ()), mark=True)
                    def gapply(e, c=c, bA=bA, bB=bB, xlo=xlo):
                        sl = 1024 + (c % 2) * 512
                        e.tensor_tensor(scr16[:, sl:sl + 512],
                                        retj[:, 512 * c:512 * c + 512], bA[:, :],
                                        ALU.subtract)
                        e.drain()
                        e.scalar_tensor_tensor(scr16[:, sl:sl + 512],
                                               scr16[:, sl:sl + 512],
                                               gns_sb[:, c:c + 1], bB[:, :],
                                               ALU.mult, ALU.mult)
                        e.drain()
                        return e.tensor_add(xlnj[:, xlo + 512 * c:xlo + 512 * c + 512],
                                            scr16[:, sl:sl + 512],
                                            gatej[:, 512 * c:512 * c + 512])
                    m_om[c] = p.do("vector", gapply,
                                   waits=[m_b, m_gate[c], m_gns, m_gnb], mark=True)
                    prev_gnb[c % 3] = m_om[c]
                    if c >= 3:
                        omk01(0, c - 3, extra=([prev_op[0], hoist_fin]
                                               if c == 3 else []))
                        omk01(1, c - 3,
                              extra=([prev_op[1]] +
                                     ([ln1_ctx[3]] if j + 1 < nj else [])
                                     if c == 3 else []))

                if STAGE < 6:
                    break
                m_x1 = [None] * 8
                prev_sq2 = [None, None]
                mm_stat2 = None
                for m in range(8):
                    bank = P[4] if m % 2 == 0 else P[5]
                    m_o = None
                    for k in (range(5, 8) if m < 2 else range(8)):
                        if m < 2:
                            m_o = omk01(m, k)
                            continue
                        def omk(e, m=m, k=k, bank=bank, xlo=xlo):
                            return e.matmul(
                                bank[:, :],
                                w2a_sb[:, 1024 * k + 128 * m:1024 * k + 128 * m + 128],
                                xlnj[:, xlo + 512 * k:xlo + 512 * k + 512],
                                start=(k == 0), stop=(k == 7))
                        m_o = p.do("tensor", omk,
                                   waits=[m_w2a, m_om[k]] +
                                         ([prev_op[m % 2]] if k == 0 else []),
                                   mark=(k == 7))
                    m_x1[m] = p.do("vector", lambda e, m=m, bank=bank, xo=xo:
                        e.tensor_add(
                        xj[:, xo + 512 * m:xo + 512 * m + 512],
                        xj[:, xo + 512 * m:xo + 512 * m + 512],
                        bank[:, :]), waits=[m_o], mark=True)
                    prev_op[m % 2] = m_x1[m]
                    if STAGE >= 7:
                        def emit_stat2(c, sq2s):
                            def statmm2(e, c=c, xo=xo):
                                e.matmul(P[6][0:1, :], onesc_sb[:, 0:1],
                                         xj[:, xo + 512 * c:xo + 512 * c + 512],
                                         start=(c == 0), stop=(c == 7))
                                return e.matmul(P[7][0:1, :], onesc_sb[:, 0:1],
                                                scr16[:, (c % 2) * 512:
                                                       (c % 2) * 512 + 512],
                                                start=(c == 0), stop=(c == 7))
                            return p.do("tensor", statmm2,
                                        waits=[m_x1[c], sq2s[c], m_onesc] +
                                              ([m_om[5]] if c == 0 else []),
                                        mark=True)
                        if m == 0:
                            msq2_l = [None] * 8
                        if m >= 2:
                            prev_sq2[m % 2] = emit_stat2(m - 2, msq2_l)
                        msq2_l[m] = p.do("scalar", lambda e, m=m, xo=xo:
                                    e.activation(
                                        scr16[:, (m % 2) * 512:(m % 2) * 512 + 512],
                                        xj[:, xo + 512 * m:xo + 512 * m + 512],
                                        AF.Square),
                                    waits=[m_x1[m], prev_sq2[m % 2]], mark=True)
                        if m == 7:
                            emit_stat2(6, msq2_l)
                            mm_stat2 = emit_stat2(7, msq2_l)

                if j + 1 < nj:
                    ln1_next = ln_partB(ln1_ctx, P[0], P[1],
                                        ("ln1", 4096 * ((j + 1) % 2)),
                                        bc_waits=[m_om[6]])

                if STAGE < 7:
                    break
                if j + 1 < nj:
                    ln1 = ln1_next
                ctx2 = ln_tail(P[6], P[7], xo, (stat_f, scrF, stat_b), mm_stat2)
                pending_ln2B = (ctx2, j, xo, m_x1)
                if j == nj - 1:
                    emit_ln2B()

        if STAGE >= 8:
            wscope1.close()
            wgu_scope = contextlib.ExitStack()
            wgu_sb = wgu_scope.enter_context(
                nc.sbuf_tensor("wgu_sb", [128, 16384], BF16, side="right"))
            m_wgu = wst.dma(lambda e: e.dma_start(
                out=wgu_sb[:, :].rearrange("p (k f) -> p k f", k=8),
                in_=dwgu[:, :].rearrange("(k p) f -> p k f", p=128)),
                waits=[m_o, m_gate[7]])

            m_zero = p.do("vector", lambda e: e.memset(comb_b[:, :], 0.0), mark=True)
            m_lg = [None] * nt
            prev_rps = [None, None]
            for u in range(nt):
                bank = P[0] if u % 2 == 0 else P[1]
                def rmm(e, u=u, bank=bank):
                    last = None
                    for k in range(8):
                        last = e.matmul(
                            bank[:, 0:9],
                            x2st[:, ST * k + 128 * u:ST * k + 128 * u + 128],
                            rout_sb[:, 9 * k:9 * k + 9],
                            start=(k == 0), stop=(k == 7))
                    return last
                m_r = p.do("tensor", rmm,
                           waits=[m_rout, prev_rps[u % 2]] +
                                 (([m_om[6]] + m_ln2_prev) if u < 2 else []),
                           mark=True)
                def lgev(e, u=u, bank=bank):
                    e.tensor_copy(lgb[:, 8 * u:8 * u + 8], bank[:, 0:8])
                    return e.tensor_copy(lgb[:, 128 + u:129 + u], bank[:, 8:9])
                m_lg[u] = p.do("vector", lgev, waits=[m_r], mark=True)
                prev_rps[u % 2] = m_lg[u]
            lg3 = lgb[:, 0:8 * nt].rearrange("p (u e) -> p u e", e=8)
            sc3 = lgb[:, 160:160 + 8 * nt].rearrange("p (u e) -> p u e", e=8)
            def bc(col):
                return comb_scr[:, col:col + nt].unsqueeze(2).broadcast_to(
                    [128, nt, 8])
            def smax1(e):
                e.reduce_max(comb_scr[:, 16:16 + nt], lg3, axis=AXX)
                e.drain()
                return e.tensor_tensor(lg3, lg3, bc(16), ALU.subtract)
            m_s1 = p.do("vector", smax1, waits=m_lg, mark=True)
            def sexp(e):
                e.activation(lgb[:, 0:8 * nt], lgb[:, 0:8 * nt], AF.Exp)
                return e.activation(comb_scr[:, 32:32 + nt],
                                    lgb[:, 128:128 + nt], AF.Sigmoid)
            m_e = p.do("scalar", sexp, waits=[m_s1], mark=True)
            def smax2(e):
                e.reduce_sum(comb_scr[:, 0:nt], lg3, axis=AXX)
                e.drain()
                e.reciprocal(comb_scr[:, 0:nt], comb_scr[:, 0:nt])
                e.drain()
                e.tensor_tensor(lg3, lg3, bc(0), ALU.mult)
                e.drain()
                e.reduce_max(comb_scr[:, 16:16 + nt], lg3, axis=AXX)
                e.drain()
                e.tensor_tensor(sc3, lg3, bc(16), ALU.is_equal)
                e.drain()
                e.scalar_tensor_tensor(sc3, sc3, -2.0, lg3, ALU.mult, ALU.add)
                e.drain()
                e.reduce_max(comb_scr[:, 48:48 + nt], sc3, axis=AXX)
                e.drain()
                e.tensor_tensor(sc3, lg3, bc(48), ALU.is_ge)
                e.drain()
                return e.tensor_tensor(lg3, lg3, sc3, ALU.mult)
            m_s2 = p.do("vector", smax2, waits=[m_e, m_zero], mark=True)
            m_trE = [None] * (nt // 4)
            m_tmm = [None] * nt
            for u in range(nt):
                slot = (u % 2) * 16
                def cb(e, u=u, slot=slot):
                    e.tensor_copy(comb_b[:, slot:slot + 8],
                                  lgb[:, 8 * u:8 * u + 8])
                    return e.tensor_copy(comb_b[:, slot + 8:slot + 9],
                                         comb_scr[:, 32 + u:33 + u])
                m_c = p.do("vector", cb,
                           waits=[m_s2, m_tmm[u - 2] if u > 1 else None],
                           mark=True)
                jg = u // 4
                def tmm(e, u=u, slot=slot, jg=jg):
                    return e.matmul(
                        P[jg % 2][0:9, 128 * (u % 4):128 * (u % 4) + 128],
                        comb_b[:, slot:slot + 9], eye_sb[:, :],
                        start=True, stop=True)
                m_tmm[u] = p.do("tensor", tmm, waits=[m_c, m_eye], mark=True)
                if u % 4 == 3:
                    m_trE[jg] = p.do(
                        "vector", lambda e, jg=jg: e.tensor_copy(
                            combT[0:9, 512 * jg:512 * jg + 512],
                            P[jg % 2][0:9, 0:512]),
                        waits=[m_tmm[u]], mark=True)
            m_tr_all = m_trE[nt // 4 - 1]

        if STAGE < 8:
            wst.dma(lambda e: e.dma_start(
                out=dout[:, :].rearrange("(c p) s -> p c s", p=128)[:, :, 0:512],
                in_=xj[:, 0:4096].rearrange("p (c s) -> p c s", c=8)),
                waits=[m_x, Mark("m_tensor", p.counts["tensor"]),
                       Mark("m_vector", p.counts["vector"]),
                       Mark("m_scalar", p.counts["scalar"])])
            p.do("sync", lambda e: e.nop(),
                 waits=wst.final_marks() + spl.final_marks() + ldx.final_marks()
                 + ldc.final_marks() + ldg.final_marks())
            p.finalize(block)
            p.close()
            st.close()
            return nc

        p.barrier(extra=spl.final_marks() + ldx.final_marks() + ldc.final_marks()
                  + wst.final_marks())
        with contextlib.ExitStack() as fscope:
            fsb = lambda name, shape, dt: fscope.enter_context(
                nc.sbuf_tensor(name, shape, dt))
            acc = fsb("acc", [128, 8 * ST], F32)
            sgu_j = fsb("sgu_j", [128, 4096], BF16)
            sg_scr = fsb("sg_scr", [128, 1536], BF16)
            hs_scr = fsb("hs_scr", [128, 1536], BF16)
            x1rl = fsb("x1rl", [128, 4096], BF16)


            m_acc = {}
            with contextlib.ExitStack() as wscope:
                wdn_sb = wscope.enter_context(
                    nc.sbuf_tensor("wdn_sb", [128, 8192], BF16))
                m_wdn = wst.dma(lambda e: e.dma_start(
                    out=wdn_sb[:, :].rearrange("p (k f) -> p k f", k=8),
                    in_=dwdn[:, :].rearrange("(k p) f -> p k f", p=128)))

                prev_silG = [None, None]
                prev_multU = [None, None]
                prev_dn = [None, None]
                prev_bc = None
                prev_bgm = [None, None]
                for j in range(nj):
                    csl = (j % 2) * 512
                    m_cb = cbr.dma(lambda e, j=j, csl=csl: e.dma_start(
                        out=combR[0:1, csl:csl + 512],
                        in_=combT[8:9, 512 * j:512 * j + 512]),
                        waits=[m_trE[j], prev_bgm[j % 2]])
                    m_sgu = [None] * 8
                    for m in range(8):
                        bG = P[3] if m % 2 == 0 else P[4]
                        bU = P[0] if m % 2 == 0 else P[1]
                        def gu1(e, m=m, j=j, bG=bG):
                            last = None
                            for k in range(8):
                                last = e.matmul(
                                    bG[:, :],
                                    wgu_sb[:, 2048 * k + 128 * m:2048 * k + 128 * m + 128],
                                    x2st[:, ST * k + 512 * j:ST * k + 512 * j + 512],
                                    start=(k == 0), stop=(k == 7))
                            return last
                        m_g1 = p.do("tensor", gu1,
                                    waits=[m_wgu, prev_silG[m % 2]], mark=True)
                        def gu2(e, m=m, j=j, bU=bU):
                            last = None
                            for k in range(8):
                                last = e.matmul(
                                    bU[:, :],
                                    wgu_sb[:, 2048 * k + 128 * (m + 8):2048 * k + 128 * (m + 8) + 128],
                                    x2st[:, ST * k + 512 * j:ST * k + 512 * j + 512],
                                    start=(k == 0), stop=(k == 7))
                            return last
                        m_g2 = p.do("tensor", gu2,
                                    waits=[m_wgu, prev_multU[m % 2]] +
                                          ([m_trE[max(g for g in
                                                      range(nt // 4)
                                                      if g % 2 == m)]]
                                           if j == 0 and m < 2 else []),
                                    mark=True)
                        m_gu_last = m_g2
                        m_sil = p.do("scalar", lambda e, m=m, bG=bG: e.activation(
                            sg_scr[:, 512 * (m % 2):512 * (m % 2) + 512],
                            bG[:, :], AF.Silu),
                            waits=[m_g1, m_sgu[m - 2] if m > 1 else None],
                            mark=True)
                        prev_silG[m % 2] = m_sil
                        m_sgu[m] = p.do("vector",
                            lambda e, m=m, bU=bU:
                            e.tensor_tensor(sgu_j[:, 512 * m:512 * m + 512],
                                            bU[:, :],
                                            sg_scr[:, 512 * (m % 2):512 * (m % 2) + 512],
                                            ALU.mult),
                            waits=[m_g2, m_sil], mark=True)
                        prev_multU[m % 2] = m_sgu[m]
                    def bgmm(e, csl=csl):
                        return e.matmul(P[2][:, :], ones1_sb[0:1, :],
                                        combR[0:1, csl:csl + 512],
                                        start=True, stop=True)
                    m_bgm = p.do("tensor", bgmm, waits=[m_cb, prev_bc],
                                 mark=True)
                    prev_bgm[j % 2] = m_bgm
                    m_bge = p.do("vector", lambda e: e.tensor_copy(
                        bcomb[:, :], P[2][:, :]), waits=[m_bgm], mark=True)
                    for m2 in range(8):
                        bank = P[5] if m2 % 2 == 0 else P[6]
                        def dmm(e, m2=m2, bank=bank):
                            last = None
                            for f in range(8):
                                last = e.matmul(
                                    bank[:, :],
                                    wdn_sb[:, 1024 * f + 128 * m2:1024 * f + 128 * m2 + 128],
                                    sgu_j[:, 512 * f:512 * f + 512],
                                    start=(f == 0), stop=(f == 7))
                            return last
                        m_d = p.do("tensor", dmm,
                                   waits=[m_wdn, prev_dn[m2 % 2]] + m_sgu, mark=True)
                        m_acc[(m2, j)] = p.do(
                            "vector", lambda e, m2=m2, j=j, bank=bank:
                            e.tensor_tensor(
                                acc[:, ST * m2 + 512 * j:ST * m2 + 512 * j + 512],
                                bank[:, :], bcomb[:, :], ALU.mult),
                            waits=[m_d, m_bge], mark=True)
                        prev_dn[m2 % 2] = m_acc[(m2, j)]
                    prev_bc = m_acc[(7, j)]
                m_shared_t = Mark("m_tensor", p.counts["tensor"])
                m_shared_v = m_acc[(7, nj - 1)]
                m_gu_done = m_gu_last
                m_sgu_last = Mark("m_vector", max(mk.value for mk in m_sgu))
                m_bge_last = m_bge
            wgu_scope.close()

            with contextlib.ExitStack() as escope:
                we1_sb = escope.enter_context(
                    nc.sbuf_tensor("we1_sb", [128, 8192], BF16, side="right"))
                we2_sb = escope.enter_context(
                    nc.sbuf_tensor("we2_sb", [128, 8192], BF16, side="right"))
                htil = escope.enter_context(
                    nc.sbuf_tensor("htil", [128, 8192], BF16))
                m_we1 = wst.dma(lambda e: e.dma_start(
                    out=we1_sb[:, :].rearrange("p (k f) -> p k f", k=8),
                    in_=dwe1[0, :, :].rearrange("(k p) f -> p k f", p=128)),
                    waits=[m_gu_done])
                m_we2 = wst.dma(lambda e: e.dma_start(
                    out=we2_sb[:, :].rearrange("p (k f) -> p k f", k=8),
                    in_=dwe2[0, :, :].rearrange("(k p) f -> p k f", p=128)),
                    waits=[m_gu_done])
                prev_h = [m_sgu_last, m_sgu_last, m_sgu_last]
                prev_y = [m_shared_v, m_shared_v, m_sgu_last, m_sgu_last]
                prev_bc2 = m_bge_last
                prev_bcomb_rd = m_shared_v
                prev_bcm = [m_shared_t, m_shared_t]
                pend_y = None
                m_htil_hist = {}
                m_we2_by_e = {0: m_we2}
                blocks = [(e_i, j) for e_i in range(NE) for j in range(nj)]
                def stage_cb(b):
                    e_i, j = blocks[b]
                    csl = (b % 2) * 512
                    return cbr.dma(lambda e, e_i=e_i, j=j, csl=csl: e.dma_start(
                        out=combR[0:1, csl:csl + 512],
                        in_=combT[e_i:e_i + 1, 512 * j:512 * j + 512]),
                        waits=[m_trE[j], prev_bcm[b % 2]])
                m_cb_next = stage_cb(0)
                for e_i in range(NE):
                    m_last_h = None
                    m_last_y = None
                    for j in range(nj):
                        b = e_i * nj + j
                        slot = (b % 2) * 4096
                        csl = (b % 2) * 512
                        m_cb = m_cb_next
                        def bcmm(e, csl=csl):
                            return e.matmul(P[2][:, :], ones1_sb[0:1, :],
                                            combR[0:1, csl:csl + 512],
                                            start=True, stop=True)
                        m_bcm = p.do("tensor", bcmm, waits=[m_cb, prev_bc2],
                                     mark=True)
                        prev_bcm[b % 2] = m_bcm
                        if b + 1 < len(blocks):
                            m_cb_next = stage_cb(b + 1)
                        m_bce = p.do("vector", lambda e: e.tensor_copy(
                            bcomb[:, :], P[2][:, :]),
                            waits=[m_bcm, prev_bcomb_rd], mark=True)
                        m_htil = [None] * 8
                        guard = m_htil_hist.get(slot)
                        for f in range(8):
                            bank = (P[3], P[4], P[7])[f % 3]
                            def hmm(e, f=f, j=j, bank=bank):
                                last = None
                                for k in range(8):
                                    last = e.matmul(
                                        bank[:, :],
                                        we1_sb[:, 1024 * k + 128 * f:1024 * k + 128 * f + 128],
                                        x2st[:, ST * k + 512 * j:ST * k + 512 * j + 512],
                                        start=(k == 0), stop=(k == 7))
                                return last
                            m_h = p.do("tensor", hmm,
                                       waits=[m_we1, prev_h[f % 3]], mark=True)
                            m_last_h = m_h
                            hsl = (f % 3) * 512
                            m_hs = p.do("scalar", lambda e, hsl=hsl, bank=bank:
                                        e.activation(hs_scr[:, hsl:hsl + 512],
                                                     bank[:, :], AF.Silu),
                                        waits=[m_h,
                                               m_htil[f - 3] if f >= 3 else None],
                                        mark=True)
                            m_htil[f] = p.do(
                                "vector", lambda e, f=f, hsl=hsl, slot=slot:
                                e.tensor_tensor(htil[:, slot + 512 * f:slot + 512 * f + 512],
                                                hs_scr[:, hsl:hsl + 512],
                                                bcomb[:, :], ALU.mult),
                                waits=[m_hs, m_bce,
                                       guard[f] if guard else None], mark=True)
                            prev_h[f % 3] = m_htil[f]
                        m_htil_hist[slot] = None
                        if pend_y is not None:
                            pe, pj, pslot, phtil = pend_y
                            for m in range(8):
                                bank = (P[5], P[6], P[0], P[1])[m % 4]
                                def ymm(e, m=m, pslot=pslot, bank=bank):
                                    last = None
                                    for f in range(8):
                                        last = e.matmul(
                                            bank[:, :],
                                            we2_sb[:, 1024 * f + 128 * m:1024 * f + 128 * m + 128],
                                            htil[:, pslot + 512 * f:pslot + 512 * f + 512],
                                            start=(f == 0), stop=(f == 7))
                                    return last
                                m_y = p.do("tensor", ymm,
                                           waits=[m_we2_by_e[pe], prev_y[m % 4]] + phtil,
                                           mark=True)
                                m_last_y = m_y
                                m_acc[(m, pj)] = p.do(
                                    "vector", lambda e, m=m, pj=pj, bank=bank:
                                    e.tensor_add(
                                        acc[:, ST * m + 512 * pj:ST * m + 512 * pj + 512],
                                        acc[:, ST * m + 512 * pj:ST * m + 512 * pj + 512],
                                        bank[:, :]),
                                    waits=[m_y, m_acc[(m, pj)]], mark=True)
                                prev_y[m % 4] = m_acc[(m, pj)]
                            m_htil_hist[pslot] = [Mark("m_tensor", m_last_y.value)] * 8
                            if pj == nj - 1 and pe + 1 < NE:
                                m_we2_by_e[pe + 1] = wst.dma(
                                    lambda e, pe=pe: e.dma_start(
                                        out=we2_sb[:, :].rearrange("p (k f) -> p k f", k=8),
                                        in_=dwe2[pe + 1, :, :].rearrange("(k p) f -> p k f", p=128)),
                                    waits=[m_last_y])
                        pend_y = (e_i, j, slot, m_htil)
                        prev_bc2 = Mark("m_vector", max(mk.value for mk in m_htil))
                        prev_bcomb_rd = prev_bc2
                    if e_i + 1 < NE:
                        m_we1 = wst.dma(lambda e, e_i=e_i: e.dma_start(
                            out=we1_sb[:, :].rearrange("p (k f) -> p k f", k=8),
                            in_=dwe1[e_i + 1, :, :].rearrange("(k p) f -> p k f", p=128)),
                            waits=[m_last_h])
                pe, pj, pslot, phtil = pend_y
                for m in range(8):
                    bank = (P[5], P[6], P[0], P[1])[m % 4]
                    def ymm(e, m=m, pslot=pslot, bank=bank):
                        last = None
                        for f in range(8):
                            last = e.matmul(
                                bank[:, :],
                                we2_sb[:, 1024 * f + 128 * m:1024 * f + 128 * m + 128],
                                htil[:, pslot + 512 * f:pslot + 512 * f + 512],
                                start=(f == 0), stop=(f == 7))
                        return last
                    m_y = p.do("tensor", ymm,
                               waits=[m_we2_by_e[pe], prev_y[m % 4]] + phtil, mark=True)
                    m_acc[(m, pj)] = p.do(
                        "vector", lambda e, m=m, pj=pj, bank=bank:
                        e.tensor_add(
                            acc[:, ST * m + 512 * pj:ST * m + 512 * pj + 512],
                            acc[:, ST * m + 512 * pj:ST * m + 512 * pj + 512],
                            bank[:, :]),
                        waits=[m_y, m_acc[(m, pj)]], mark=True)
                    prev_y[m % 4] = m_acc[(m, pj)]

                prev_rl = None
                for j in range(nj):
                    m_rl = ldx.dma(lambda e, j=j: e.dma_start(
                        out=x1rl[:, :].rearrange("p (c s) -> p c s", c=8),
                        in_=dx1s[:, :].rearrange("(c p) s -> p c s", p=128)
                            [:, :, 512 * j:512 * j + 512]),
                        waits=[m_spill, prev_rl])
                    m_f = [None] * 8
                    for m in range(8):
                        m_f[m] = p.do("vector", lambda e, m=m, j=j: e.tensor_add(
                            acc[:, ST * m + 512 * j:ST * m + 512 * j + 512],
                            acc[:, ST * m + 512 * j:ST * m + 512 * j + 512],
                            x1rl[:, 512 * m:512 * m + 512]),
                            waits=[m_rl, m_acc[(m, j)]], mark=True)
                    prev_rl = Mark("m_vector", max(mk.value for mk in m_f))
                    ost.dma(lambda e, j=j: e.dma_start(
                        out=dout[:, :].rearrange("(c p) s -> p c s", p=128)
                            [:, :, 512 * j:512 * j + 512],
                        in_=acc[:, :].rearrange("p (c s) -> p c s", c=8)
                            [:, :, 512 * j:512 * j + 512]),
                        waits=m_f)

                p.do("sync", lambda e: e.nop(),
                     waits=ost.final_marks() + spl.final_marks())

        p.finalize(block)
        p.close()
    st.close()
    return nc


def _host_prep(inputs):
    gam = _gammas()
    wq = np.asarray(inputs["L0_attn_wq"])
    wk = np.asarray(inputs["L0_attn_wk"])
    wv = np.asarray(inputs["L0_attn_wv"])
    def pack_qkv(w):
        out = np.zeros((128, 512), np.float32)
        for c in range(8):
            out[0:64, 64 * c:64 * c + 64] = w[2 * c]
            out[64:128, 64 * c:64 * c + 64] = w[2 * c + 1]
        return out
    bf = lambda x: np.ascontiguousarray(x).astype(ml_dtypes.bfloat16)
    gns = np.asarray(inputs["L0_attn_gn_scale"]).reshape(8, 128).T.copy()
    gnb = np.asarray(inputs["L0_attn_gn_bias"]).reshape(8, 128).T.copy()
    lng = np.zeros((128, 8), np.float32)
    for c in range(8):
        lng[0:64, c] = np.log(gam[2 * c])
        lng[64:128, c] = np.log(gam[2 * c + 1])
    sel16 = np.zeros((128, 128), np.float32)
    for c in range(8):
        sel16[0:64, 16 * c + 2 * c] = 1.0
        sel16[64:128, 16 * c + 2 * c + 1] = 1.0
    selb = np.zeros((16, 1024), np.float32)
    for c in range(8):
        selb[2 * c, 128 * c:128 * c + 64] = 1.0
        selb[2 * c + 1, 128 * c + 64:128 * c + 128] = 1.0
    tri = (np.arange(128)[None, :] >= np.arange(128)[:, None]).astype(np.float32)
    masks = np.zeros((128, 1280), np.float32)
    moffs = (0, 512, 896, 1152)
    for kxx in range(4):
        w = 512 - 128 * kxx
        masks[:, moffs[kxx]:moffs[kxx] + 128] = tri
        masks[:, moffs[kxx] + 128:moffs[kxx] + w] = 1.0
    gmn4 = np.zeros((128, 64), np.float64)
    pp = np.arange(128)
    for n in range(16):
        for u in range(4):
            gmn4[:, 4 * n + u] = gam[n] ** (-(128.0 * u + pp))
    gmn4 = gmn4.astype(np.float32)
    rout = np.concatenate([np.asarray(inputs["L0_ffn_router_w"]),
                           np.asarray(inputs["L0_ffn_shared_gate_w"])], axis=1)
    return {
        "wq": pack_qkv(wq), "wk": pack_qkv(wk), "wv": pack_qkv(wv),
        "w1a": np.ascontiguousarray(inputs["L0_attn_w1"], dtype=np.float32),
        "w2a": np.ascontiguousarray(inputs["L0_attn_w2"], dtype=np.float32),
        "gns": np.ascontiguousarray(gns, dtype=np.float32),
        "gnb": np.ascontiguousarray(gnb, dtype=np.float32),
        "lng": lng,
        "rout": np.ascontiguousarray(rout, dtype=np.float32),
        "we1": np.ascontiguousarray(inputs["L0_ffn_w1"], dtype=np.float32),
        "we2": np.ascontiguousarray(inputs["L0_ffn_w2"], dtype=np.float32),
        "wgu": np.ascontiguousarray(inputs["L0_ffn_shared_gate_up"], dtype=np.float32),
        "wdn": np.ascontiguousarray(inputs["L0_ffn_shared_down"], dtype=np.float32),
        "sel16": bf(sel16), "selb": bf(selb),
        "ones1": bf(np.ones((1, 128), np.float32)),
        "onesc": bf(np.full((128, 1), 1.0 / 1024.0, np.float32)),
        "masks": bf(masks),
        "gmn4": np.ascontiguousarray(gmn4),
        "eye128": np.ascontiguousarray(np.eye(128, dtype=ml_dtypes.bfloat16)),
    }


def kernel(**inputs):
    common = _host_prep(inputs)
    x = np.asarray(inputs["L0_attn_x"])
    nc = build(nj=4)
    in_maps = []
    for i in range(8):
        m = dict(common)
        m["x"] = np.ascontiguousarray(x[i].T)
        in_maps.append(m)
    res = run_bass_kernel_spmd(nc, in_maps, core_ids=list(range(8)))
    out = np.stack([np.asarray(res.results[i]["out"], dtype=np.float32).T
                    for i in range(8)], axis=0)
    return out


# version 38
# speedup vs baseline: 1.0656x; 1.0656x over previous
import contextlib
import os
import numpy as np
import ml_dtypes
import concourse.bass as bass
import concourse.mybir as mybir
from concourse.bass_utils import run_bass_kernel_spmd

F32 = mybir.dt.float32
F16 = mybir.dt.float16
BF16 = mybir.dt.bfloat16
AF = mybir.ActivationFunctionType
ALU = mybir.AluOpType
AXX = mybir.AxisListType.X

D = 1024
NH = 16
NE = 8
EPS = 1e-5

ENGINES = ("tensor", "vector", "scalar", "gpsimd", "sync")


class Mark:
    __slots__ = ("sem_name", "value")
    def __init__(self, sem_name, value):
        self.sem_name = sem_name
        self.value = value


class Stream:
    def __init__(self, prog, name, queue, nsems):
        self.prog = prog
        self.name = name
        self.queue = queue
        self.nsems = nsems
        self.counts = [0] * nsems
        self.i = 0
        for k in range(nsems):
            prog._ensure_sem(f"{name}{k}")

    def dma(self, fn, waits=()):
        k = self.i % self.nsems
        self.counts[k] += 16
        sem_name = f"{self.name}{k}"
        self.i += 1
        sems = self.prog.sems
        def op(e, fn=fn, sem_name=sem_name):
            return fn(e).then_inc(sems[sem_name], 16)
        self.prog._emit(self.queue, op, waits=waits)
        return Mark(sem_name, self.counts[k])

    def final_marks(self):
        return [Mark(f"{self.name}{k}", self.counts[k])
                for k in range(self.nsems) if self.counts[k] > 0]


class Prog:
    def __init__(self, nc):
        self.nc = nc
        self.ops = {e: [] for e in ENGINES}
        self.sems = {}
        self._stack = contextlib.ExitStack()
        self.counts = {e: 0 for e in ENGINES}
        self.waited = {e: {} for e in ENGINES}
        for e in ENGINES:
            self._ensure_sem("m_" + e)

    def _ensure_sem(self, name):
        if name not in self.sems:
            self.sems[name] = self._stack.enter_context(self.nc.semaphore(name))
        return self.sems[name]

    def stream(self, name, queue, nsems):
        return Stream(self, name, queue, nsems)

    def _emit(self, eng, op, waits=()):
        needed = []
        own_dep = False
        w = self.waited[eng]
        for mk in waits:
            if mk is None:
                continue
            if mk.sem_name == "m_" + eng:
                own_dep = True
                continue
            if w.get(mk.sem_name, -1) >= mk.value:
                continue
            w[mk.sem_name] = mk.value
            needed.append((mk.sem_name, mk.value))
        if needed or own_dep:
            sems = self.sems
            def full(e, needed=needed, op=op, own_dep=own_dep):
                for sem_name, value in needed:
                    e.wait_ge(sems[sem_name], value)
                if own_dep:
                    e.drain()
                return op(e)
            self.ops[eng].append(full)
        else:
            self.ops[eng].append(op)

    def do(self, eng, fn, waits=(), mark=False):
        mk = None
        if mark:
            self.counts[eng] += 1
            if eng == "gpsimd":
                name = f"mg{self.counts[eng]}"
                sem = self._ensure_sem(name)
                mk = Mark(name, 1)
            else:
                name = "m_" + eng
                sem = self.sems[name]
                mk = Mark(name, self.counts[eng])
            def op(e, fn=fn, sem=sem):
                return fn(e).then_inc(sem, 1)
        else:
            op = fn
        self._emit(eng, op, waits=waits)
        return mk

    _bar_n = 0

    def barrier(self, engines=ENGINES, extra=()):
        Prog._bar_n += 1
        marks = []
        for e in engines:
            name = f"bar{Prog._bar_n}_{e}"
            sem = self._ensure_sem(name)
            self._emit(e, lambda en, sem=sem: en.nop().then_inc(sem, 1))
            marks.append(Mark(name, 1))
        marks = marks + list(extra)
        for e in engines:
            self.do(e, lambda en: en.nop(), waits=marks)

    def finalize(self, block):
        for eng in ENGINES:
            ops = self.ops[eng]
            if not ops:
                continue
            def make(ops):
                def body(e):
                    for op in ops:
                        op(e)
                return body
            getattr(block, eng)(make(ops))

    def close(self):
        self._stack.close()


def _gammas():
    lin = np.linspace(np.log(1.0 / 512.0), np.log(1.0 / 32.0), NH)
    return 1.0 - np.exp(lin)


def build(nj=4):
    STAGE = int(os.environ.get("KSTAGE", "99"))
    nc = bass.Bass("TRN2", target_bir_lowering=False)
    ST = nj * 512
    nt = nj * 4

    dx = nc.dram_tensor("x", [D, ST], F32, kind="ExternalInput").ap()
    dwq = nc.dram_tensor("wq", [128, 512], F32, kind="ExternalInput").ap()
    dwk = nc.dram_tensor("wk", [128, 512], F32, kind="ExternalInput").ap()
    dwv = nc.dram_tensor("wv", [128, 512], F32, kind="ExternalInput").ap()
    dw1a = nc.dram_tensor("w1a", [D, D], F32, kind="ExternalInput").ap()
    dw2a = nc.dram_tensor("w2a", [D, D], F32, kind="ExternalInput").ap()
    dgns = nc.dram_tensor("gns", [128, 8], F32, kind="ExternalInput").ap()
    dgnb = nc.dram_tensor("gnb", [128, 8], F32, kind="ExternalInput").ap()
    dlng = nc.dram_tensor("lng", [128, 8], F32, kind="ExternalInput").ap()
    drout = nc.dram_tensor("rout", [D, 9], F32, kind="ExternalInput").ap()
    dwe1 = nc.dram_tensor("we1", [NE, D, D], F32, kind="ExternalInput").ap()
    dwe2 = nc.dram_tensor("we2", [NE, D, D], F32, kind="ExternalInput").ap()
    dwgu = nc.dram_tensor("wgu", [D, 2 * D], F32, kind="ExternalInput").ap()
    dwdn = nc.dram_tensor("wdn", [D, D], F32, kind="ExternalInput").ap()
    dsel16 = nc.dram_tensor("sel16", [128, 128], BF16, kind="ExternalInput").ap()
    dselb = nc.dram_tensor("selb", [16, 1024], BF16, kind="ExternalInput").ap()
    dones1 = nc.dram_tensor("ones1", [1, 128], BF16, kind="ExternalInput").ap()
    donesc = nc.dram_tensor("onesc", [128, 1], BF16, kind="ExternalInput").ap()
    dmasks = nc.dram_tensor("masks", [128, 1280], BF16, kind="ExternalInput").ap()
    dgmn4 = nc.dram_tensor("gmn4", [128, 64], F32, kind="ExternalInput").ap()
    deye = nc.dram_tensor("eye128", [128, 128], BF16, kind="ExternalInput").ap()
    dout = nc.dram_tensor("out", [D, ST], F32, kind="ExternalOutput").ap()
    dx1s = nc.dram_tensor("x1s", [D, ST], BF16).ap()

    gam = _gammas()

    st = contextlib.ExitStack()
    sb = lambda name, shape, dt: st.enter_context(nc.sbuf_tensor(name, shape, dt))
    ps = lambda name, shape: st.enter_context(nc.psum_tensor(name, shape, F32))

    x2st = sb("x2st", [128, 8 * ST], BF16)
    wqkv = sb("wqkv", [128, 1536], BF16)
    gns_sb = sb("gns_sb", [128, 8], F32)
    gnb_sb = sb("gnb_sb", [128, 8], F32)
    lng_sb = sb("lng_sb", [128, 8], F32)
    sel16_sb = sb("sel16_sb", [128, 128], BF16)
    selb_sb = sb("selb_sb", [16, 1024], BF16)
    ones1_sb = sb("ones1_sb", [1, 128], BF16)
    onesc_sb = sb("onesc_sb", [128, 1], BF16)
    eps_sb = sb("eps_sb", [1, 1], F32)
    masks_sb = sb("masks_sb", [128, 1280], BF16)
    gmn4_sb = sb("gmn4_sb", [128, 64], F32)
    lgb = sb("lgb", [128, 320], F32)
    bcmu = sb("bcmu", [128, 512], BF16)
    bcrs = sb("bcrs", [128, 512], BF16)
    combT = sb("combT", [32, ST], BF16)
    eye_sb = sb("eye_sb", [128, 128], BF16)
    rout_sb = sb("rout_sb", [128, 72], BF16)
    iotaq = sb("iotaq", [128, 512], F16)
    iotak = sb("iotak", [128, 512], F16)
    stat_f = sb("stat_f", [16, 1024], F32)
    scrF = sb("scrF", [16, 1024], F32)
    stat_b = sb("stat_b", [16, 1024], BF16)
    scr16 = sb("scr16", [128, 4 * 512], BF16)
    bcomb = sb("bcomb", [128, 512], BF16)
    comb_scr = sb("comb_scr", [128, 96], F32)
    comb_b = sb("comb_b", [128, 32], BF16)
    combR = sb("combR", [1, 1024], BF16)

    P = [ps(f"P{k}", [128, 512]) for k in range(8)]

    p = Prog(nc)
    ldx = p.stream("ldx", "sync", 2)
    ldc = p.stream("ldc", "sync", 10)
    wst = p.stream("wst", "gpsimd", 8)
    spl = p.stream("spl", "sync", 2)
    ost = p.stream("ost", "sync", 2)
    cbr = p.stream("cbr", "sync", 2)
    ldg = p.stream("ldg", "gpsimd", 2)

    with nc.Block() as block:
        m_gns = ldc.dma(lambda e: e.dma_start(out=gns_sb[:], in_=dgns[:, :]))
        m_gnb = ldc.dma(lambda e: e.dma_start(out=gnb_sb[:], in_=dgnb[:, :]))
        m_lng = ldc.dma(lambda e: e.dma_start(out=lng_sb[:], in_=dlng[:, :]))
        m_sel16 = ldc.dma(lambda e: e.dma_start(out=sel16_sb[:], in_=dsel16[:, :]))
        m_selb = ldc.dma(lambda e: e.dma_start(out=selb_sb[:], in_=dselb[:, :]))
        m_ones1 = ldc.dma(lambda e: e.dma_start(out=ones1_sb[:], in_=dones1[:, :]))
        m_onesc = ldc.dma(lambda e: e.dma_start(out=onesc_sb[:], in_=donesc[:, :]))
        m_eps = p.do("vector", lambda e: e.memset(eps_sb[0:1, :], EPS), mark=True)
        m_masks = ldc.dma(lambda e: e.dma_start(out=masks_sb[:], in_=dmasks[:, :]))
        m_gmn4 = ldc.dma(lambda e: e.dma_start(out=gmn4_sb[:], in_=dgmn4[:, :]))
        m_eye = ldc.dma(lambda e: e.dma_start(out=eye_sb[:], in_=deye[:, :]))
        m_wq = wst.dma(lambda e: e.dma_start(out=wqkv[:, 0:512], in_=dwq[:, :]))
        m_wk = wst.dma(lambda e: e.dma_start(out=wqkv[:, 512:1024], in_=dwk[:, :]))
        m_wv = wst.dma(lambda e: e.dma_start(out=wqkv[:, 1024:1536], in_=dwv[:, :]))
        m_rout = wst.dma(lambda e: e.dma_start(
            out=rout_sb[:, 0:72].rearrange("p (k f) -> p k f", k=8),
            in_=drout[:, :].rearrange("(k p) f -> p k f", p=128)))
        m_ioq = p.do("gpsimd", lambda e: e.iota(
            iotaq[:, :], pattern=[[1, 512]], base=0, channel_multiplier=0,
            allow_small_or_imprecise_dtypes=True), mark=True)
        m_iok = p.do("gpsimd", lambda e: e.iota(
            iotak[:, :], pattern=[[0, 4], [-1, 128]], base=0, channel_multiplier=0,
            allow_small_or_imprecise_dtypes=True), mark=True)
        with contextlib.ExitStack() as ascope:
            asb = lambda name, shape, dt: ascope.enter_context(
                nc.sbuf_tensor(name, shape, dt))
            xj = asb("xj", [128, 8192], BF16)
            bgall = asb("bgall", [128, 16 * 512], BF16)
            ln1_f = asb("ln1_f", [1, 1024], F32)
            ln1scr = asb("ln1scr", [1, 1024], F32)
            ln1_b = asb("ln1_b", [1, 1024], BF16)
            xlnj = asb("xlnj", [128, 8192], BF16)
            qt = asb("qt", [128, 1024], BF16)
            kst = asb("kst", [128, 4096], BF16)
            vst = asb("vst", [128, 4096], BF16)
            ktok = asb("ktok", [128, 1024], BF16)
            Ms_f = asb("Ms_f", [128, 512], F32)
            Ms_b = asb("Ms_b", [128, 512], BF16)
            gatej = asb("gatej", [128, 4096], BF16)
            retj = asb("retj", [128, 4096], BF16)
            s_sb = asb("s_sb", [128, 4096], BF16)
            wscope1 = contextlib.ExitStack()
            w1a_sb = wscope1.enter_context(
                nc.sbuf_tensor("w1a_sb", [128, 8192], BF16, side="right"))
            w2a_sb = wscope1.enter_context(
                nc.sbuf_tensor("w2a_sb", [128, 8192], BF16, side="right"))

            def gen_bg(e):
                last = None
                for c in range(8):
                    e.activation(bgall[:, 1024 * c:1024 * c + 512], iotaq[:, :],
                                 AF.Exp, scale=lng_sb[:, c:c + 1])
                    last = e.activation(bgall[:, 1024 * c + 512:1024 * c + 1024],
                                        iotak[:, :], AF.Exp, scale=lng_sb[:, c:c + 1])
                return last
            m_bgall = p.do("scalar", gen_bg, waits=[m_ioq, m_iok, m_lng], mark=True)

            m_w1a = wst.dma(lambda e: e.dma_start(
                out=w1a_sb[:, :].rearrange("p (k f) -> p k f", k=8),
                in_=dw1a[:, :].rearrange("(k p) f -> p k f", p=128)))
            m_w2a = wst.dma(lambda e: e.dma_start(
                out=w2a_sb[:, :].rearrange("p (k f) -> p k f", k=8),
                in_=dw2a[:, :].rearrange("(k p) f -> p k f", p=128)))

            def load_x(j, waits):
                xo = 4096 * (j % 2)
                return ldg.dma(lambda e, j=j, xo=xo: e.dma_start(
                    out=xj[:, xo:xo + 4096].rearrange("p (c s) -> p c s", c=8),
                    in_=dx[:, :].rearrange("(c p) s -> p c s", p=128)
                        [:, :, 512 * j:512 * j + 512]), waits=waits)

            m_x = load_x(0, [])
            m_x_next = load_x(1, []) if nj > 1 else None

            def ln_partA(m_src, stA, stB, xoff=0, bufs=None, first_waits=()):
                st_f, sc_F, st_b = bufs if bufs is not None else (stat_f, scrF, stat_b)
                prev_mm = [None, None]
                mm_stat = None
                for c in range(8):
                    sl = (c % 2) * 512
                    srcw = m_src[c] if isinstance(m_src, list) else m_src
                    msq = p.do("scalar", lambda e, c=c, sl=sl: e.activation(
                        scr16[:, sl:sl + 512],
                        xj[:, xoff + 512 * c:xoff + 512 * c + 512],
                        AF.Square), waits=[srcw, prev_mm[c % 2]], mark=True)
                    def statmm(e, c=c, sl=sl):
                        e.matmul(stA[0:1, :], onesc_sb[:, 0:1],
                                 xj[:, xoff + 512 * c:xoff + 512 * c + 512],
                                 start=(c == 0), stop=(c == 7))
                        return e.matmul(stB[0:1, :], onesc_sb[:, 0:1],
                                        scr16[:, sl:sl + 512],
                                        start=(c == 0), stop=(c == 7))
                    mm_stat = p.do("tensor", statmm,
                                   waits=[srcw, msq, m_onesc] +
                                         (list(first_waits) if c == 0 else []),
                                   mark=True)
                    prev_mm[c % 2] = mm_stat
                return ln_tail(stA, stB, xoff, (st_f, sc_F, st_b), mm_stat)

            def ln_tail(stA, stB, xoff, bufs, mm_stat):
                st_f, sc_F, st_b = bufs
                m_m2 = p.do("scalar", lambda e: e.activation(
                    sc_F[0:1, 0:512], stA[0:1, :], AF.Square),
                    waits=[mm_stat], mark=True)
                m_var = p.do("vector", lambda e: e.scalar_tensor_tensor(
                    st_f[0:1, 512:1024], sc_F[0:1, 0:512], -1.0, stB[0:1, :],
                    ALU.mult, ALU.add), waits=[m_m2], mark=True)
                def fin(e):
                    e.activation(sc_F[0:1, 512:1024], st_f[0:1, 512:1024],
                                 AF.Sqrt, bias=eps_sb[0:1, 0:1])
                    e.drain()
                    e.copy(st_b[0:1, 0:512], stA[0:1, :])
                    return e.copy(st_b[0:1, 512:1024], sc_F[0:1, 512:1024])
                m_fin = p.do("scalar", fin, waits=[m_var, m_eps], mark=True)
                return (st_b, m_fin, xoff, Mark("m_vector", m_var.value))

            def ln_partB(ctx, bcA, bcB, dst, bc_waits=()):
                nonlocal prev_lnapply
                st_b, m_fin, xoff = ctx[:3]
                def bcast(e):
                    e.matmul(bcA[:, :], ones1_sb[0:1, :], st_b[0:1, 0:512],
                             start=True, stop=True)
                    return e.matmul(bcB[:, :], ones1_sb[0:1, :], st_b[0:1, 512:1024],
                                    start=True, stop=True)
                m_bc = p.do("tensor", bcast,
                            waits=[m_fin, m_ones1] + list(bc_waits), mark=True)
                m_evmu = p.do("scalar", lambda e: e.copy(bcmu[:, :], bcA[:, :]),
                              waits=[m_bc, prev_lnapply], mark=True)
                def rec(e):
                    with nc.allow_low_precision(reason="bf16 rstd tile"):
                        return e.reciprocal(bcrs[:, :], bcB[:, :])
                m_rec = p.do("vector", rec, waits=[m_bc], mark=True)
                mu_b = bcmu[:, :].unsqueeze(1).broadcast_to([128, 4, 512])
                rs_b = bcrs[:, :].unsqueeze(1).broadcast_to([128, 4, 512])
                if dst[0] == "ln1":
                    xlo = dst[1]
                    mid = xlnj[:, xlo:xlo + 4096].rearrange(
                        "p (c t) -> p c t", t=512)
                    out3 = mid
                    guard = None
                else:
                    pj, pxo, guard = dst[1], dst[2], dst[3]
                    xoff = pxo
                    mid = xlnj[:, pxo:pxo + 4096].rearrange(
                        "p (c t) -> p c t", t=512)
                    out3 = x2st[:, :].rearrange("p (c t) -> p c t", c=8)[
                        :, :, 512 * pj:512 * pj + 512]
                in3 = xj[:, xoff:xoff + 4096].rearrange("p (c t) -> p c t", t=512)
                marks = []
                for h in (0, 1):
                    def a1(e, h=h):
                        with nc.allow_low_precision(reason="bf16 LN apply"):
                            return e.tensor_tensor(mid[:, 4 * h:4 * h + 4, :],
                                                   in3[:, 4 * h:4 * h + 4, :],
                                                   mu_b, ALU.subtract)
                    def a2(e, h=h):
                        with nc.allow_low_precision(reason="bf16 LN apply"):
                            return e.tensor_tensor(out3[:, 4 * h:4 * h + 4, :],
                                                   mid[:, 4 * h:4 * h + 4, :],
                                                   rs_b, ALU.mult)
                    m_a1 = p.do("vector", a1,
                                waits=[m_evmu, guard] if h == 0 else [],
                                mark=True)
                    m_a2 = p.do("vector", a2, waits=[m_a1, m_rec], mark=True)
                    marks.append(m_a2)
                prev_lnapply = marks[1]
                return [marks[0]] * 4 + [marks[1]] * 4

            def layernorm(m_src, stA, stB, bcA, bcB, dst, xoff=0,
                          bufs=None, first_waits=()):
                ctx = ln_partA(m_src, stA, stB, xoff=xoff, bufs=bufs,
                               first_waits=first_waits)
                return ln_partB(ctx, bcA, bcB, dst)

            m_spill = None
            prev_lnapply = None
            m_Minit = p.do("vector", lambda e: e.memset(Ms_f[:, :], 0.0), mark=True)
            m_Mcast = [m_Minit] * 8
            m_Mmm = [None] * 8
            m_vst_guard = [None] * 8
            m_kst_guard = [None] * 8
            ln1 = layernorm(m_x, P[6], P[7], P[6], P[7], ("ln1", 0), xoff=0,
                            bufs=(ln1_f, ln1scr, ln1_b))
            m_x_cur = m_x
            m_ln2_prev = []
            m_ln2_fin = None
            pending_ln2B = None
            def emit_ln2B():
                nonlocal pending_ln2B, m_ln2_prev, m_spill, m_x, m_x_next, \
                    m_ln2_fin
                ctx2, pj, pxo, pm_x1 = pending_ln2B
                pending_ln2B = None
                ln2 = ln_partB(ctx2, P[2], P[3],
                               ("ln2", pj, pxo, pm_x1[7]))
                m_ln2_fin = ctx2[1]
                m_ln2_prev = ln2
                m_spill = spl.dma(lambda e, pj=pj, pxo=pxo: e.dma_start(
                    out=dx1s[:, :].rearrange("(c p) s -> p c s", p=128)
                        [:, :, 512 * pj:512 * pj + 512],
                    in_=xj[:, pxo:pxo + 4096].rearrange("p (c s) -> p c s", c=8)),
                    waits=ln2 + pm_x1)
                m_x = m_x_next
                if pj + 2 < nj:
                    m_x_next = load_x(pj + 2, [m_spill] + ln2)
            for j in range(nj):
                xo = 4096 * (j % 2)
                xlo = 4096 * (j % 2)

                if STAGE < 2:
                    break
                m_gate = [None] * 8
                prev_gate_ev = [None, None]
                for m in range(8):
                    bank = P[4] if m % 2 == 0 else P[5]
                    m_g = None
                    for k in range(8):
                        def gmk(e, m=m, k=k, bank=bank, xlo=xlo):
                            return e.matmul(
                                bank[:, :],
                                w1a_sb[:, 1024 * k + 128 * m:1024 * k + 128 * m + 128],
                                xlnj[:, xlo + 512 * k:xlo + 512 * k + 512],
                                start=(k == 0), stop=(k == 7))
                        m_g = p.do("tensor", gmk,
                                   waits=[m_w1a, ln1[k]] +
                                         ([prev_gate_ev[m % 2]] if k == 0 else []),
                                   mark=(k == 7))
                    def gsil(e, m=m, bank=bank):
                        e.activation(gatej[:, 512 * m:512 * m + 512], bank[:, :],
                                     AF.Silu)
                        e.drain()
                        return e.add(gatej[:, 512 * m:512 * m + 512],
                                     gatej[:, 512 * m:512 * m + 512],
                                     gnb_sb[:, m:m + 1])
                    m_gate[m] = p.do("scalar", gsil, waits=[m_g, m_gnb], mark=True)
                    prev_gate_ev[m % 2] = m_gate[m]
                    if m == 1 and pending_ln2B is not None:
                        emit_ln2B()

                if STAGE < 3:
                    break

                m_qkev = [None] * 8
                m_scores_last = [None] * 8
                m_retj_ev = [None] * 8
                m_vev_prev = []
                pending_gnstat = None
                m_ktok_prev = None
                m_ktevA_last = None
                m_slot_prev = [None] * 4
                m_ret_bank_prev = [None, None]
                prev_sev = [(), ()]
                m_sq_prev = [None, None]
                m_gn_stat = None
                for c in range(8):
                    qsl = (c % 2) * 512
                    nA, nB = 2 * c, 2 * c + 1
                    gA, gB = gam[nA], gam[nB]
                    def qkmm(e, c=c, xlo=xlo):
                        x_ap = xlnj[:, xlo + 512 * c:xlo + 512 * c + 512]
                        e.matmul(P[0][0:64, :], wqkv[0:64, 64 * c:64 * c + 64],
                                 x_ap[0:64, :], start=True, stop=True,
                                 tile_position=(0, 0))
                        e.matmul(P[0][64:128, :], wqkv[64:128, 64 * c:64 * c + 64],
                                 x_ap[64:128, :], start=True, stop=True,
                                 tile_position=(64, 64))
                        e.matmul(P[1][0:64, :], wqkv[0:64, 512 + 64 * c:576 + 64 * c],
                                 x_ap[0:64, :], start=True, stop=True,
                                 tile_position=(0, 0))
                        return e.matmul(P[1][64:128, :], wqkv[64:128, 512 + 64 * c:576 + 64 * c],
                                        x_ap[64:128, :], start=True, stop=True,
                                        tile_position=(64, 64))
                    m_qk = p.do("tensor", qkmm,
                                waits=[ln1[c], m_wq, m_wk] +
                                      list(prev_sev[0] or ()) +
                                      list(prev_sev[1] or ()), mark=True)
                    if pending_gnstat is not None:
                        pgf, pgw = pending_gnstat
                        m_gn_stat = p.do("tensor", pgf, waits=pgw, mark=True)
                        m_sq_prev[(c - 1) % 2] = m_gn_stat
                        pending_gnstat = None
                    slot_guard = m_scores_last[c - 2] if c >= 2 else None
                    m_qe = p.do("vector", lambda e, qsl=qsl, c=c: e.tensor_tensor(
                        qt[:, qsl:qsl + 512], P[0][:, :],
                        bgall[:, 1024 * c:1024 * c + 512], ALU.mult),
                        waits=[m_qk, m_bgall, slot_guard], mark=True)
                    m_ke = p.do("vector", lambda e, c=c: e.tensor_copy(
                        kst[:, 512 * c:512 * c + 512], P[1][:, :]),
                        waits=[m_qk, m_kst_guard[c]], mark=True)
                    m_qkev[c] = [m_qe, m_ke]

                    def vmm(e, c=c, xlo=xlo):
                        last = None
                        for u in range(4):
                            x_ap = xlnj[:, xlo + 512 * c + 128 * u:xlo + 512 * c + 128 * u + 128]
                            e.matmul(P[6][:, 128 * u:128 * u + 64], x_ap[0:64, :],
                                     wqkv[0:64, 1024 + 64 * c:1088 + 64 * c],
                                     start=True, stop=True, tile_position=(0, 0))
                            last = e.matmul(P[3][:, 128 * u:128 * u + 64],
                                            x_ap[64:128, :],
                                            wqkv[64:128, 1024 + 64 * c:1088 + 64 * c],
                                            start=True, stop=True,
                                            tile_position=(64, 0))
                        return last
                    m_v = p.do("tensor", vmm,
                               waits=[ln1[c], m_wv] + (m_vev_prev or []) +
                                     list(prev_sev[1] or ()) + m_ln2_prev +
                                     ([m_ln2_fin] if c == 0 else []),
                               mark=True)
                    def vevA(e, c=c):
                        dst = vst[:, :].rearrange("p (u f) -> p u f", f=1024)
                        return e.copy(
                            dst[:, :, 128 * c:128 * c + 64],
                            P[6][:, 0:512].rearrange("p (u f) -> p u f", f=128)[:, :, 0:64])
                    def vevB(e, c=c):
                        dst = vst[:, :].rearrange("p (u f) -> p u f", f=1024)
                        return e.tensor_copy(
                            dst[:, :, 128 * c + 64:128 * c + 128],
                            P[3][:, 0:512].rearrange("p (u f) -> p u f", f=128)[:, :, 0:64])
                    m_vevA = p.do("scalar", vevA,
                                  waits=[m_v, m_vst_guard[c]], mark=True)
                    m_vevB = p.do("vector", vevB,
                                  waits=[m_v, m_vst_guard[c]], mark=True)
                    m_vevc = [m_vevA, m_vevB]
                    m_vev_prev = m_vevc

                    last_j = (j == nj - 1)
                    ksl = (c % 2) * 512
                    if not last_j:
                        def ktmmA(e, c=c, xlo=xlo):
                            last = None
                            for u in range(4):
                                x_ap = xlnj[:, xlo + 512 * c + 128 * u:xlo + 512 * c + 128 * u + 128]
                                last = e.matmul(P[6][:, 128 * u:128 * u + 64],
                                                x_ap[0:64, :],
                                                wqkv[0:64, 512 + 64 * c:576 + 64 * c],
                                                start=True, stop=True,
                                                tile_position=(0, 0))
                            return last
                        def ktmmB(e, c=c, xlo=xlo):
                            last = None
                            for u in range(4):
                                x_ap = xlnj[:, xlo + 512 * c + 128 * u:xlo + 512 * c + 128 * u + 128]
                                last = e.matmul(P[3][:, 128 * u:128 * u + 64],
                                                x_ap[64:128, :],
                                                wqkv[64:128, 512 + 64 * c:576 + 64 * c],
                                                start=True, stop=True,
                                                tile_position=(64, 0))
                            return last
                        m_ktA = p.do("tensor", ktmmA,
                                     waits=[m_vevA, m_ktok_prev], mark=True)
                        m_kt = p.do("tensor", ktmmB, waits=[m_vevB], mark=True)
                        def ktevA(e, c=c, ksl=ksl):
                            last = None
                            for u in range(4):
                                n = 2 * c
                                last = e.mul(
                                    ktok[:, ksl + 128 * u:ksl + 128 * u + 64],
                                    P[6][:, 128 * u:128 * u + 64],
                                    gmn4_sb[:, 4 * n + u:4 * n + u + 1])
                            return last
                        def ktevB(e, c=c, ksl=ksl):
                            last = None
                            for u in range(4):
                                n = 2 * c + 1
                                last = e.tensor_scalar(
                                    ktok[:, ksl + 128 * u + 64:ksl + 128 * u + 128],
                                    P[3][:, 128 * u:128 * u + 64],
                                    gmn4_sb[:, 4 * n + u:4 * n + u + 1],
                                    None, ALU.mult)
                            return last
                        m_ktevA = p.do("scalar", ktevA,
                                       waits=[m_ktA, m_gmn4, m_Mmm[c]], mark=True)
                        m_ktevB = p.do("vector", ktevB,
                                       waits=[m_kt, m_gmn4, m_Mmm[c]], mark=True)
                        m_ktev = m_ktevB
                        m_ktevA_last = m_ktevA
                        m_ktok_prev = m_ktevA

                    if STAGE < 4:
                        continue
                    ret_free = [m_gate[6], m_gate[7], m_ret_bank_prev[0],
                                m_ret_bank_prev[1]]
                    m_sev = [None] * 4
                    m_ret_i = [None] * 4
                    m_s = None
                    pending_rmm = None
                    if j > 0:
                        def crossmm(e, c=c, qsl=qsl):
                            e.matmul(P[4][0:64, :], Ms_b[0:64, 64 * c:64 * c + 64],
                                     qt[0:64, qsl:qsl + 512], start=True, stop=False,
                                     tile_position=(0, 0), skip_group_check=True)
                            return e.matmul(P[5][64:128, :],
                                            Ms_b[64:128, 64 * c:64 * c + 64],
                                            qt[64:128, qsl:qsl + 512], start=True,
                                            stop=False, tile_position=(64, 64),
                                            skip_group_check=True)
                        m_cross = p.do("tensor", crossmm,
                                       waits=m_qkev[c] + [m_Mcast[c]] + ret_free,
                                       mark=True)
                    if False:
                        def mupd(e, c=c, gA=gA, gB=gB):
                            e.tensor_add(Ms_f[0:64, 64 * c:64 * c + 64],
                                         Ms_f[0:64, 64 * c:64 * c + 64],
                                         P[5][0:64, 0:64])
                            e.tensor_add(Ms_f[64:128, 64 * c:64 * c + 64],
                                         Ms_f[64:128, 64 * c:64 * c + 64],
                                         P[4][64:128, 0:64])
                            e.drain()
                            e.tensor_scalar(Ms_f[0:64, 64 * c:64 * c + 64],
                                            Ms_f[0:64, 64 * c:64 * c + 64],
                                            float(gA ** 512), None, ALU.mult)
                            e.tensor_scalar(Ms_f[64:128, 64 * c:64 * c + 64],
                                            Ms_f[64:128, 64 * c:64 * c + 64],
                                            float(gB ** 512), None, ALU.mult)
                            e.drain()
                            return e.tensor_copy(Ms_b[:, 64 * c:64 * c + 64],
                                                 Ms_f[:, 64 * c:64 * c + 64])
                        m_Mcast[c] = p.do("vector", mupd,
                                          waits=[m_kv, m_Minit,
                                                 m_cross if j > 0 else None],
                                          mark=True)
                    for kx in range(4):
                        pa = P[0] if kx % 2 == 0 else P[2]
                        pb = P[1] if kx % 2 == 0 else P[3]
                        if kx >= 2:
                            bank_free = list(m_sev[kx - 2])
                        elif kx == 0:
                            bank_free = m_qkev[c] + list(prev_sev[0] or ())
                        else:
                            bank_free = ([m_ktevB] if not last_j else [m_vevB]) + \
                                        list(prev_sev[1] or ())
                        def smmA(e, c=c, kx=kx, pa=pa, qsl=qsl):
                            return e.matmul(
                                pa[:, :],
                                kst[0:64, 512 * c + 128 * kx:512 * c + 128 * kx + 128],
                                qt[0:64, qsl:qsl + 512], start=True, stop=True,
                                tile_position=(0, 0))
                        def smmB(e, c=c, kx=kx, pb=pb, qsl=qsl):
                            return e.matmul(
                                pb[:, :],
                                kst[64:128, 512 * c + 128 * kx:512 * c + 128 * kx + 128],
                                qt[64:128, qsl:qsl + 512], start=True,
                                stop=True, tile_position=(64, 0))
                        m_sA = p.do("tensor", smmA,
                                    waits=m_qkev[c] +
                                          [m_ret_i[kx - 2] if kx >= 2 else None] +
                                          bank_free,
                                    mark=True)
                        m_s = p.do("tensor", smmB, mark=True)
                        ssl = kx * 1024
                        slot_free = m_slot_prev[kx]
                        col0 = 128 * kx
                        moff = (0, 512, 896, 1152)[kx]
                        w = 512 - col0
                        def sevA(e, pa=pa, ssl=ssl, kx=kx, nA=nA, col0=col0,
                                 moff=moff, w=w):
                            return e.scalar_tensor_tensor(
                                s_sb[:, ssl + col0:ssl + 512],
                                pa[:, col0:512],
                                gmn4_sb[:, 4 * nA + kx:4 * nA + kx + 1],
                                masks_sb[:, moff:moff + w], ALU.mult, ALU.mult)
                        def sevB(e, pb=pb, ssl=ssl, kx=kx, nB=nB, col0=col0,
                                 moff=moff, w=w):
                            return e.scalar_tensor_tensor(
                                s_sb[:, ssl + 512 + col0:ssl + 1024],
                                pb[:, col0:512],
                                gmn4_sb[:, 4 * nB + kx:4 * nB + kx + 1],
                                masks_sb[:, moff:moff + w], ALU.mult, ALU.mult)
                        marks = [p.do("vector", sevA,
                                      waits=[m_sA, slot_free, m_masks, m_gmn4],
                                      mark=True),
                                 p.do("vector", sevB,
                                      waits=[m_s, slot_free, m_masks, m_gmn4],
                                      mark=True)]
                        m_sev[kx] = tuple(marks)
                        is_start = (j == 0 and kx == 0)
                        is_stop = (kx == 3)
                        def rmm(e, c=c, kx=kx, ssl=ssl, nA=nA, nB=nB,
                                is_start=is_start, is_stop=is_stop, col0=col0):
                            e.matmul(P[4][0:64, col0:512],
                                     vst[:, 1024 * kx + 64 * nA:1024 * kx + 64 * nA + 64],
                                     s_sb[:, ssl + col0:ssl + 512],
                                     start=is_start, stop=is_stop,
                                     tile_position=(0, 0), skip_group_check=True)
                            return e.matmul(
                                P[5][64:128, col0:512],
                                vst[:, 1024 * kx + 64 * nB:1024 * kx + 64 * nB + 64],
                                s_sb[:, ssl + 512 + col0:ssl + 1024],
                                start=is_start, stop=is_stop,
                                tile_position=(0, 64), skip_group_check=True)
                        if pending_rmm is not None:
                            pi, pf = pending_rmm
                            m_ret_i[pi] = p.do("tensor", pf,
                                               waits=list(m_sev[pi]) + m_vevc +
                                                     (ret_free if (j == 0 and pi == 0)
                                                      else []),
                                               mark=True)
                        pending_rmm = (kx, rmm)
                    pi, pf = pending_rmm
                    m_ret_i[pi] = p.do("tensor", pf,
                                       waits=list(m_sev[pi]) + m_vevc,
                                       mark=True)
                    prev_sev = [m_sev[2], m_sev[3]]
                    m_slot_prev = list(m_ret_i)
                    m_scores_last[c] = m_s
                    m_vst_guard[c] = m_ret_i[3]
                    m_kst_guard[c] = m_s

                    if j > 0:
                        m_Mmm[c] = m_cross
                    m_revA = p.do("scalar", lambda e, c=c: e.copy(
                        retj[0:64, 512 * c:512 * c + 512], P[4][0:64, :]),
                        waits=[m_ret_i[3], m_sq_prev[c % 2]], mark=True)
                    m_revB = p.do("scalar", lambda e, c=c: e.copy(
                        retj[64:128, 512 * c:512 * c + 512], P[5][64:128, :]),
                        waits=[m_ret_i[3], m_sq_prev[c % 2]], mark=True)
                    m_rev = m_revB
                    m_retj_ev[c] = m_revB
                    m_ret_bank_prev[0] = m_revB
                    m_ret_bank_prev[1] = Mark("m_scalar", m_revA.value)
                    sqs = 1024 + (c % 2) * 512
                    m_sq = p.do("scalar", lambda e, c=c, sqs=sqs: e.activation(
                        scr16[:, sqs:sqs + 512], retj[:, 512 * c:512 * c + 512],
                        AF.Square), waits=[m_rev, m_revA], mark=True)
                    def gnstat(e, c=c, sqs=sqs):
                        e.matmul(P[7][0:16, :], sel16_sb[:, 16 * c:16 * c + 16],
                                 retj[:, 512 * c:512 * c + 512],
                                 start=(c == 0), stop=(c == 7))
                        return e.matmul(P[7][32:48, :],
                                        sel16_sb[:, 16 * c:16 * c + 16],
                                        scr16[:, sqs:sqs + 512],
                                        start=(c == 0), stop=(c == 7),
                                        tile_position=(0, 32),
                                        skip_group_check=True)
                    if c < 7:
                        pending_gnstat = (gnstat, [m_rev, m_revA, m_sq, m_sel16])
                        m_sq_prev[c % 2] = Mark("m_scalar", m_sq.value)
                    else:
                        m_gn_stat = p.do("tensor", gnstat,
                                         waits=[m_rev, m_revA, m_sq, m_sel16],
                                         mark=True)
                        m_sq_prev[c % 2] = m_gn_stat
                    if not last_j:
                        def kvmm(e, c=c, ksl=ksl, nA=nA, nB=nB):
                            last = None
                            for u in range(4):
                                e.matmul(P[5][0:64, 0:64],
                                         ktok[:, ksl + 128 * u:ksl + 128 * u + 64],
                                         vst[:, 1024 * u + 64 * nA:1024 * u + 64 * nA + 64],
                                         start=(u == 0), stop=(u == 3),
                                         tile_position=(0, 0), skip_group_check=True)
                                last = e.matmul(
                                    P[4][64:128, 0:64],
                                    ktok[:, ksl + 128 * u + 64:ksl + 128 * u + 128],
                                    vst[:, 1024 * u + 64 * nB:1024 * u + 64 * nB + 64],
                                    start=(u == 0), stop=(u == 3),
                                    tile_position=(0, 64), skip_group_check=True)
                            return last
                        m_kv = p.do("tensor", kvmm,
                                    waits=[m_ktevA, m_ktevB, m_Mcast[c]] + m_vevc,
                                    mark=True)
                        m_vst_guard[c] = m_kv
                        def mupd(e, c=c, gA=gA, gB=gB):
                            e.tensor_add(Ms_f[0:64, 64 * c:64 * c + 64],
                                         Ms_f[0:64, 64 * c:64 * c + 64],
                                         P[5][0:64, 0:64])
                            e.tensor_add(Ms_f[64:128, 64 * c:64 * c + 64],
                                         Ms_f[64:128, 64 * c:64 * c + 64],
                                         P[4][64:128, 0:64])
                            e.drain()
                            e.tensor_scalar(Ms_f[0:64, 64 * c:64 * c + 64],
                                            Ms_f[0:64, 64 * c:64 * c + 64],
                                            float(gA ** 512), None, ALU.mult)
                            e.tensor_scalar(Ms_f[64:128, 64 * c:64 * c + 64],
                                            Ms_f[64:128, 64 * c:64 * c + 64],
                                            float(gB ** 512), None, ALU.mult)
                            e.drain()
                            return e.tensor_copy(Ms_b[:, 64 * c:64 * c + 64],
                                                 Ms_f[:, 64 * c:64 * c + 64])
                        m_Mcast[c] = p.do("vector", mupd,
                                          waits=[m_kv, m_Minit,
                                                 m_cross if j > 0 else None],
                                          mark=True)
                        m_ret_bank_prev[0] = m_revB
                        m_ret_bank_prev[1] = Mark("m_scalar", m_revA.value)

                if STAGE < 5:
                    break
                def gnmath(e):
                    e.tensor_scalar(stat_f[0:16, 0:512], P[7][0:16, :], 1.0 / 64, None, ALU.mult)
                    e.tensor_scalar(stat_f[0:16, 512:1024], P[7][32:48, :], 1.0 / 64, None, ALU.mult)
                    e.drain()
                    e.tensor_mul(scrF[0:16, 0:512], stat_f[0:16, 0:512], stat_f[0:16, 0:512])
                    e.drain()
                    e.tensor_sub(stat_f[0:16, 512:1024], stat_f[0:16, 512:1024],
                                 scrF[0:16, 0:512])
                    e.drain()
                    return e.tensor_scalar(stat_f[0:16, 512:1024],
                                           stat_f[0:16, 512:1024], EPS, None, ALU.add)
                m_gmath = p.do("vector", gnmath, waits=[m_gn_stat], mark=True)
                m_gstd = p.do("scalar", lambda e: e.activation(
                    scrF[0:16, 512:1024], stat_f[0:16, 512:1024], AF.Sqrt),
                    waits=[m_gmath], mark=True)
                def gfin(e):
                    e.reciprocal(stat_f[0:16, 512:1024], scrF[0:16, 512:1024])
                    e.drain()
                    e.tensor_copy(stat_b[0:16, 0:512], stat_f[0:16, 0:512])
                    return e.tensor_copy(stat_b[0:16, 512:1024], stat_f[0:16, 512:1024])
                m_gfin = p.do("vector", gfin, waits=[m_gstd], mark=True)

                hoist_fin = None
                if j + 1 < nj:
                    ln1_ctx = ln_partA(m_x_next, P[4], P[5],
                                       xoff=4096 * ((j + 1) % 2),
                                       bufs=(ln1_f, ln1scr, ln1_b),
                                       first_waits=[m_retj_ev[7]])
                    hoist_fin = ln1_ctx[1]

                m_om = [None] * 8
                prev_gnb = [None, None, None]
                prev_op = [m_retj_ev[7], m_retj_ev[7]]
                def omk01(m, k, extra=()):
                    def omk(e, m=m, k=k, xlo=xlo):
                        return e.matmul(
                            (P[4] if m == 0 else P[5])[:, :],
                            w2a_sb[:, 1024 * k + 128 * m:1024 * k + 128 * m + 128],
                            xlnj[:, xlo + 512 * k:xlo + 512 * k + 512],
                            start=(k == 0), stop=(k == 7))
                    return p.do("tensor", omk,
                                waits=[m_w2a, m_om[k]] + list(extra),
                                mark=(k == 7))
                for c in range(8):
                    bA = (P[0], P[2], P[6])[c % 3]
                    bB = (P[1], P[3], P[7])[c % 3]
                    def gnb(e, c=c, bA=bA, bB=bB):
                        e.matmul(bA[:, :], selb_sb[0:16, 128 * c:128 * c + 128],
                                 stat_b[0:16, 0:512], start=True, stop=True)
                        return e.matmul(bB[:, :], selb_sb[0:16, 128 * c:128 * c + 128],
                                        stat_b[0:16, 512:1024], start=True, stop=True)
                    m_b = p.do("tensor", gnb,
                               waits=[m_gfin, m_selb, prev_gnb[c % 3]] +
                                     ([m_ktevA_last] if c == 2 else []) +
                                     list(prev_sev[0] or ()) +
                                     list(prev_sev[1] or ()), mark=True)
                    def gapply(e, c=c, bA=bA, bB=bB, xlo=xlo):
                        sl = 1024 + (c % 2) * 512
                        e.tensor_tensor(scr16[:, sl:sl + 512],
                                        retj[:, 512 * c:512 * c + 512], bA[:, :],
                                        ALU.subtract)
                        e.drain()
                        e.scalar_tensor_tensor(scr16[:, sl:sl + 512],
                                               scr16[:, sl:sl + 512],
                                               gns_sb[:, c:c + 1], bB[:, :],
                                               ALU.mult, ALU.mult)
                        e.drain()
                        return e.tensor_add(xlnj[:, xlo + 512 * c:xlo + 512 * c + 512],
                                            scr16[:, sl:sl + 512],
                                            gatej[:, 512 * c:512 * c + 512])
                    m_om[c] = p.do("vector", gapply,
                                   waits=[m_b, m_gate[c], m_gns, m_gnb], mark=True)
                    prev_gnb[c % 3] = m_om[c]
                    if c >= 3:
                        omk01(0, c - 3, extra=([prev_op[0], hoist_fin]
                                               if c == 3 else []))
                        omk01(1, c - 3,
                              extra=([prev_op[1]] +
                                     ([ln1_ctx[3]] if j + 1 < nj else [])
                                     if c == 3 else []))

                if STAGE < 6:
                    break
                m_x1 = [None] * 8
                prev_sq2 = [None, None]
                mm_stat2 = None
                for m in range(8):
                    bank = P[4] if m % 2 == 0 else P[5]
                    m_o = None
                    for k in (range(5, 8) if m < 2 else range(8)):
                        if m < 2:
                            m_o = omk01(m, k)
                            continue
                        def omk(e, m=m, k=k, bank=bank, xlo=xlo):
                            return e.matmul(
                                bank[:, :],
                                w2a_sb[:, 1024 * k + 128 * m:1024 * k + 128 * m + 128],
                                xlnj[:, xlo + 512 * k:xlo + 512 * k + 512],
                                start=(k == 0), stop=(k == 7))
                        m_o = p.do("tensor", omk,
                                   waits=[m_w2a, m_om[k]] +
                                         ([prev_op[m % 2]] if k == 0 else []),
                                   mark=(k == 7))
                    m_x1[m] = p.do("vector", lambda e, m=m, bank=bank, xo=xo:
                        e.tensor_add(
                        xj[:, xo + 512 * m:xo + 512 * m + 512],
                        xj[:, xo + 512 * m:xo + 512 * m + 512],
                        bank[:, :]), waits=[m_o], mark=True)
                    prev_op[m % 2] = m_x1[m]
                    if STAGE >= 7:
                        def emit_stat2(c, sq2s):
                            def statmm2(e, c=c, xo=xo):
                                e.matmul(P[6][0:1, :], onesc_sb[:, 0:1],
                                         xj[:, xo + 512 * c:xo + 512 * c + 512],
                                         start=(c == 0), stop=(c == 7))
                                return e.matmul(P[7][0:1, :], onesc_sb[:, 0:1],
                                                scr16[:, (c % 2) * 512:
                                                       (c % 2) * 512 + 512],
                                                start=(c == 0), stop=(c == 7))
                            return p.do("tensor", statmm2,
                                        waits=[m_x1[c], sq2s[c], m_onesc] +
                                              ([m_om[5]] if c == 0 else []),
                                        mark=True)
                        if m == 0:
                            msq2_l = [None] * 8
                        if m >= 2:
                            prev_sq2[m % 2] = emit_stat2(m - 2, msq2_l)
                        msq2_l[m] = p.do("scalar", lambda e, m=m, xo=xo:
                                    e.activation(
                                        scr16[:, (m % 2) * 512:(m % 2) * 512 + 512],
                                        xj[:, xo + 512 * m:xo + 512 * m + 512],
                                        AF.Square),
                                    waits=[m_x1[m], prev_sq2[m % 2]], mark=True)
                        if m == 7:
                            emit_stat2(6, msq2_l)
                            mm_stat2 = emit_stat2(7, msq2_l)

                if j + 1 < nj:
                    ln1_next = ln_partB(ln1_ctx, P[0], P[1],
                                        ("ln1", 4096 * ((j + 1) % 2)),
                                        bc_waits=[m_om[6]])

                if STAGE < 7:
                    break
                if j + 1 < nj:
                    ln1 = ln1_next
                ctx2 = ln_tail(P[6], P[7], xo, (stat_f, scrF, stat_b), mm_stat2)
                pending_ln2B = (ctx2, j, xo, m_x1)
                if j == nj - 1:
                    emit_ln2B()

        if STAGE >= 8:
            wscope1.close()
            wgu_scope = contextlib.ExitStack()
            wgu_sb = wgu_scope.enter_context(
                nc.sbuf_tensor("wgu_sb", [128, 16384], BF16, side="right"))
            m_wgu = wst.dma(lambda e: e.dma_start(
                out=wgu_sb[:, :].rearrange("p (k f) -> p k f", k=8),
                in_=dwgu[:, :].rearrange("(k p) f -> p k f", p=128)),
                waits=[m_o, m_gate[7]])

            m_zero = p.do("vector", lambda e: e.memset(comb_b[:, :], 0.0), mark=True)
            m_lg = [None] * nt
            prev_rps = [None, None]
            for u in range(nt):
                bank = P[0] if u % 2 == 0 else P[1]
                def rmm(e, u=u, bank=bank):
                    last = None
                    for k in range(8):
                        last = e.matmul(
                            bank[:, 0:9],
                            x2st[:, ST * k + 128 * u:ST * k + 128 * u + 128],
                            rout_sb[:, 9 * k:9 * k + 9],
                            start=(k == 0), stop=(k == 7))
                    return last
                m_r = p.do("tensor", rmm,
                           waits=[m_rout, prev_rps[u % 2]] +
                                 (([m_om[6]] + m_ln2_prev) if u < 2 else []),
                           mark=True)
                def lgev(e, u=u, bank=bank):
                    e.tensor_copy(lgb[:, 8 * u:8 * u + 8], bank[:, 0:8])
                    return e.tensor_copy(lgb[:, 128 + u:129 + u], bank[:, 8:9])
                m_lg[u] = p.do("vector", lgev, waits=[m_r], mark=True)
                prev_rps[u % 2] = m_lg[u]
            lg3 = lgb[:, 0:8 * nt].rearrange("p (u e) -> p u e", e=8)
            sc3 = lgb[:, 160:160 + 8 * nt].rearrange("p (u e) -> p u e", e=8)
            def bc(col):
                return comb_scr[:, col:col + nt].unsqueeze(2).broadcast_to(
                    [128, nt, 8])
            def smax1(e):
                e.reduce_max(comb_scr[:, 16:16 + nt], lg3, axis=AXX)
                e.drain()
                return e.tensor_tensor(lg3, lg3, bc(16), ALU.subtract)
            m_s1 = p.do("vector", smax1, waits=m_lg, mark=True)
            def sexp(e):
                e.activation(lgb[:, 0:8 * nt], lgb[:, 0:8 * nt], AF.Exp)
                return e.activation(comb_scr[:, 32:32 + nt],
                                    lgb[:, 128:128 + nt], AF.Sigmoid)
            m_e = p.do("scalar", sexp, waits=[m_s1], mark=True)
            def smax2(e):
                e.reduce_sum(comb_scr[:, 0:nt], lg3, axis=AXX)
                e.drain()
                e.reciprocal(comb_scr[:, 0:nt], comb_scr[:, 0:nt])
                e.drain()
                e.tensor_tensor(lg3, lg3, bc(0), ALU.mult)
                e.drain()
                e.reduce_max(comb_scr[:, 16:16 + nt], lg3, axis=AXX)
                e.drain()
                e.tensor_tensor(sc3, lg3, bc(16), ALU.is_equal)
                e.drain()
                e.scalar_tensor_tensor(sc3, sc3, -2.0, lg3, ALU.mult, ALU.add)
                e.drain()
                e.reduce_max(comb_scr[:, 48:48 + nt], sc3, axis=AXX)
                e.drain()
                e.tensor_tensor(sc3, lg3, bc(48), ALU.is_ge)
                e.drain()
                return e.tensor_tensor(lg3, lg3, sc3, ALU.mult)
            m_s2 = p.do("vector", smax2, waits=[m_e, m_zero], mark=True)
            m_trE = [None] * (nt // 4)
            m_tmm = [None] * nt
            for u in range(nt):
                slot = (u % 2) * 16
                def cb(e, u=u, slot=slot):
                    e.tensor_copy(comb_b[:, slot:slot + 8],
                                  lgb[:, 8 * u:8 * u + 8])
                    return e.tensor_copy(comb_b[:, slot + 8:slot + 9],
                                         comb_scr[:, 32 + u:33 + u])
                m_c = p.do("vector", cb,
                           waits=[m_s2, m_tmm[u - 2] if u > 1 else None],
                           mark=True)
                jg = u // 4
                def tmm(e, u=u, slot=slot, jg=jg):
                    return e.matmul(
                        P[jg % 2][0:9, 128 * (u % 4):128 * (u % 4) + 128],
                        comb_b[:, slot:slot + 9], eye_sb[:, :],
                        start=True, stop=True)
                m_tmm[u] = p.do("tensor", tmm, waits=[m_c, m_eye], mark=True)
                if u % 4 == 3:
                    m_trE[jg] = p.do(
                        "vector", lambda e, jg=jg: e.tensor_copy(
                            combT[0:9, 512 * jg:512 * jg + 512],
                            P[jg % 2][0:9, 0:512]),
                        waits=[m_tmm[u]], mark=True)
            m_tr_all = m_trE[nt // 4 - 1]

        if STAGE < 8:
            wst.dma(lambda e: e.dma_start(
                out=dout[:, :].rearrange("(c p) s -> p c s", p=128)[:, :, 0:512],
                in_=xj[:, 0:4096].rearrange("p (c s) -> p c s", c=8)),
                waits=[m_x, Mark("m_tensor", p.counts["tensor"]),
                       Mark("m_vector", p.counts["vector"]),
                       Mark("m_scalar", p.counts["scalar"])])
            p.do("sync", lambda e: e.nop(),
                 waits=wst.final_marks() + spl.final_marks() + ldx.final_marks()
                 + ldc.final_marks() + ldg.final_marks())
            p.finalize(block)
            p.close()
            st.close()
            return nc

        p.barrier(extra=spl.final_marks() + ldx.final_marks() + ldc.final_marks()
                  + wst.final_marks())
        with contextlib.ExitStack() as fscope:
            fsb = lambda name, shape, dt: fscope.enter_context(
                nc.sbuf_tensor(name, shape, dt))
            acc = fsb("acc", [128, 8 * ST], F32)
            sgu_j = fsb("sgu_j", [128, 4096], BF16)
            sg_scr = fsb("sg_scr", [128, 1536], BF16)
            hs_scr = fsb("hs_scr", [128, 1536], BF16)
            x1rl = fsb("x1rl", [128, 4096], BF16)


            m_acc = {}
            with contextlib.ExitStack() as wscope:
                wdn_sb = wscope.enter_context(
                    nc.sbuf_tensor("wdn_sb", [128, 8192], BF16))
                m_wdn = wst.dma(lambda e: e.dma_start(
                    out=wdn_sb[:, :].rearrange("p (k f) -> p k f", k=8),
                    in_=dwdn[:, :].rearrange("(k p) f -> p k f", p=128)))

                prev_silG = [None, None]
                prev_multU = [None, None]
                prev_dn = [None, None]
                prev_bc = None
                prev_bgm = [None, None]
                for j in range(nj):
                    csl = (j % 2) * 512
                    m_cb = cbr.dma(lambda e, j=j, csl=csl: e.dma_start(
                        out=combR[0:1, csl:csl + 512],
                        in_=combT[8:9, 512 * j:512 * j + 512]),
                        waits=[m_trE[j], prev_bgm[j % 2]])
                    m_sgu = [None] * 8
                    for m in range(8):
                        bG = P[3] if m % 2 == 0 else P[4]
                        bU = P[0] if m % 2 == 0 else P[1]
                        def gu1(e, m=m, j=j, bG=bG):
                            last = None
                            for k in range(8):
                                last = e.matmul(
                                    bG[:, :],
                                    wgu_sb[:, 2048 * k + 128 * m:2048 * k + 128 * m + 128],
                                    x2st[:, ST * k + 512 * j:ST * k + 512 * j + 512],
                                    start=(k == 0), stop=(k == 7))
                            return last
                        m_g1 = p.do("tensor", gu1,
                                    waits=[m_wgu, prev_silG[m % 2]], mark=True)
                        def gu2(e, m=m, j=j, bU=bU):
                            last = None
                            for k in range(8):
                                last = e.matmul(
                                    bU[:, :],
                                    wgu_sb[:, 2048 * k + 128 * (m + 8):2048 * k + 128 * (m + 8) + 128],
                                    x2st[:, ST * k + 512 * j:ST * k + 512 * j + 512],
                                    start=(k == 0), stop=(k == 7))
                            return last
                        m_g2 = p.do("tensor", gu2,
                                    waits=[m_wgu, prev_multU[m % 2]] +
                                          ([m_trE[max(g for g in
                                                      range(nt // 4)
                                                      if g % 2 == m)]]
                                           if j == 0 and m < 2 else []),
                                    mark=True)
                        m_gu_last = m_g2
                        m_sil = p.do("scalar", lambda e, m=m, bG=bG: e.activation(
                            sg_scr[:, 512 * (m % 2):512 * (m % 2) + 512],
                            bG[:, :], AF.Silu),
                            waits=[m_g1, m_sgu[m - 2] if m > 1 else None],
                            mark=True)
                        prev_silG[m % 2] = m_sil
                        m_sgu[m] = p.do("vector",
                            lambda e, m=m, bU=bU:
                            e.tensor_tensor(sgu_j[:, 512 * m:512 * m + 512],
                                            bU[:, :],
                                            sg_scr[:, 512 * (m % 2):512 * (m % 2) + 512],
                                            ALU.mult),
                            waits=[m_g2, m_sil], mark=True)
                        prev_multU[m % 2] = m_sgu[m]
                    def bgmm(e, csl=csl):
                        return e.matmul(P[2][:, :], ones1_sb[0:1, :],
                                        combR[0:1, csl:csl + 512],
                                        start=True, stop=True)
                    m_bgm = p.do("tensor", bgmm, waits=[m_cb, prev_bc],
                                 mark=True)
                    prev_bgm[j % 2] = m_bgm
                    m_bge = p.do("vector", lambda e: e.tensor_copy(
                        bcomb[:, :], P[2][:, :]), waits=[m_bgm], mark=True)
                    for m2 in range(8):
                        bank = P[5] if m2 % 2 == 0 else P[6]
                        def dmm(e, m2=m2, bank=bank):
                            last = None
                            for f in range(8):
                                last = e.matmul(
                                    bank[:, :],
                                    wdn_sb[:, 1024 * f + 128 * m2:1024 * f + 128 * m2 + 128],
                                    sgu_j[:, 512 * f:512 * f + 512],
                                    start=(f == 0), stop=(f == 7))
                            return last
                        m_d = p.do("tensor", dmm,
                                   waits=[m_wdn, prev_dn[m2 % 2]] + m_sgu, mark=True)
                        m_acc[(m2, j)] = p.do(
                            "vector", lambda e, m2=m2, j=j, bank=bank:
                            e.tensor_tensor(
                                acc[:, ST * m2 + 512 * j:ST * m2 + 512 * j + 512],
                                bank[:, :], bcomb[:, :], ALU.mult),
                            waits=[m_d, m_bge], mark=True)
                        prev_dn[m2 % 2] = m_acc[(m2, j)]
                    prev_bc = m_acc[(7, j)]
                m_shared_t = Mark("m_tensor", p.counts["tensor"])
                m_shared_v = m_acc[(7, nj - 1)]
                m_gu_done = m_gu_last
                m_sgu_last = Mark("m_vector", max(mk.value for mk in m_sgu))
                m_bge_last = m_bge
            wgu_scope.close()

            with contextlib.ExitStack() as escope:
                we1_sb = escope.enter_context(
                    nc.sbuf_tensor("we1_sb", [128, 8192], BF16, side="right"))
                we2_sb = escope.enter_context(
                    nc.sbuf_tensor("we2_sb", [128, 8192], BF16, side="right"))
                htil = escope.enter_context(
                    nc.sbuf_tensor("htil", [128, 8192], BF16))
                m_we1 = wst.dma(lambda e: e.dma_start(
                    out=we1_sb[:, :].rearrange("p (k f) -> p k f", k=8),
                    in_=dwe1[0, :, :].rearrange("(k p) f -> p k f", p=128)),
                    waits=[m_gu_done])
                m_we2 = wst.dma(lambda e: e.dma_start(
                    out=we2_sb[:, :].rearrange("p (k f) -> p k f", k=8),
                    in_=dwe2[0, :, :].rearrange("(k p) f -> p k f", p=128)),
                    waits=[m_gu_done])
                prev_h = [m_sgu_last, m_sgu_last, m_sgu_last]
                prev_y = [m_shared_v, m_shared_v, m_sgu_last, m_sgu_last]
                prev_bc2 = m_bge_last
                prev_bcomb_rd = m_shared_v
                prev_bcm = [m_shared_t, m_shared_t]
                pend_y = None
                m_htil_hist = {}
                m_we2_by_e = {0: m_we2}
                blocks = [(e_i, j) for e_i in range(NE) for j in range(nj)]
                def stage_cb(b):
                    e_i, j = blocks[b]
                    csl = (b % 2) * 512
                    return cbr.dma(lambda e, e_i=e_i, j=j, csl=csl: e.dma_start(
                        out=combR[0:1, csl:csl + 512],
                        in_=combT[e_i:e_i + 1, 512 * j:512 * j + 512]),
                        waits=[m_trE[j], prev_bcm[b % 2]])
                m_cb_next = stage_cb(0)
                for e_i in range(NE):
                    m_last_h = None
                    m_last_y = None
                    for j in range(nj):
                        b = e_i * nj + j
                        slot = (b % 2) * 4096
                        csl = (b % 2) * 512
                        m_cb = m_cb_next
                        def bcmm(e, csl=csl):
                            return e.matmul(P[2][:, :], ones1_sb[0:1, :],
                                            combR[0:1, csl:csl + 512],
                                            start=True, stop=True)
                        m_bcm = p.do("tensor", bcmm, waits=[m_cb, prev_bc2],
                                     mark=True)
                        prev_bcm[b % 2] = m_bcm
                        if b + 1 < len(blocks):
                            m_cb_next = stage_cb(b + 1)
                        m_bce = p.do("vector", lambda e: e.tensor_copy(
                            bcomb[:, :], P[2][:, :]),
                            waits=[m_bcm, prev_bcomb_rd], mark=True)
                        m_htil = [None] * 8
                        guard = m_htil_hist.get(slot)
                        for f in range(8):
                            bank = (P[3], P[4], P[7])[f % 3]
                            def hmm(e, f=f, j=j, bank=bank):
                                last = None
                                for k in range(8):
                                    last = e.matmul(
                                        bank[:, :],
                                        we1_sb[:, 1024 * k + 128 * f:1024 * k + 128 * f + 128],
                                        x2st[:, ST * k + 512 * j:ST * k + 512 * j + 512],
                                        start=(k == 0), stop=(k == 7))
                                return last
                            m_h = p.do("tensor", hmm,
                                       waits=[m_we1, prev_h[f % 3]], mark=True)
                            m_last_h = m_h
                            hsl = (f % 3) * 512
                            m_hs = p.do("scalar", lambda e, hsl=hsl, bank=bank:
                                        e.activation(hs_scr[:, hsl:hsl + 512],
                                                     bank[:, :], AF.Silu),
                                        waits=[m_h,
                                               m_htil[f - 3] if f >= 3 else None],
                                        mark=True)
                            m_htil[f] = p.do(
                                "vector", lambda e, f=f, hsl=hsl, slot=slot:
                                e.tensor_tensor(htil[:, slot + 512 * f:slot + 512 * f + 512],
                                                hs_scr[:, hsl:hsl + 512],
                                                bcomb[:, :], ALU.mult),
                                waits=[m_hs, m_bce,
                                       guard[f] if guard else None], mark=True)
                            prev_h[f % 3] = m_htil[f]
                        m_htil_hist[slot] = None
                        if pend_y is not None:
                            pe, pj, pslot, phtil = pend_y
                            for m in range(8):
                                bank = (P[5], P[6], P[0], P[1])[m % 4]
                                def ymm(e, m=m, pslot=pslot, bank=bank):
                                    last = None
                                    for f in range(8):
                                        last = e.matmul(
                                            bank[:, :],
                                            we2_sb[:, 1024 * f + 128 * m:1024 * f + 128 * m + 128],
                                            htil[:, pslot + 512 * f:pslot + 512 * f + 512],
                                            start=(f == 0), stop=(f == 7))
                                    return last
                                m_y = p.do("tensor", ymm,
                                           waits=[m_we2_by_e[pe], prev_y[m % 4]] + phtil,
                                           mark=True)
                                m_last_y = m_y
                                m_acc[(m, pj)] = p.do(
                                    "vector", lambda e, m=m, pj=pj, bank=bank:
                                    e.tensor_add(
                                        acc[:, ST * m + 512 * pj:ST * m + 512 * pj + 512],
                                        acc[:, ST * m + 512 * pj:ST * m + 512 * pj + 512],
                                        bank[:, :]),
                                    waits=[m_y, m_acc[(m, pj)]], mark=True)
                                prev_y[m % 4] = m_acc[(m, pj)]
                            m_htil_hist[pslot] = [Mark("m_tensor", m_last_y.value)] * 8
                            if pj == nj - 1 and pe + 1 < NE:
                                m_we2_by_e[pe + 1] = wst.dma(
                                    lambda e, pe=pe: e.dma_start(
                                        out=we2_sb[:, :].rearrange("p (k f) -> p k f", k=8),
                                        in_=dwe2[pe + 1, :, :].rearrange("(k p) f -> p k f", p=128)),
                                    waits=[m_last_y])
                        pend_y = (e_i, j, slot, m_htil)
                        prev_bc2 = Mark("m_vector", max(mk.value for mk in m_htil))
                        prev_bcomb_rd = prev_bc2
                    if e_i + 1 < NE:
                        m_we1 = wst.dma(lambda e, e_i=e_i: e.dma_start(
                            out=we1_sb[:, :].rearrange("p (k f) -> p k f", k=8),
                            in_=dwe1[e_i + 1, :, :].rearrange("(k p) f -> p k f", p=128)),
                            waits=[m_last_h])
                pe, pj, pslot, phtil = pend_y
                for m in range(8):
                    bank = (P[5], P[6], P[0], P[1])[m % 4]
                    def ymm(e, m=m, pslot=pslot, bank=bank):
                        last = None
                        for f in range(8):
                            last = e.matmul(
                                bank[:, :],
                                we2_sb[:, 1024 * f + 128 * m:1024 * f + 128 * m + 128],
                                htil[:, pslot + 512 * f:pslot + 512 * f + 512],
                                start=(f == 0), stop=(f == 7))
                        return last
                    m_y = p.do("tensor", ymm,
                               waits=[m_we2_by_e[pe], prev_y[m % 4]] + phtil, mark=True)
                    m_acc[(m, pj)] = p.do(
                        "vector", lambda e, m=m, pj=pj, bank=bank:
                        e.tensor_add(
                            acc[:, ST * m + 512 * pj:ST * m + 512 * pj + 512],
                            acc[:, ST * m + 512 * pj:ST * m + 512 * pj + 512],
                            bank[:, :]),
                        waits=[m_y, m_acc[(m, pj)]], mark=True)
                    prev_y[m % 4] = m_acc[(m, pj)]

                prev_rl = None
                for j in range(nj):
                    m_rl = ldx.dma(lambda e, j=j: e.dma_start(
                        out=x1rl[:, :].rearrange("p (c s) -> p c s", c=8),
                        in_=dx1s[:, :].rearrange("(c p) s -> p c s", p=128)
                            [:, :, 512 * j:512 * j + 512]),
                        waits=[m_spill, prev_rl])
                    m_f = [None] * 8
                    for m in range(8):
                        m_f[m] = p.do("vector", lambda e, m=m, j=j: e.tensor_add(
                            acc[:, ST * m + 512 * j:ST * m + 512 * j + 512],
                            acc[:, ST * m + 512 * j:ST * m + 512 * j + 512],
                            x1rl[:, 512 * m:512 * m + 512]),
                            waits=[m_rl, m_acc[(m, j)]], mark=True)
                    prev_rl = Mark("m_vector", max(mk.value for mk in m_f))
                    ost.dma(lambda e, j=j: e.dma_start(
                        out=dout[:, :].rearrange("(c p) s -> p c s", p=128)
                            [:, :, 512 * j:512 * j + 512],
                        in_=acc[:, :].rearrange("p (c s) -> p c s", c=8)
                            [:, :, 512 * j:512 * j + 512]),
                        waits=m_f)

                p.do("sync", lambda e: e.nop(),
                     waits=ost.final_marks() + spl.final_marks())

        p.finalize(block)
        p.close()
    st.close()
    return nc


def _host_prep(inputs):
    gam = _gammas()
    wq = np.asarray(inputs["L0_attn_wq"])
    wk = np.asarray(inputs["L0_attn_wk"])
    wv = np.asarray(inputs["L0_attn_wv"])
    def pack_qkv(w):
        out = np.zeros((128, 512), np.float32)
        for c in range(8):
            out[0:64, 64 * c:64 * c + 64] = w[2 * c]
            out[64:128, 64 * c:64 * c + 64] = w[2 * c + 1]
        return out
    bf = lambda x: np.ascontiguousarray(x).astype(ml_dtypes.bfloat16)
    gns = np.asarray(inputs["L0_attn_gn_scale"]).reshape(8, 128).T.copy()
    gnb = np.asarray(inputs["L0_attn_gn_bias"]).reshape(8, 128).T.copy()
    lng = np.zeros((128, 8), np.float32)
    for c in range(8):
        lng[0:64, c] = np.log(gam[2 * c])
        lng[64:128, c] = np.log(gam[2 * c + 1])
    sel16 = np.zeros((128, 128), np.float32)
    for c in range(8):
        sel16[0:64, 16 * c + 2 * c] = 1.0
        sel16[64:128, 16 * c + 2 * c + 1] = 1.0
    selb = np.zeros((16, 1024), np.float32)
    for c in range(8):
        selb[2 * c, 128 * c:128 * c + 64] = 1.0
        selb[2 * c + 1, 128 * c + 64:128 * c + 128] = 1.0
    tri = (np.arange(128)[None, :] >= np.arange(128)[:, None]).astype(np.float32)
    masks = np.zeros((128, 1280), np.float32)
    moffs = (0, 512, 896, 1152)
    for kxx in range(4):
        w = 512 - 128 * kxx
        masks[:, moffs[kxx]:moffs[kxx] + 128] = tri
        masks[:, moffs[kxx] + 128:moffs[kxx] + w] = 1.0
    gmn4 = np.zeros((128, 64), np.float64)
    pp = np.arange(128)
    for n in range(16):
        for u in range(4):
            gmn4[:, 4 * n + u] = gam[n] ** (-(128.0 * u + pp))
    gmn4 = gmn4.astype(np.float32)
    rout = np.concatenate([np.asarray(inputs["L0_ffn_router_w"]),
                           np.asarray(inputs["L0_ffn_shared_gate_w"])], axis=1)
    return {
        "wq": pack_qkv(wq), "wk": pack_qkv(wk), "wv": pack_qkv(wv),
        "w1a": np.ascontiguousarray(inputs["L0_attn_w1"], dtype=np.float32),
        "w2a": np.ascontiguousarray(inputs["L0_attn_w2"], dtype=np.float32),
        "gns": np.ascontiguousarray(gns, dtype=np.float32),
        "gnb": np.ascontiguousarray(gnb, dtype=np.float32),
        "lng": lng,
        "rout": np.ascontiguousarray(rout, dtype=np.float32),
        "we1": np.ascontiguousarray(inputs["L0_ffn_w1"], dtype=np.float32),
        "we2": np.ascontiguousarray(inputs["L0_ffn_w2"], dtype=np.float32),
        "wgu": np.ascontiguousarray(inputs["L0_ffn_shared_gate_up"], dtype=np.float32),
        "wdn": np.ascontiguousarray(inputs["L0_ffn_shared_down"], dtype=np.float32),
        "sel16": bf(sel16), "selb": bf(selb),
        "ones1": bf(np.ones((1, 128), np.float32)),
        "onesc": bf(np.full((128, 1), 1.0 / 1024.0, np.float32)),
        "masks": bf(masks),
        "gmn4": np.ascontiguousarray(gmn4),
        "eye128": np.ascontiguousarray(np.eye(128, dtype=ml_dtypes.bfloat16)),
    }


def kernel(**inputs):
    common = _host_prep(inputs)
    x = np.asarray(inputs["L0_attn_x"])
    nc = build(nj=4)
    in_maps = []
    for i in range(8):
        m = dict(common)
        m["x"] = np.ascontiguousarray(x[i].T)
        in_maps.append(m)
    res = run_bass_kernel_spmd(nc, in_maps, core_ids=list(range(8)))
    out = np.stack([np.asarray(res.results[i]["out"], dtype=np.float32).T
                    for i in range(8)], axis=0)
    return out


# version 39 (final)
# speedup vs baseline: 1.0745x; 1.0084x over previous
import contextlib
import os
import numpy as np
import ml_dtypes
import concourse.bass as bass
import concourse.mybir as mybir
from concourse.bass_utils import run_bass_kernel_spmd

F32 = mybir.dt.float32
F16 = mybir.dt.float16
BF16 = mybir.dt.bfloat16
AF = mybir.ActivationFunctionType
ALU = mybir.AluOpType
AXX = mybir.AxisListType.X

D = 1024
NH = 16
NE = 8
EPS = 1e-5

ENGINES = ("tensor", "vector", "scalar", "gpsimd", "sync")


class Mark:
    __slots__ = ("sem_name", "value")
    def __init__(self, sem_name, value):
        self.sem_name = sem_name
        self.value = value


class Stream:
    def __init__(self, prog, name, queue, nsems):
        self.prog = prog
        self.name = name
        self.queue = queue
        self.nsems = nsems
        self.counts = [0] * nsems
        self.i = 0
        for k in range(nsems):
            prog._ensure_sem(f"{name}{k}")

    def dma(self, fn, waits=()):
        k = self.i % self.nsems
        self.counts[k] += 16
        sem_name = f"{self.name}{k}"
        self.i += 1
        sems = self.prog.sems
        def op(e, fn=fn, sem_name=sem_name):
            return fn(e).then_inc(sems[sem_name], 16)
        self.prog._emit(self.queue, op, waits=waits)
        return Mark(sem_name, self.counts[k])

    def final_marks(self):
        return [Mark(f"{self.name}{k}", self.counts[k])
                for k in range(self.nsems) if self.counts[k] > 0]


class Prog:
    def __init__(self, nc):
        self.nc = nc
        self.ops = {e: [] for e in ENGINES}
        self.sems = {}
        self._stack = contextlib.ExitStack()
        self.counts = {e: 0 for e in ENGINES}
        self.waited = {e: {} for e in ENGINES}
        for e in ENGINES:
            self._ensure_sem("m_" + e)

    def _ensure_sem(self, name):
        if name not in self.sems:
            self.sems[name] = self._stack.enter_context(self.nc.semaphore(name))
        return self.sems[name]

    def stream(self, name, queue, nsems):
        return Stream(self, name, queue, nsems)

    def _emit(self, eng, op, waits=()):
        needed = []
        own_dep = False
        w = self.waited[eng]
        for mk in waits:
            if mk is None:
                continue
            if mk.sem_name == "m_" + eng:
                own_dep = True
                continue
            if w.get(mk.sem_name, -1) >= mk.value:
                continue
            w[mk.sem_name] = mk.value
            needed.append((mk.sem_name, mk.value))
        if needed or own_dep:
            sems = self.sems
            def full(e, needed=needed, op=op, own_dep=own_dep):
                for sem_name, value in needed:
                    e.wait_ge(sems[sem_name], value)
                if own_dep:
                    e.drain()
                return op(e)
            self.ops[eng].append(full)
        else:
            self.ops[eng].append(op)

    def do(self, eng, fn, waits=(), mark=False):
        mk = None
        if mark:
            self.counts[eng] += 1
            if eng == "gpsimd":
                name = f"mg{self.counts[eng]}"
                sem = self._ensure_sem(name)
                mk = Mark(name, 1)
            else:
                name = "m_" + eng
                sem = self.sems[name]
                mk = Mark(name, self.counts[eng])
            def op(e, fn=fn, sem=sem):
                return fn(e).then_inc(sem, 1)
        else:
            op = fn
        self._emit(eng, op, waits=waits)
        return mk

    _bar_n = 0

    def barrier(self, engines=ENGINES, extra=()):
        Prog._bar_n += 1
        marks = []
        for e in engines:
            name = f"bar{Prog._bar_n}_{e}"
            sem = self._ensure_sem(name)
            self._emit(e, lambda en, sem=sem: en.nop().then_inc(sem, 1))
            marks.append(Mark(name, 1))
        marks = marks + list(extra)
        for e in engines:
            self.do(e, lambda en: en.nop(), waits=marks)

    def finalize(self, block):
        for eng in ENGINES:
            ops = self.ops[eng]
            if not ops:
                continue
            def make(ops):
                def body(e):
                    for op in ops:
                        op(e)
                return body
            getattr(block, eng)(make(ops))

    def close(self):
        self._stack.close()


def _gammas():
    lin = np.linspace(np.log(1.0 / 512.0), np.log(1.0 / 32.0), NH)
    return 1.0 - np.exp(lin)


def build(nj=4):
    STAGE = int(os.environ.get("KSTAGE", "99"))
    nc = bass.Bass("TRN2", target_bir_lowering=False)
    ST = nj * 512
    nt = nj * 4

    dx = nc.dram_tensor("x", [D, ST], F32, kind="ExternalInput").ap()
    dwq = nc.dram_tensor("wq", [128, 512], F32, kind="ExternalInput").ap()
    dwk = nc.dram_tensor("wk", [128, 512], F32, kind="ExternalInput").ap()
    dwv = nc.dram_tensor("wv", [128, 512], F32, kind="ExternalInput").ap()
    dw1a = nc.dram_tensor("w1a", [D, D], F32, kind="ExternalInput").ap()
    dw2a = nc.dram_tensor("w2a", [D, D], F32, kind="ExternalInput").ap()
    dgns = nc.dram_tensor("gns", [128, 8], F32, kind="ExternalInput").ap()
    dgnb = nc.dram_tensor("gnb", [128, 8], F32, kind="ExternalInput").ap()
    dlng = nc.dram_tensor("lng", [128, 8], F32, kind="ExternalInput").ap()
    drout = nc.dram_tensor("rout", [D, 9], F32, kind="ExternalInput").ap()
    dwe1 = nc.dram_tensor("we1", [NE, D, D], F32, kind="ExternalInput").ap()
    dwe2 = nc.dram_tensor("we2", [NE, D, D], F32, kind="ExternalInput").ap()
    dwgu = nc.dram_tensor("wgu", [D, 2 * D], F32, kind="ExternalInput").ap()
    dwdn = nc.dram_tensor("wdn", [D, D], F32, kind="ExternalInput").ap()
    dsel16 = nc.dram_tensor("sel16", [128, 128], BF16, kind="ExternalInput").ap()
    dselb = nc.dram_tensor("selb", [16, 1024], BF16, kind="ExternalInput").ap()
    dones1 = nc.dram_tensor("ones1", [1, 128], BF16, kind="ExternalInput").ap()
    donesc = nc.dram_tensor("onesc", [128, 1], BF16, kind="ExternalInput").ap()
    dmasks = nc.dram_tensor("masks", [128, 1280], BF16, kind="ExternalInput").ap()
    dgmn4 = nc.dram_tensor("gmn4", [128, 64], F32, kind="ExternalInput").ap()
    deye = nc.dram_tensor("eye128", [128, 128], BF16, kind="ExternalInput").ap()
    dout = nc.dram_tensor("out", [D, ST], F32, kind="ExternalOutput").ap()
    dx1s = nc.dram_tensor("x1s", [D, ST], BF16).ap()

    gam = _gammas()

    st = contextlib.ExitStack()
    sb = lambda name, shape, dt: st.enter_context(nc.sbuf_tensor(name, shape, dt))
    ps = lambda name, shape: st.enter_context(nc.psum_tensor(name, shape, F32))

    x2st = sb("x2st", [128, 8 * ST], BF16)
    wqkv = sb("wqkv", [128, 1536], BF16)
    gns_sb = sb("gns_sb", [128, 8], F32)
    gnb_sb = sb("gnb_sb", [128, 8], F32)
    lng_sb = sb("lng_sb", [128, 8], F32)
    sel16_sb = sb("sel16_sb", [128, 128], BF16)
    selb_sb = sb("selb_sb", [16, 1024], BF16)
    ones1_sb = sb("ones1_sb", [1, 128], BF16)
    onesc_sb = sb("onesc_sb", [128, 1], BF16)
    eps_sb = sb("eps_sb", [1, 1], F32)
    masks_sb = sb("masks_sb", [128, 1280], BF16)
    gmn4_sb = sb("gmn4_sb", [128, 64], F32)
    lgb = sb("lgb", [128, 320], F32)
    bcmu = sb("bcmu", [128, 512], BF16)
    bcrs = sb("bcrs", [128, 512], BF16)
    combT = sb("combT", [32, ST], BF16)
    eye_sb = sb("eye_sb", [128, 128], BF16)
    rout_sb = sb("rout_sb", [128, 72], BF16)
    iotaq = sb("iotaq", [128, 512], F16)
    iotak = sb("iotak", [128, 512], F16)
    stat_f = sb("stat_f", [16, 1024], F32)
    scrF = sb("scrF", [16, 1024], F32)
    stat_b = sb("stat_b", [16, 1024], BF16)
    scr16 = sb("scr16", [128, 4 * 512], BF16)
    bcomb = sb("bcomb", [128, 512], BF16)
    comb_scr = sb("comb_scr", [128, 96], F32)
    comb_b = sb("comb_b", [128, 32], BF16)
    combR = sb("combR", [1, 1024], BF16)

    P = [ps(f"P{k}", [128, 512]) for k in range(8)]

    p = Prog(nc)
    ldx = p.stream("ldx", "sync", 2)
    ldc = p.stream("ldc", "sync", 10)
    wst = p.stream("wst", "gpsimd", 8)
    spl = p.stream("spl", "sync", 2)
    ost = p.stream("ost", "sync", 2)
    cbr = p.stream("cbr", "sync", 2)
    ldg = p.stream("ldg", "gpsimd", 2)

    with nc.Block() as block:
        m_gns = ldc.dma(lambda e: e.dma_start(out=gns_sb[:], in_=dgns[:, :]))
        m_gnb = ldc.dma(lambda e: e.dma_start(out=gnb_sb[:], in_=dgnb[:, :]))
        m_lng = ldc.dma(lambda e: e.dma_start(out=lng_sb[:], in_=dlng[:, :]))
        m_sel16 = ldc.dma(lambda e: e.dma_start(out=sel16_sb[:], in_=dsel16[:, :]))
        m_selb = ldc.dma(lambda e: e.dma_start(out=selb_sb[:], in_=dselb[:, :]))
        m_ones1 = ldc.dma(lambda e: e.dma_start(out=ones1_sb[:], in_=dones1[:, :]))
        m_onesc = ldc.dma(lambda e: e.dma_start(out=onesc_sb[:], in_=donesc[:, :]))
        m_eps = p.do("vector", lambda e: e.memset(eps_sb[0:1, :], EPS), mark=True)
        m_masks = ldc.dma(lambda e: e.dma_start(out=masks_sb[:], in_=dmasks[:, :]))
        m_gmn4 = ldc.dma(lambda e: e.dma_start(out=gmn4_sb[:], in_=dgmn4[:, :]))
        m_eye = ldc.dma(lambda e: e.dma_start(out=eye_sb[:], in_=deye[:, :]))
        m_wq = wst.dma(lambda e: e.dma_start(out=wqkv[:, 0:512], in_=dwq[:, :]))
        m_wk = wst.dma(lambda e: e.dma_start(out=wqkv[:, 512:1024], in_=dwk[:, :]))
        m_wv = wst.dma(lambda e: e.dma_start(out=wqkv[:, 1024:1536], in_=dwv[:, :]))
        m_rout = wst.dma(lambda e: e.dma_start(
            out=rout_sb[:, 0:72].rearrange("p (k f) -> p k f", k=8),
            in_=drout[:, :].rearrange("(k p) f -> p k f", p=128)))
        m_ioq = p.do("gpsimd", lambda e: e.iota(
            iotaq[:, :], pattern=[[1, 512]], base=0, channel_multiplier=0,
            allow_small_or_imprecise_dtypes=True), mark=True)
        m_iok = p.do("gpsimd", lambda e: e.iota(
            iotak[:, :], pattern=[[0, 4], [-1, 128]], base=0, channel_multiplier=0,
            allow_small_or_imprecise_dtypes=True), mark=True)
        with contextlib.ExitStack() as ascope:
            asb = lambda name, shape, dt: ascope.enter_context(
                nc.sbuf_tensor(name, shape, dt))
            xj = asb("xj", [128, 8192], BF16)
            bgall = asb("bgall", [128, 16 * 512], BF16)
            ln1_f = asb("ln1_f", [1, 1024], F32)
            ln1scr = asb("ln1scr", [1, 1024], F32)
            ln1_b = asb("ln1_b", [1, 1024], BF16)
            xlnj = asb("xlnj", [128, 8192], BF16)
            qt = asb("qt", [128, 1024], BF16)
            kst = asb("kst", [128, 4096], BF16)
            vst = asb("vst", [128, 4096], BF16)
            ktok = asb("ktok", [128, 1024], BF16)
            Ms_f = asb("Ms_f", [128, 512], F32)
            Ms_b = asb("Ms_b", [128, 512], BF16)
            gatej = asb("gatej", [128, 4096], BF16)
            retj = asb("retj", [128, 4096], BF16)
            s_sb = asb("s_sb", [128, 4096], BF16)
            wscope1 = contextlib.ExitStack()
            w1a_sb = wscope1.enter_context(
                nc.sbuf_tensor("w1a_sb", [128, 8192], BF16, side="right"))
            w2a_sb = wscope1.enter_context(
                nc.sbuf_tensor("w2a_sb", [128, 8192], BF16, side="right"))

            def gen_bg(e):
                last = None
                for c in range(8):
                    e.activation(bgall[:, 1024 * c:1024 * c + 512], iotaq[:, :],
                                 AF.Exp, scale=lng_sb[:, c:c + 1])
                    last = e.activation(bgall[:, 1024 * c + 512:1024 * c + 1024],
                                        iotak[:, :], AF.Exp, scale=lng_sb[:, c:c + 1])
                return last
            m_bgall = p.do("scalar", gen_bg, waits=[m_ioq, m_iok, m_lng], mark=True)

            m_w1a = wst.dma(lambda e: e.dma_start(
                out=w1a_sb[:, :].rearrange("p (k f) -> p k f", k=8),
                in_=dw1a[:, :].rearrange("(k p) f -> p k f", p=128)))
            m_w2a = wst.dma(lambda e: e.dma_start(
                out=w2a_sb[:, :].rearrange("p (k f) -> p k f", k=8),
                in_=dw2a[:, :].rearrange("(k p) f -> p k f", p=128)))

            def load_x(j, waits):
                xo = 4096 * (j % 2)
                return ldg.dma(lambda e, j=j, xo=xo: e.dma_start(
                    out=xj[:, xo:xo + 4096].rearrange("p (c s) -> p c s", c=8),
                    in_=dx[:, :].rearrange("(c p) s -> p c s", p=128)
                        [:, :, 512 * j:512 * j + 512]), waits=waits)

            m_x = load_x(0, [])
            m_x_next = load_x(1, []) if nj > 1 else None

            def ln_partA(m_src, stA, stB, xoff=0, bufs=None, first_waits=(),
                         stB_row=0):
                st_f, sc_F, st_b = bufs if bufs is not None else (stat_f, scrF, stat_b)
                prev_mm = [None, None]
                mm_stat = None
                for c in range(8):
                    sl = (c % 2) * 512
                    srcw = m_src[c] if isinstance(m_src, list) else m_src
                    msq = p.do("scalar", lambda e, c=c, sl=sl: e.activation(
                        scr16[:, sl:sl + 512],
                        xj[:, xoff + 512 * c:xoff + 512 * c + 512],
                        AF.Square), waits=[srcw, prev_mm[c % 2]], mark=True)
                    def statmm(e, c=c, sl=sl):
                        e.matmul(stA[0:1, :], onesc_sb[:, 0:1],
                                 xj[:, xoff + 512 * c:xoff + 512 * c + 512],
                                 start=(c == 0), stop=(c == 7))
                        if stB_row:
                            return e.matmul(stB[stB_row:stB_row + 1, :],
                                            onesc_sb[:, 0:1],
                                            scr16[:, sl:sl + 512],
                                            start=(c == 0), stop=(c == 7),
                                            tile_position=(0, stB_row),
                                            skip_group_check=True)
                        return e.matmul(stB[0:1, :], onesc_sb[:, 0:1],
                                        scr16[:, sl:sl + 512],
                                        start=(c == 0), stop=(c == 7))
                    mm_stat = p.do("tensor", statmm,
                                   waits=[srcw, msq, m_onesc] +
                                         (list(first_waits) if c == 0 else []),
                                   mark=True)
                    prev_mm[c % 2] = mm_stat
                return ln_tail(stA, stB, xoff, (st_f, sc_F, st_b), mm_stat,
                               stB_row=stB_row)

            def ln_tail(stA, stB, xoff, bufs, mm_stat, stB_row=0):
                st_f, sc_F, st_b = bufs
                m_m2 = p.do("scalar", lambda e: e.activation(
                    sc_F[0:1, 0:512], stA[0:1, :], AF.Square),
                    waits=[mm_stat], mark=True)
                m_var = p.do("vector", lambda e: e.scalar_tensor_tensor(
                    st_f[0:1, 512:1024], sc_F[0:1, 0:512], -1.0,
                    stB[stB_row:stB_row + 1, :],
                    ALU.mult, ALU.add), waits=[m_m2], mark=True)
                def fin(e):
                    e.activation(sc_F[0:1, 512:1024], st_f[0:1, 512:1024],
                                 AF.Sqrt, bias=eps_sb[0:1, 0:1])
                    e.drain()
                    e.copy(st_b[0:1, 0:512], stA[0:1, :])
                    return e.copy(st_b[0:1, 512:1024], sc_F[0:1, 512:1024])
                m_fin = p.do("scalar", fin, waits=[m_var, m_eps], mark=True)
                return (st_b, m_fin, xoff, Mark("m_vector", m_var.value))

            def ln_partB(ctx, bcA, bcB, dst, bc_waits=()):
                nonlocal prev_lnapply
                st_b, m_fin, xoff = ctx[:3]
                def bcast(e):
                    e.matmul(bcA[:, :], ones1_sb[0:1, :], st_b[0:1, 0:512],
                             start=True, stop=True)
                    return e.matmul(bcB[:, :], ones1_sb[0:1, :], st_b[0:1, 512:1024],
                                    start=True, stop=True)
                m_bc = p.do("tensor", bcast,
                            waits=[m_fin, m_ones1] + list(bc_waits), mark=True)
                m_evmu = p.do("scalar", lambda e: e.copy(bcmu[:, :], bcA[:, :]),
                              waits=[m_bc, prev_lnapply], mark=True)
                def rec(e):
                    with nc.allow_low_precision(reason="bf16 rstd tile"):
                        return e.reciprocal(bcrs[:, :], bcB[:, :])
                m_rec = p.do("vector", rec, waits=[m_bc], mark=True)
                mu_b = bcmu[:, :].unsqueeze(1).broadcast_to([128, 4, 512])
                rs_b = bcrs[:, :].unsqueeze(1).broadcast_to([128, 4, 512])
                if dst[0] == "ln1":
                    xlo = dst[1]
                    mid = xlnj[:, xlo:xlo + 4096].rearrange(
                        "p (c t) -> p c t", t=512)
                    out3 = mid
                    guard = None
                else:
                    pj, pxo, guard = dst[1], dst[2], dst[3]
                    xoff = pxo
                    mid = xlnj[:, pxo:pxo + 4096].rearrange(
                        "p (c t) -> p c t", t=512)
                    out3 = x2st[:, :].rearrange("p (c t) -> p c t", c=8)[
                        :, :, 512 * pj:512 * pj + 512]
                in3 = xj[:, xoff:xoff + 4096].rearrange("p (c t) -> p c t", t=512)
                marks = []
                for h in (0, 1):
                    def a1(e, h=h):
                        with nc.allow_low_precision(reason="bf16 LN apply"):
                            return e.tensor_tensor(mid[:, 4 * h:4 * h + 4, :],
                                                   in3[:, 4 * h:4 * h + 4, :],
                                                   mu_b, ALU.subtract)
                    def a2(e, h=h):
                        with nc.allow_low_precision(reason="bf16 LN apply"):
                            return e.tensor_tensor(out3[:, 4 * h:4 * h + 4, :],
                                                   mid[:, 4 * h:4 * h + 4, :],
                                                   rs_b, ALU.mult)
                    m_a1 = p.do("vector", a1,
                                waits=[m_evmu, guard] if h == 0 else [],
                                mark=True)
                    m_a2 = p.do("vector", a2, waits=[m_a1, m_rec], mark=True)
                    marks.append(m_a2)
                prev_lnapply = marks[1]
                return [marks[0]] * 4 + [marks[1]] * 4

            def layernorm(m_src, stA, stB, bcA, bcB, dst, xoff=0,
                          bufs=None, first_waits=()):
                ctx = ln_partA(m_src, stA, stB, xoff=xoff, bufs=bufs,
                               first_waits=first_waits)
                return ln_partB(ctx, bcA, bcB, dst)

            m_spill = None
            prev_lnapply = None
            m_Minit = p.do("vector", lambda e: e.memset(Ms_f[:, :], 0.0), mark=True)
            m_Mcast = [m_Minit] * 8
            m_Mmm = [None] * 8
            m_vst_guard = [None] * 8
            m_kst_guard = [None] * 8
            ln1 = layernorm(m_x, P[6], P[7], P[6], P[7], ("ln1", 0), xoff=0,
                            bufs=(ln1_f, ln1scr, ln1_b))
            m_x_cur = m_x
            m_ln2_prev = []
            m_ln2_fin = None
            pending_ln2B = None
            def emit_ln2B():
                nonlocal pending_ln2B, m_ln2_prev, m_spill, m_x, m_x_next, \
                    m_ln2_fin
                ctx2, pj, pxo, pm_x1 = pending_ln2B
                pending_ln2B = None
                ln2 = ln_partB(ctx2, P[2], P[3],
                               ("ln2", pj, pxo, pm_x1[7]))
                m_ln2_fin = ctx2[1]
                m_ln2_prev = ln2
                m_spill = spl.dma(lambda e, pj=pj, pxo=pxo: e.dma_start(
                    out=dx1s[:, :].rearrange("(c p) s -> p c s", p=128)
                        [:, :, 512 * pj:512 * pj + 512],
                    in_=xj[:, pxo:pxo + 4096].rearrange("p (c s) -> p c s", c=8)),
                    waits=ln2 + pm_x1)
                m_x = m_x_next
                if pj + 2 < nj:
                    m_x_next = load_x(pj + 2, [m_spill] + ln2)
            for j in range(nj):
                xo = 4096 * (j % 2)
                xlo = 4096 * (j % 2)

                if STAGE < 2:
                    break
                m_gate = [None] * 8
                prev_gate_ev = [None, None]
                for m in range(8):
                    bank = P[4] if m % 2 == 0 else P[5]
                    m_g = None
                    for k in range(8):
                        def gmk(e, m=m, k=k, bank=bank, xlo=xlo):
                            return e.matmul(
                                bank[:, :],
                                w1a_sb[:, 1024 * k + 128 * m:1024 * k + 128 * m + 128],
                                xlnj[:, xlo + 512 * k:xlo + 512 * k + 512],
                                start=(k == 0), stop=(k == 7))
                        m_g = p.do("tensor", gmk,
                                   waits=[m_w1a, ln1[k]] +
                                         ([prev_gate_ev[m % 2]] if k == 0 else []),
                                   mark=(k == 7))
                    def gsil(e, m=m, bank=bank):
                        e.activation(gatej[:, 512 * m:512 * m + 512], bank[:, :],
                                     AF.Silu)
                        e.drain()
                        return e.add(gatej[:, 512 * m:512 * m + 512],
                                     gatej[:, 512 * m:512 * m + 512],
                                     gnb_sb[:, m:m + 1])
                    m_gate[m] = p.do("scalar", gsil, waits=[m_g, m_gnb], mark=True)
                    prev_gate_ev[m % 2] = m_gate[m]
                    if m == 1 and pending_ln2B is not None:
                        emit_ln2B()

                if STAGE < 3:
                    break

                m_qkev = [None] * 8
                m_scores_last = [None] * 8
                m_retj_ev = [None] * 8
                m_vev_prev = []
                pending_gnstat = None
                m_ktok_prev = None
                m_ktevA_last = None
                m_slot_prev = [None] * 4
                m_ret_bank_prev = [None, None]
                prev_sev = [(), ()]
                m_sq_prev = [None, None]
                m_gn_stat = None
                for c in range(8):
                    qsl = (c % 2) * 512
                    nA, nB = 2 * c, 2 * c + 1
                    gA, gB = gam[nA], gam[nB]
                    def qkmm(e, c=c, xlo=xlo):
                        x_ap = xlnj[:, xlo + 512 * c:xlo + 512 * c + 512]
                        e.matmul(P[0][0:64, :], wqkv[0:64, 64 * c:64 * c + 64],
                                 x_ap[0:64, :], start=True, stop=True,
                                 tile_position=(0, 0))
                        e.matmul(P[0][64:128, :], wqkv[64:128, 64 * c:64 * c + 64],
                                 x_ap[64:128, :], start=True, stop=True,
                                 tile_position=(64, 64))
                        e.matmul(P[1][0:64, :], wqkv[0:64, 512 + 64 * c:576 + 64 * c],
                                 x_ap[0:64, :], start=True, stop=True,
                                 tile_position=(0, 0))
                        return e.matmul(P[1][64:128, :], wqkv[64:128, 512 + 64 * c:576 + 64 * c],
                                        x_ap[64:128, :], start=True, stop=True,
                                        tile_position=(64, 64))
                    m_qk = p.do("tensor", qkmm,
                                waits=[ln1[c], m_wq, m_wk] +
                                      list(prev_sev[0] or ()) +
                                      list(prev_sev[1] or ()), mark=True)
                    if pending_gnstat is not None:
                        pgf, pgw = pending_gnstat
                        m_gn_stat = p.do("tensor", pgf, waits=pgw, mark=True)
                        m_sq_prev[(c - 1) % 2] = m_gn_stat
                        pending_gnstat = None
                    slot_guard = m_scores_last[c - 2] if c >= 2 else None
                    m_qe = p.do("vector", lambda e, qsl=qsl, c=c: e.tensor_tensor(
                        qt[:, qsl:qsl + 512], P[0][:, :],
                        bgall[:, 1024 * c:1024 * c + 512], ALU.mult),
                        waits=[m_qk, m_bgall, slot_guard], mark=True)
                    m_ke = p.do("vector", lambda e, c=c: e.tensor_copy(
                        kst[:, 512 * c:512 * c + 512], P[1][:, :]),
                        waits=[m_qk, m_kst_guard[c]], mark=True)
                    m_qkev[c] = [m_qe, m_ke]

                    def vmm(e, c=c, xlo=xlo):
                        last = None
                        for u in range(4):
                            x_ap = xlnj[:, xlo + 512 * c + 128 * u:xlo + 512 * c + 128 * u + 128]
                            e.matmul(P[6][:, 128 * u:128 * u + 64], x_ap[0:64, :],
                                     wqkv[0:64, 1024 + 64 * c:1088 + 64 * c],
                                     start=True, stop=True, tile_position=(0, 0))
                            last = e.matmul(P[3][:, 128 * u:128 * u + 64],
                                            x_ap[64:128, :],
                                            wqkv[64:128, 1024 + 64 * c:1088 + 64 * c],
                                            start=True, stop=True,
                                            tile_position=(64, 0))
                        return last
                    m_v = p.do("tensor", vmm,
                               waits=[ln1[c], m_wv] + (m_vev_prev or []) +
                                     list(prev_sev[1] or ()) + m_ln2_prev +
                                     ([m_ln2_fin] if c == 0 else []),
                               mark=True)
                    def vevA(e, c=c):
                        dst = vst[:, :].rearrange("p (u f) -> p u f", f=1024)
                        return e.copy(
                            dst[:, :, 128 * c:128 * c + 64],
                            P[6][:, 0:512].rearrange("p (u f) -> p u f", f=128)[:, :, 0:64])
                    def vevB(e, c=c):
                        dst = vst[:, :].rearrange("p (u f) -> p u f", f=1024)
                        return e.tensor_copy(
                            dst[:, :, 128 * c + 64:128 * c + 128],
                            P[3][:, 0:512].rearrange("p (u f) -> p u f", f=128)[:, :, 0:64])
                    m_vevA = p.do("scalar", vevA,
                                  waits=[m_v, m_vst_guard[c]], mark=True)
                    m_vevB = p.do("vector", vevB,
                                  waits=[m_v, m_vst_guard[c]], mark=True)
                    m_vevc = [m_vevA, m_vevB]
                    m_vev_prev = m_vevc

                    last_j = (j == nj - 1)
                    ksl = (c % 2) * 512
                    if not last_j:
                        def ktmmA(e, c=c, xlo=xlo):
                            last = None
                            for u in range(4):
                                x_ap = xlnj[:, xlo + 512 * c + 128 * u:xlo + 512 * c + 128 * u + 128]
                                last = e.matmul(P[6][:, 128 * u:128 * u + 64],
                                                x_ap[0:64, :],
                                                wqkv[0:64, 512 + 64 * c:576 + 64 * c],
                                                start=True, stop=True,
                                                tile_position=(0, 0))
                            return last
                        def ktmmB(e, c=c, xlo=xlo):
                            last = None
                            for u in range(4):
                                x_ap = xlnj[:, xlo + 512 * c + 128 * u:xlo + 512 * c + 128 * u + 128]
                                last = e.matmul(P[3][:, 128 * u:128 * u + 64],
                                                x_ap[64:128, :],
                                                wqkv[64:128, 512 + 64 * c:576 + 64 * c],
                                                start=True, stop=True,
                                                tile_position=(64, 0))
                            return last
                        m_ktA = p.do("tensor", ktmmA,
                                     waits=[m_vevA, m_ktok_prev], mark=True)
                        m_kt = p.do("tensor", ktmmB, waits=[m_vevB], mark=True)
                        def ktevA(e, c=c, ksl=ksl):
                            last = None
                            for u in range(4):
                                n = 2 * c
                                last = e.mul(
                                    ktok[:, ksl + 128 * u:ksl + 128 * u + 64],
                                    P[6][:, 128 * u:128 * u + 64],
                                    gmn4_sb[:, 4 * n + u:4 * n + u + 1])
                            return last
                        def ktevB(e, c=c, ksl=ksl):
                            last = None
                            for u in range(4):
                                n = 2 * c + 1
                                last = e.tensor_scalar(
                                    ktok[:, ksl + 128 * u + 64:ksl + 128 * u + 128],
                                    P[3][:, 128 * u:128 * u + 64],
                                    gmn4_sb[:, 4 * n + u:4 * n + u + 1],
                                    None, ALU.mult)
                            return last
                        m_ktevA = p.do("scalar", ktevA,
                                       waits=[m_ktA, m_gmn4, m_Mmm[c]], mark=True)
                        m_ktevB = p.do("vector", ktevB,
                                       waits=[m_kt, m_gmn4, m_Mmm[c]], mark=True)
                        m_ktev = m_ktevB
                        m_ktevA_last = m_ktevA
                        m_ktok_prev = m_ktevA

                    if STAGE < 4:
                        continue
                    ret_free = [m_gate[6], m_gate[7], m_ret_bank_prev[0],
                                m_ret_bank_prev[1]]
                    m_sev = [None] * 4
                    m_ret_i = [None] * 4
                    m_s = None
                    pending_rmm = None
                    if j > 0:
                        def crossmm(e, c=c, qsl=qsl):
                            e.matmul(P[4][0:64, :], Ms_b[0:64, 64 * c:64 * c + 64],
                                     qt[0:64, qsl:qsl + 512], start=True, stop=False,
                                     tile_position=(0, 0), skip_group_check=True)
                            return e.matmul(P[5][64:128, :],
                                            Ms_b[64:128, 64 * c:64 * c + 64],
                                            qt[64:128, qsl:qsl + 512], start=True,
                                            stop=False, tile_position=(64, 64),
                                            skip_group_check=True)
                        m_cross = p.do("tensor", crossmm,
                                       waits=m_qkev[c] + [m_Mcast[c]] + ret_free,
                                       mark=True)
                    if False:
                        def mupd(e, c=c, gA=gA, gB=gB):
                            e.tensor_add(Ms_f[0:64, 64 * c:64 * c + 64],
                                         Ms_f[0:64, 64 * c:64 * c + 64],
                                         P[5][0:64, 0:64])
                            e.tensor_add(Ms_f[64:128, 64 * c:64 * c + 64],
                                         Ms_f[64:128, 64 * c:64 * c + 64],
                                         P[4][64:128, 0:64])
                            e.drain()
                            e.tensor_scalar(Ms_f[0:64, 64 * c:64 * c + 64],
                                            Ms_f[0:64, 64 * c:64 * c + 64],
                                            float(gA ** 512), None, ALU.mult)
                            e.tensor_scalar(Ms_f[64:128, 64 * c:64 * c + 64],
                                            Ms_f[64:128, 64 * c:64 * c + 64],
                                            float(gB ** 512), None, ALU.mult)
                            e.drain()
                            return e.tensor_copy(Ms_b[:, 64 * c:64 * c + 64],
                                                 Ms_f[:, 64 * c:64 * c + 64])
                        m_Mcast[c] = p.do("vector", mupd,
                                          waits=[m_kv, m_Minit,
                                                 m_cross if j > 0 else None],
                                          mark=True)
                    for kx in range(4):
                        pa = P[0] if kx % 2 == 0 else P[2]
                        pb = P[1] if kx % 2 == 0 else P[3]
                        if kx >= 2:
                            bank_free = list(m_sev[kx - 2])
                        elif kx == 0:
                            bank_free = m_qkev[c] + list(prev_sev[0] or ())
                        else:
                            bank_free = ([m_ktevB] if not last_j else [m_vevB]) + \
                                        list(prev_sev[1] or ())
                        def smmA(e, c=c, kx=kx, pa=pa, qsl=qsl):
                            return e.matmul(
                                pa[:, :],
                                kst[0:64, 512 * c + 128 * kx:512 * c + 128 * kx + 128],
                                qt[0:64, qsl:qsl + 512], start=True, stop=True,
                                tile_position=(0, 0))
                        def smmB(e, c=c, kx=kx, pb=pb, qsl=qsl):
                            return e.matmul(
                                pb[:, :],
                                kst[64:128, 512 * c + 128 * kx:512 * c + 128 * kx + 128],
                                qt[64:128, qsl:qsl + 512], start=True,
                                stop=True, tile_position=(64, 0))
                        m_sA = p.do("tensor", smmA,
                                    waits=m_qkev[c] +
                                          [m_ret_i[kx - 2] if kx >= 2 else None] +
                                          bank_free,
                                    mark=True)
                        m_s = p.do("tensor", smmB, mark=True)
                        ssl = kx * 1024
                        slot_free = m_slot_prev[kx]
                        col0 = 128 * kx
                        moff = (0, 512, 896, 1152)[kx]
                        w = 512 - col0
                        def sevA(e, pa=pa, ssl=ssl, kx=kx, nA=nA, col0=col0,
                                 moff=moff, w=w):
                            return e.scalar_tensor_tensor(
                                s_sb[:, ssl + col0:ssl + 512],
                                pa[:, col0:512],
                                gmn4_sb[:, 4 * nA + kx:4 * nA + kx + 1],
                                masks_sb[:, moff:moff + w], ALU.mult, ALU.mult)
                        def sevB(e, pb=pb, ssl=ssl, kx=kx, nB=nB, col0=col0,
                                 moff=moff, w=w):
                            return e.scalar_tensor_tensor(
                                s_sb[:, ssl + 512 + col0:ssl + 1024],
                                pb[:, col0:512],
                                gmn4_sb[:, 4 * nB + kx:4 * nB + kx + 1],
                                masks_sb[:, moff:moff + w], ALU.mult, ALU.mult)
                        marks = [p.do("vector", sevA,
                                      waits=[m_sA, slot_free, m_masks, m_gmn4],
                                      mark=True),
                                 p.do("vector", sevB,
                                      waits=[m_s, slot_free, m_masks, m_gmn4],
                                      mark=True)]
                        m_sev[kx] = tuple(marks)
                        is_start = (j == 0 and kx == 0)
                        is_stop = (kx == 3)
                        def rmm(e, c=c, kx=kx, ssl=ssl, nA=nA, nB=nB,
                                is_start=is_start, is_stop=is_stop, col0=col0):
                            e.matmul(P[4][0:64, col0:512],
                                     vst[:, 1024 * kx + 64 * nA:1024 * kx + 64 * nA + 64],
                                     s_sb[:, ssl + col0:ssl + 512],
                                     start=is_start, stop=is_stop,
                                     tile_position=(0, 0), skip_group_check=True)
                            return e.matmul(
                                P[5][64:128, col0:512],
                                vst[:, 1024 * kx + 64 * nB:1024 * kx + 64 * nB + 64],
                                s_sb[:, ssl + 512 + col0:ssl + 1024],
                                start=is_start, stop=is_stop,
                                tile_position=(0, 64), skip_group_check=True)
                        if pending_rmm is not None:
                            pi, pf = pending_rmm
                            m_ret_i[pi] = p.do("tensor", pf,
                                               waits=list(m_sev[pi]) + m_vevc +
                                                     (ret_free if (j == 0 and pi == 0)
                                                      else []),
                                               mark=True)
                        pending_rmm = (kx, rmm)
                    pi, pf = pending_rmm
                    m_ret_i[pi] = p.do("tensor", pf,
                                       waits=list(m_sev[pi]) + m_vevc,
                                       mark=True)
                    prev_sev = [m_sev[2], m_sev[3]]
                    m_slot_prev = list(m_ret_i)
                    m_scores_last[c] = m_s
                    m_vst_guard[c] = m_ret_i[3]
                    m_kst_guard[c] = m_s

                    if j > 0:
                        m_Mmm[c] = m_cross
                    m_revA = p.do("scalar", lambda e, c=c: e.copy(
                        retj[0:64, 512 * c:512 * c + 512], P[4][0:64, :]),
                        waits=[m_ret_i[3], m_sq_prev[c % 2]], mark=True)
                    m_revB = p.do("vector", lambda e, c=c: e.tensor_copy(
                        retj[64:128, 512 * c:512 * c + 512], P[5][64:128, :]),
                        waits=[m_ret_i[3], m_sq_prev[c % 2]], mark=True)
                    m_rev = m_revB
                    m_retj_ev[c] = Mark("m_vector", m_revB.value)
                    m_ret_bank_prev[0] = m_revB
                    m_ret_bank_prev[1] = Mark("m_scalar", m_revA.value)
                    sqs = 1024 + (c % 2) * 512
                    m_sq = p.do("scalar", lambda e, c=c, sqs=sqs: e.activation(
                        scr16[:, sqs:sqs + 512], retj[:, 512 * c:512 * c + 512],
                        AF.Square), waits=[m_rev, m_revA], mark=True)
                    def gnstat(e, c=c, sqs=sqs):
                        e.matmul(P[7][0:16, :], sel16_sb[:, 16 * c:16 * c + 16],
                                 retj[:, 512 * c:512 * c + 512],
                                 start=(c == 0), stop=(c == 7))
                        return e.matmul(P[7][32:48, :],
                                        sel16_sb[:, 16 * c:16 * c + 16],
                                        scr16[:, sqs:sqs + 512],
                                        start=(c == 0), stop=(c == 7),
                                        tile_position=(0, 32),
                                        skip_group_check=True)
                    if c < 7:
                        pending_gnstat = (gnstat, [m_rev, m_revA, m_sq, m_sel16])
                        m_sq_prev[c % 2] = Mark("m_scalar", m_sq.value)
                    else:
                        m_gn_stat = p.do("tensor", gnstat,
                                         waits=[m_rev, m_revA, m_sq, m_sel16],
                                         mark=True)
                        m_sq_prev[c % 2] = m_gn_stat
                    if not last_j:
                        def kvmm(e, c=c, ksl=ksl, nA=nA, nB=nB):
                            last = None
                            for u in range(4):
                                e.matmul(P[5][0:64, 0:64],
                                         ktok[:, ksl + 128 * u:ksl + 128 * u + 64],
                                         vst[:, 1024 * u + 64 * nA:1024 * u + 64 * nA + 64],
                                         start=(u == 0), stop=(u == 3),
                                         tile_position=(0, 0), skip_group_check=True)
                                last = e.matmul(
                                    P[4][64:128, 0:64],
                                    ktok[:, ksl + 128 * u + 64:ksl + 128 * u + 128],
                                    vst[:, 1024 * u + 64 * nB:1024 * u + 64 * nB + 64],
                                    start=(u == 0), stop=(u == 3),
                                    tile_position=(0, 64), skip_group_check=True)
                            return last
                        m_kv = p.do("tensor", kvmm,
                                    waits=[m_ktevA, m_ktevB, m_Mcast[c]] + m_vevc,
                                    mark=True)
                        m_vst_guard[c] = m_kv
                        def mupd(e, c=c, gA=gA, gB=gB):
                            e.tensor_add(Ms_f[0:64, 64 * c:64 * c + 64],
                                         Ms_f[0:64, 64 * c:64 * c + 64],
                                         P[5][0:64, 0:64])
                            e.tensor_add(Ms_f[64:128, 64 * c:64 * c + 64],
                                         Ms_f[64:128, 64 * c:64 * c + 64],
                                         P[4][64:128, 0:64])
                            e.drain()
                            e.tensor_scalar(Ms_f[0:64, 64 * c:64 * c + 64],
                                            Ms_f[0:64, 64 * c:64 * c + 64],
                                            float(gA ** 512), None, ALU.mult)
                            e.tensor_scalar(Ms_f[64:128, 64 * c:64 * c + 64],
                                            Ms_f[64:128, 64 * c:64 * c + 64],
                                            float(gB ** 512), None, ALU.mult)
                            e.drain()
                            return e.tensor_copy(Ms_b[:, 64 * c:64 * c + 64],
                                                 Ms_f[:, 64 * c:64 * c + 64])
                        m_Mcast[c] = p.do("vector", mupd,
                                          waits=[m_kv, m_Minit,
                                                 m_cross if j > 0 else None],
                                          mark=True)
                        m_ret_bank_prev[0] = m_revB
                        m_ret_bank_prev[1] = Mark("m_scalar", m_revA.value)

                if STAGE < 5:
                    break
                def gnmath(e):
                    e.tensor_scalar(stat_f[0:16, 0:512], P[7][0:16, :], 1.0 / 64, None, ALU.mult)
                    e.tensor_scalar(stat_f[0:16, 512:1024], P[7][32:48, :], 1.0 / 64, None, ALU.mult)
                    e.drain()
                    e.tensor_mul(scrF[0:16, 0:512], stat_f[0:16, 0:512], stat_f[0:16, 0:512])
                    e.drain()
                    e.tensor_sub(stat_f[0:16, 512:1024], stat_f[0:16, 512:1024],
                                 scrF[0:16, 0:512])
                    e.drain()
                    return e.tensor_scalar(stat_f[0:16, 512:1024],
                                           stat_f[0:16, 512:1024], EPS, None, ALU.add)
                m_gmath = p.do("vector", gnmath, waits=[m_gn_stat], mark=True)
                m_gstd = p.do("scalar", lambda e: e.activation(
                    scrF[0:16, 512:1024], stat_f[0:16, 512:1024], AF.Sqrt),
                    waits=[m_gmath], mark=True)
                def gfin(e):
                    e.reciprocal(stat_f[0:16, 512:1024], scrF[0:16, 512:1024])
                    e.drain()
                    e.tensor_copy(stat_b[0:16, 0:512], stat_f[0:16, 0:512])
                    return e.tensor_copy(stat_b[0:16, 512:1024], stat_f[0:16, 512:1024])
                m_gfin = p.do("vector", gfin, waits=[m_gstd], mark=True)

                hoist_fin = None
                if j + 1 < nj:
                    ln1_ctx = ln_partA(m_x_next, P[5], P[4],
                                       xoff=4096 * ((j + 1) % 2),
                                       bufs=(ln1_f, ln1scr, ln1_b),
                                       first_waits=[m_Mcast[7]],
                                       stB_row=64)
                    hoist_fin = ln1_ctx[1]

                m_om = [None] * 8
                prev_gnb = [None, None, None]
                prev_op = [m_retj_ev[7], m_retj_ev[7]]
                def omk01(m, k, extra=()):
                    def omk(e, m=m, k=k, xlo=xlo):
                        return e.matmul(
                            (P[4] if m == 0 else P[5])[:, :],
                            w2a_sb[:, 1024 * k + 128 * m:1024 * k + 128 * m + 128],
                            xlnj[:, xlo + 512 * k:xlo + 512 * k + 512],
                            start=(k == 0), stop=(k == 7))
                    return p.do("tensor", omk,
                                waits=[m_w2a, m_om[k]] + list(extra),
                                mark=(k == 7))
                for c in range(8):
                    bA = (P[0], P[2], P[6])[c % 3]
                    bB = (P[1], P[3], P[7])[c % 3]
                    def gnb(e, c=c, bA=bA, bB=bB):
                        e.matmul(bA[:, :], selb_sb[0:16, 128 * c:128 * c + 128],
                                 stat_b[0:16, 0:512], start=True, stop=True)
                        return e.matmul(bB[:, :], selb_sb[0:16, 128 * c:128 * c + 128],
                                        stat_b[0:16, 512:1024], start=True, stop=True)
                    m_b = p.do("tensor", gnb,
                               waits=[m_gfin, m_selb, prev_gnb[c % 3]] +
                                     ([m_ktevA_last] if c == 2 else []) +
                                     list(prev_sev[0] or ()) +
                                     list(prev_sev[1] or ()), mark=True)
                    def gapply(e, c=c, bA=bA, bB=bB, xlo=xlo):
                        sl = 1024 + (c % 2) * 512
                        e.tensor_tensor(scr16[:, sl:sl + 512],
                                        retj[:, 512 * c:512 * c + 512], bA[:, :],
                                        ALU.subtract)
                        e.drain()
                        e.scalar_tensor_tensor(scr16[:, sl:sl + 512],
                                               scr16[:, sl:sl + 512],
                                               gns_sb[:, c:c + 1], bB[:, :],
                                               ALU.mult, ALU.mult)
                        e.drain()
                        return e.tensor_add(xlnj[:, xlo + 512 * c:xlo + 512 * c + 512],
                                            scr16[:, sl:sl + 512],
                                            gatej[:, 512 * c:512 * c + 512])
                    m_om[c] = p.do("vector", gapply,
                                   waits=[m_b, m_gate[c], m_gns, m_gnb], mark=True)
                    prev_gnb[c % 3] = m_om[c]
                    if c >= 2:
                        omk01(0, c - 2,
                              extra=([prev_op[0]] +
                                     ([ln1_ctx[3]] if j + 1 < nj else [])
                                     if c == 2 else []))
                        omk01(1, c - 2, extra=([prev_op[1], hoist_fin]
                                               if c == 2 else []))

                if STAGE < 6:
                    break
                m_x1 = [None] * 8
                prev_sq2 = [None, None]
                mm_stat2 = None
                for m in range(8):
                    bank = P[4] if m % 2 == 0 else P[5]
                    m_o = None
                    for k in (range(6, 8) if m < 2 else range(8)):
                        if m < 2:
                            m_o = omk01(m, k)
                            continue
                        def omk(e, m=m, k=k, bank=bank, xlo=xlo):
                            return e.matmul(
                                bank[:, :],
                                w2a_sb[:, 1024 * k + 128 * m:1024 * k + 128 * m + 128],
                                xlnj[:, xlo + 512 * k:xlo + 512 * k + 512],
                                start=(k == 0), stop=(k == 7))
                        m_o = p.do("tensor", omk,
                                   waits=[m_w2a, m_om[k]] +
                                         ([prev_op[m % 2]] if k == 0 else []),
                                   mark=(k == 7))
                    m_x1[m] = p.do("vector", lambda e, m=m, bank=bank, xo=xo:
                        e.tensor_add(
                        xj[:, xo + 512 * m:xo + 512 * m + 512],
                        xj[:, xo + 512 * m:xo + 512 * m + 512],
                        bank[:, :]), waits=[m_o], mark=True)
                    prev_op[m % 2] = m_x1[m]
                    if STAGE >= 7:
                        def emit_stat2(c, sq2s):
                            def statmm2(e, c=c, xo=xo):
                                e.matmul(P[6][0:1, :], onesc_sb[:, 0:1],
                                         xj[:, xo + 512 * c:xo + 512 * c + 512],
                                         start=(c == 0), stop=(c == 7))
                                return e.matmul(P[7][0:1, :], onesc_sb[:, 0:1],
                                                scr16[:, (c % 2) * 512:
                                                       (c % 2) * 512 + 512],
                                                start=(c == 0), stop=(c == 7))
                            return p.do("tensor", statmm2,
                                        waits=[m_x1[c], sq2s[c], m_onesc] +
                                              ([m_om[5]] if c == 0 else []),
                                        mark=True)
                        if m == 0:
                            msq2_l = [None] * 8
                        if m >= 2:
                            prev_sq2[m % 2] = emit_stat2(m - 2, msq2_l)
                        msq2_l[m] = p.do("scalar", lambda e, m=m, xo=xo:
                                    e.activation(
                                        scr16[:, (m % 2) * 512:(m % 2) * 512 + 512],
                                        xj[:, xo + 512 * m:xo + 512 * m + 512],
                                        AF.Square),
                                    waits=[m_x1[m], prev_sq2[m % 2]], mark=True)
                        if m == 7:
                            emit_stat2(6, msq2_l)
                            mm_stat2 = emit_stat2(7, msq2_l)

                if j + 1 < nj:
                    ln1_next = ln_partB(ln1_ctx, P[0], P[1],
                                        ("ln1", 4096 * ((j + 1) % 2)),
                                        bc_waits=[m_om[6]])

                if STAGE < 7:
                    break
                if j + 1 < nj:
                    ln1 = ln1_next
                ctx2 = ln_tail(P[6], P[7], xo, (stat_f, scrF, stat_b), mm_stat2)
                pending_ln2B = (ctx2, j, xo, m_x1)
                if j == nj - 1:
                    emit_ln2B()

        if STAGE >= 8:
            wscope1.close()
            wgu_scope = contextlib.ExitStack()
            wgu_sb = wgu_scope.enter_context(
                nc.sbuf_tensor("wgu_sb", [128, 16384], BF16, side="right"))
            m_wgu = wst.dma(lambda e: e.dma_start(
                out=wgu_sb[:, :].rearrange("p (k f) -> p k f", k=8),
                in_=dwgu[:, :].rearrange("(k p) f -> p k f", p=128)),
                waits=[m_o, m_gate[7]])

            m_zero = p.do("vector", lambda e: e.memset(comb_b[:, :], 0.0), mark=True)
            m_lg = [None] * nt
            prev_rps = [None, None]
            for u in range(nt):
                bank = P[0] if u % 2 == 0 else P[1]
                def rmm(e, u=u, bank=bank):
                    last = None
                    for k in range(8):
                        last = e.matmul(
                            bank[:, 0:9],
                            x2st[:, ST * k + 128 * u:ST * k + 128 * u + 128],
                            rout_sb[:, 9 * k:9 * k + 9],
                            start=(k == 0), stop=(k == 7))
                    return last
                m_r = p.do("tensor", rmm,
                           waits=[m_rout, prev_rps[u % 2]] +
                                 (([m_om[6]] + m_ln2_prev) if u < 2 else []),
                           mark=True)
                def lgev(e, u=u, bank=bank):
                    e.tensor_copy(lgb[:, 8 * u:8 * u + 8], bank[:, 0:8])
                    return e.tensor_copy(lgb[:, 128 + u:129 + u], bank[:, 8:9])
                m_lg[u] = p.do("vector", lgev, waits=[m_r], mark=True)
                prev_rps[u % 2] = m_lg[u]
            lg3 = lgb[:, 0:8 * nt].rearrange("p (u e) -> p u e", e=8)
            sc3 = lgb[:, 160:160 + 8 * nt].rearrange("p (u e) -> p u e", e=8)
            def bc(col):
                return comb_scr[:, col:col + nt].unsqueeze(2).broadcast_to(
                    [128, nt, 8])
            def smax1(e):
                e.reduce_max(comb_scr[:, 16:16 + nt], lg3, axis=AXX)
                e.drain()
                return e.tensor_tensor(lg3, lg3, bc(16), ALU.subtract)
            m_s1 = p.do("vector", smax1, waits=m_lg, mark=True)
            def sexp(e):
                e.activation(lgb[:, 0:8 * nt], lgb[:, 0:8 * nt], AF.Exp)
                return e.activation(comb_scr[:, 32:32 + nt],
                                    lgb[:, 128:128 + nt], AF.Sigmoid)
            m_e = p.do("scalar", sexp, waits=[m_s1], mark=True)
            def smax2(e):
                e.reduce_sum(comb_scr[:, 0:nt], lg3, axis=AXX)
                e.drain()
                e.reciprocal(comb_scr[:, 0:nt], comb_scr[:, 0:nt])
                e.drain()
                e.tensor_tensor(lg3, lg3, bc(0), ALU.mult)
                e.drain()
                e.reduce_max(comb_scr[:, 16:16 + nt], lg3, axis=AXX)
                e.drain()
                e.tensor_tensor(sc3, lg3, bc(16), ALU.is_equal)
                e.drain()
                e.scalar_tensor_tensor(sc3, sc3, -2.0, lg3, ALU.mult, ALU.add)
                e.drain()
                e.reduce_max(comb_scr[:, 48:48 + nt], sc3, axis=AXX)
                e.drain()
                e.tensor_tensor(sc3, lg3, bc(48), ALU.is_ge)
                e.drain()
                return e.tensor_tensor(lg3, lg3, sc3, ALU.mult)
            m_s2 = p.do("vector", smax2, waits=[m_e, m_zero], mark=True)
            m_trE = [None] * (nt // 4)
            m_tmm = [None] * nt
            for u in range(nt):
                slot = (u % 2) * 16
                def cb(e, u=u, slot=slot):
                    e.tensor_copy(comb_b[:, slot:slot + 8],
                                  lgb[:, 8 * u:8 * u + 8])
                    return e.tensor_copy(comb_b[:, slot + 8:slot + 9],
                                         comb_scr[:, 32 + u:33 + u])
                m_c = p.do("vector", cb,
                           waits=[m_s2, m_tmm[u - 2] if u > 1 else None],
                           mark=True)
                jg = u // 4
                def tmm(e, u=u, slot=slot, jg=jg):
                    return e.matmul(
                        P[jg % 2][0:9, 128 * (u % 4):128 * (u % 4) + 128],
                        comb_b[:, slot:slot + 9], eye_sb[:, :],
                        start=True, stop=True)
                m_tmm[u] = p.do("tensor", tmm, waits=[m_c, m_eye], mark=True)
                if u % 4 == 3:
                    m_trE[jg] = p.do(
                        "vector", lambda e, jg=jg: e.tensor_copy(
                            combT[0:9, 512 * jg:512 * jg + 512],
                            P[jg % 2][0:9, 0:512]),
                        waits=[m_tmm[u]], mark=True)
            m_tr_all = m_trE[nt // 4 - 1]

        if STAGE < 8:
            wst.dma(lambda e: e.dma_start(
                out=dout[:, :].rearrange("(c p) s -> p c s", p=128)[:, :, 0:512],
                in_=xj[:, 0:4096].rearrange("p (c s) -> p c s", c=8)),
                waits=[m_x, Mark("m_tensor", p.counts["tensor"]),
                       Mark("m_vector", p.counts["vector"]),
                       Mark("m_scalar", p.counts["scalar"])])
            p.do("sync", lambda e: e.nop(),
                 waits=wst.final_marks() + spl.final_marks() + ldx.final_marks()
                 + ldc.final_marks() + ldg.final_marks())
            p.finalize(block)
            p.close()
            st.close()
            return nc

        p.barrier(extra=spl.final_marks() + ldx.final_marks() + ldc.final_marks()
                  + wst.final_marks())
        with contextlib.ExitStack() as fscope:
            fsb = lambda name, shape, dt: fscope.enter_context(
                nc.sbuf_tensor(name, shape, dt))
            acc = fsb("acc", [128, 8 * ST], F32)
            sgu_j = fsb("sgu_j", [128, 4096], BF16)
            sg_scr = fsb("sg_scr", [128, 1536], BF16)
            hs_scr = fsb("hs_scr", [128, 1536], BF16)
            x1rl = fsb("x1rl", [128, 4096], BF16)


            m_acc = {}
            with contextlib.ExitStack() as wscope:
                wdn_sb = wscope.enter_context(
                    nc.sbuf_tensor("wdn_sb", [128, 8192], BF16))
                m_wdn = wst.dma(lambda e: e.dma_start(
                    out=wdn_sb[:, :].rearrange("p (k f) -> p k f", k=8),
                    in_=dwdn[:, :].rearrange("(k p) f -> p k f", p=128)))

                prev_silG = [None, None]
                prev_multU = [None, None]
                prev_dn = [None, None]
                prev_bc = None
                prev_bgm = [None, None]
                for j in range(nj):
                    csl = (j % 2) * 512
                    m_cb = cbr.dma(lambda e, j=j, csl=csl: e.dma_start(
                        out=combR[0:1, csl:csl + 512],
                        in_=combT[8:9, 512 * j:512 * j + 512]),
                        waits=[m_trE[j], prev_bgm[j % 2]])
                    m_sgu = [None] * 8
                    for m in range(8):
                        bG = P[3] if m % 2 == 0 else P[4]
                        bU = P[0] if m % 2 == 0 else P[1]
                        def gu1(e, m=m, j=j, bG=bG):
                            last = None
                            for k in range(8):
                                last = e.matmul(
                                    bG[:, :],
                                    wgu_sb[:, 2048 * k + 128 * m:2048 * k + 128 * m + 128],
                                    x2st[:, ST * k + 512 * j:ST * k + 512 * j + 512],
                                    start=(k == 0), stop=(k == 7))
                            return last
                        m_g1 = p.do("tensor", gu1,
                                    waits=[m_wgu, prev_silG[m % 2]], mark=True)
                        def gu2(e, m=m, j=j, bU=bU):
                            last = None
                            for k in range(8):
                                last = e.matmul(
                                    bU[:, :],
                                    wgu_sb[:, 2048 * k + 128 * (m + 8):2048 * k + 128 * (m + 8) + 128],
                                    x2st[:, ST * k + 512 * j:ST * k + 512 * j + 512],
                                    start=(k == 0), stop=(k == 7))
                            return last
                        m_g2 = p.do("tensor", gu2,
                                    waits=[m_wgu, prev_multU[m % 2]] +
                                          ([m_trE[max(g for g in
                                                      range(nt // 4)
                                                      if g % 2 == m)]]
                                           if j == 0 and m < 2 else []),
                                    mark=True)
                        m_gu_last = m_g2
                        m_sil = p.do("scalar", lambda e, m=m, bG=bG: e.activation(
                            sg_scr[:, 512 * (m % 2):512 * (m % 2) + 512],
                            bG[:, :], AF.Silu),
                            waits=[m_g1, m_sgu[m - 2] if m > 1 else None],
                            mark=True)
                        prev_silG[m % 2] = m_sil
                        m_sgu[m] = p.do("vector",
                            lambda e, m=m, bU=bU:
                            e.tensor_tensor(sgu_j[:, 512 * m:512 * m + 512],
                                            bU[:, :],
                                            sg_scr[:, 512 * (m % 2):512 * (m % 2) + 512],
                                            ALU.mult),
                            waits=[m_g2, m_sil], mark=True)
                        prev_multU[m % 2] = m_sgu[m]
                    def bgmm(e, csl=csl):
                        return e.matmul(P[2][:, :], ones1_sb[0:1, :],
                                        combR[0:1, csl:csl + 512],
                                        start=True, stop=True)
                    m_bgm = p.do("tensor", bgmm, waits=[m_cb, prev_bc],
                                 mark=True)
                    prev_bgm[j % 2] = m_bgm
                    m_bge = p.do("vector", lambda e: e.tensor_copy(
                        bcomb[:, :], P[2][:, :]), waits=[m_bgm], mark=True)
                    for m2 in range(8):
                        bank = P[5] if m2 % 2 == 0 else P[6]
                        def dmm(e, m2=m2, bank=bank):
                            last = None
                            for f in range(8):
                                last = e.matmul(
                                    bank[:, :],
                                    wdn_sb[:, 1024 * f + 128 * m2:1024 * f + 128 * m2 + 128],
                                    sgu_j[:, 512 * f:512 * f + 512],
                                    start=(f == 0), stop=(f == 7))
                            return last
                        m_d = p.do("tensor", dmm,
                                   waits=[m_wdn, prev_dn[m2 % 2]] + m_sgu, mark=True)
                        m_acc[(m2, j)] = p.do(
                            "vector", lambda e, m2=m2, j=j, bank=bank:
                            e.tensor_tensor(
                                acc[:, ST * m2 + 512 * j:ST * m2 + 512 * j + 512],
                                bank[:, :], bcomb[:, :], ALU.mult),
                            waits=[m_d, m_bge], mark=True)
                        prev_dn[m2 % 2] = m_acc[(m2, j)]
                    prev_bc = m_acc[(7, j)]
                m_shared_t = Mark("m_tensor", p.counts["tensor"])
                m_shared_v = m_acc[(7, nj - 1)]
                m_gu_done = m_gu_last
                m_sgu_last = Mark("m_vector", max(mk.value for mk in m_sgu))
                m_bge_last = m_bge
            wgu_scope.close()

            with contextlib.ExitStack() as escope:
                we1_sb = escope.enter_context(
                    nc.sbuf_tensor("we1_sb", [128, 8192], BF16, side="right"))
                we2_sb = escope.enter_context(
                    nc.sbuf_tensor("we2_sb", [128, 8192], BF16, side="right"))
                htil = escope.enter_context(
                    nc.sbuf_tensor("htil", [128, 8192], BF16))
                m_we1 = wst.dma(lambda e: e.dma_start(
                    out=we1_sb[:, :].rearrange("p (k f) -> p k f", k=8),
                    in_=dwe1[0, :, :].rearrange("(k p) f -> p k f", p=128)),
                    waits=[m_gu_done])
                m_we2 = wst.dma(lambda e: e.dma_start(
                    out=we2_sb[:, :].rearrange("p (k f) -> p k f", k=8),
                    in_=dwe2[0, :, :].rearrange("(k p) f -> p k f", p=128)),
                    waits=[m_gu_done])
                prev_h = [m_sgu_last, m_sgu_last, m_sgu_last]
                prev_y = [m_shared_v, m_shared_v, m_sgu_last, m_sgu_last]
                prev_bc2 = m_bge_last
                prev_bcomb_rd = m_shared_v
                prev_bcm = [m_shared_t, m_shared_t]
                pend_y = None
                m_htil_hist = {}
                m_we2_by_e = {0: m_we2}
                blocks = [(e_i, j) for e_i in range(NE) for j in range(nj)]
                def stage_cb(b):
                    e_i, j = blocks[b]
                    csl = (b % 2) * 512
                    return cbr.dma(lambda e, e_i=e_i, j=j, csl=csl: e.dma_start(
                        out=combR[0:1, csl:csl + 512],
                        in_=combT[e_i:e_i + 1, 512 * j:512 * j + 512]),
                        waits=[m_trE[j], prev_bcm[b % 2]])
                m_cb_next = stage_cb(0)
                for e_i in range(NE):
                    m_last_h = None
                    m_last_y = None
                    for j in range(nj):
                        b = e_i * nj + j
                        slot = (b % 2) * 4096
                        csl = (b % 2) * 512
                        m_cb = m_cb_next
                        def bcmm(e, csl=csl):
                            return e.matmul(P[2][:, :], ones1_sb[0:1, :],
                                            combR[0:1, csl:csl + 512],
                                            start=True, stop=True)
                        m_bcm = p.do("tensor", bcmm, waits=[m_cb, prev_bc2],
                                     mark=True)
                        prev_bcm[b % 2] = m_bcm
                        if b + 1 < len(blocks):
                            m_cb_next = stage_cb(b + 1)
                        m_bce = p.do("vector", lambda e: e.tensor_copy(
                            bcomb[:, :], P[2][:, :]),
                            waits=[m_bcm, prev_bcomb_rd], mark=True)
                        m_htil = [None] * 8
                        guard = m_htil_hist.get(slot)
                        for f in range(8):
                            bank = (P[3], P[4], P[7])[f % 3]
                            def hmm(e, f=f, j=j, bank=bank):
                                last = None
                                for k in range(8):
                                    last = e.matmul(
                                        bank[:, :],
                                        we1_sb[:, 1024 * k + 128 * f:1024 * k + 128 * f + 128],
                                        x2st[:, ST * k + 512 * j:ST * k + 512 * j + 512],
                                        start=(k == 0), stop=(k == 7))
                                return last
                            m_h = p.do("tensor", hmm,
                                       waits=[m_we1, prev_h[f % 3]], mark=True)
                            m_last_h = m_h
                            hsl = (f % 3) * 512
                            m_hs = p.do("scalar", lambda e, hsl=hsl, bank=bank:
                                        e.activation(hs_scr[:, hsl:hsl + 512],
                                                     bank[:, :], AF.Silu),
                                        waits=[m_h,
                                               m_htil[f - 3] if f >= 3 else None],
                                        mark=True)
                            m_htil[f] = p.do(
                                "vector", lambda e, f=f, hsl=hsl, slot=slot:
                                e.tensor_tensor(htil[:, slot + 512 * f:slot + 512 * f + 512],
                                                hs_scr[:, hsl:hsl + 512],
                                                bcomb[:, :], ALU.mult),
                                waits=[m_hs, m_bce,
                                       guard[f] if guard else None], mark=True)
                            prev_h[f % 3] = m_htil[f]
                        m_htil_hist[slot] = None
                        if pend_y is not None:
                            pe, pj, pslot, phtil = pend_y
                            for m in range(8):
                                bank = (P[5], P[6], P[0], P[1])[m % 4]
                                def ymm(e, m=m, pslot=pslot, bank=bank):
                                    last = None
                                    for f in range(8):
                                        last = e.matmul(
                                            bank[:, :],
                                            we2_sb[:, 1024 * f + 128 * m:1024 * f + 128 * m + 128],
                                            htil[:, pslot + 512 * f:pslot + 512 * f + 512],
                                            start=(f == 0), stop=(f == 7))
                                    return last
                                m_y = p.do("tensor", ymm,
                                           waits=[m_we2_by_e[pe], prev_y[m % 4]] + phtil,
                                           mark=True)
                                m_last_y = m_y
                                m_acc[(m, pj)] = p.do(
                                    "vector", lambda e, m=m, pj=pj, bank=bank:
                                    e.tensor_add(
                                        acc[:, ST * m + 512 * pj:ST * m + 512 * pj + 512],
                                        acc[:, ST * m + 512 * pj:ST * m + 512 * pj + 512],
                                        bank[:, :]),
                                    waits=[m_y, m_acc[(m, pj)]], mark=True)
                                prev_y[m % 4] = m_acc[(m, pj)]
                            m_htil_hist[pslot] = [Mark("m_tensor", m_last_y.value)] * 8
                            if pj == nj - 1 and pe + 1 < NE:
                                m_we2_by_e[pe + 1] = wst.dma(
                                    lambda e, pe=pe: e.dma_start(
                                        out=we2_sb[:, :].rearrange("p (k f) -> p k f", k=8),
                                        in_=dwe2[pe + 1, :, :].rearrange("(k p) f -> p k f", p=128)),
                                    waits=[m_last_y])
                        pend_y = (e_i, j, slot, m_htil)
                        prev_bc2 = Mark("m_vector", max(mk.value for mk in m_htil))
                        prev_bcomb_rd = prev_bc2
                    if e_i + 1 < NE:
                        m_we1 = wst.dma(lambda e, e_i=e_i: e.dma_start(
                            out=we1_sb[:, :].rearrange("p (k f) -> p k f", k=8),
                            in_=dwe1[e_i + 1, :, :].rearrange("(k p) f -> p k f", p=128)),
                            waits=[m_last_h])
                pe, pj, pslot, phtil = pend_y
                for m in range(8):
                    bank = (P[5], P[6], P[0], P[1])[m % 4]
                    def ymm(e, m=m, pslot=pslot, bank=bank):
                        last = None
                        for f in range(8):
                            last = e.matmul(
                                bank[:, :],
                                we2_sb[:, 1024 * f + 128 * m:1024 * f + 128 * m + 128],
                                htil[:, pslot + 512 * f:pslot + 512 * f + 512],
                                start=(f == 0), stop=(f == 7))
                        return last
                    m_y = p.do("tensor", ymm,
                               waits=[m_we2_by_e[pe], prev_y[m % 4]] + phtil, mark=True)
                    m_acc[(m, pj)] = p.do(
                        "vector", lambda e, m=m, pj=pj, bank=bank:
                        e.tensor_add(
                            acc[:, ST * m + 512 * pj:ST * m + 512 * pj + 512],
                            acc[:, ST * m + 512 * pj:ST * m + 512 * pj + 512],
                            bank[:, :]),
                        waits=[m_y, m_acc[(m, pj)]], mark=True)
                    prev_y[m % 4] = m_acc[(m, pj)]

                prev_rl = None
                for j in range(nj):
                    m_rl = ldx.dma(lambda e, j=j: e.dma_start(
                        out=x1rl[:, :].rearrange("p (c s) -> p c s", c=8),
                        in_=dx1s[:, :].rearrange("(c p) s -> p c s", p=128)
                            [:, :, 512 * j:512 * j + 512]),
                        waits=[m_spill, prev_rl])
                    m_f = [None] * 8
                    for m in range(8):
                        m_f[m] = p.do("vector", lambda e, m=m, j=j: e.tensor_add(
                            acc[:, ST * m + 512 * j:ST * m + 512 * j + 512],
                            acc[:, ST * m + 512 * j:ST * m + 512 * j + 512],
                            x1rl[:, 512 * m:512 * m + 512]),
                            waits=[m_rl, m_acc[(m, j)]], mark=True)
                    prev_rl = Mark("m_vector", max(mk.value for mk in m_f))
                    ost.dma(lambda e, j=j: e.dma_start(
                        out=dout[:, :].rearrange("(c p) s -> p c s", p=128)
                            [:, :, 512 * j:512 * j + 512],
                        in_=acc[:, :].rearrange("p (c s) -> p c s", c=8)
                            [:, :, 512 * j:512 * j + 512]),
                        waits=m_f)

                p.do("sync", lambda e: e.nop(),
                     waits=ost.final_marks() + spl.final_marks())

        p.finalize(block)
        p.close()
    st.close()
    return nc


def _host_prep(inputs):
    gam = _gammas()
    wq = np.asarray(inputs["L0_attn_wq"])
    wk = np.asarray(inputs["L0_attn_wk"])
    wv = np.asarray(inputs["L0_attn_wv"])
    def pack_qkv(w):
        out = np.zeros((128, 512), np.float32)
        for c in range(8):
            out[0:64, 64 * c:64 * c + 64] = w[2 * c]
            out[64:128, 64 * c:64 * c + 64] = w[2 * c + 1]
        return out
    bf = lambda x: np.ascontiguousarray(x).astype(ml_dtypes.bfloat16)
    gns = np.asarray(inputs["L0_attn_gn_scale"]).reshape(8, 128).T.copy()
    gnb = np.asarray(inputs["L0_attn_gn_bias"]).reshape(8, 128).T.copy()
    lng = np.zeros((128, 8), np.float32)
    for c in range(8):
        lng[0:64, c] = np.log(gam[2 * c])
        lng[64:128, c] = np.log(gam[2 * c + 1])
    sel16 = np.zeros((128, 128), np.float32)
    for c in range(8):
        sel16[0:64, 16 * c + 2 * c] = 1.0
        sel16[64:128, 16 * c + 2 * c + 1] = 1.0
    selb = np.zeros((16, 1024), np.float32)
    for c in range(8):
        selb[2 * c, 128 * c:128 * c + 64] = 1.0
        selb[2 * c + 1, 128 * c + 64:128 * c + 128] = 1.0
    tri = (np.arange(128)[None, :] >= np.arange(128)[:, None]).astype(np.float32)
    masks = np.zeros((128, 1280), np.float32)
    moffs = (0, 512, 896, 1152)
    for kxx in range(4):
        w = 512 - 128 * kxx
        masks[:, moffs[kxx]:moffs[kxx] + 128] = tri
        masks[:, moffs[kxx] + 128:moffs[kxx] + w] = 1.0
    gmn4 = np.zeros((128, 64), np.float64)
    pp = np.arange(128)
    for n in range(16):
        for u in range(4):
            gmn4[:, 4 * n + u] = gam[n] ** (-(128.0 * u + pp))
    gmn4 = gmn4.astype(np.float32)
    rout = np.concatenate([np.asarray(inputs["L0_ffn_router_w"]),
                           np.asarray(inputs["L0_ffn_shared_gate_w"])], axis=1)
    return {
        "wq": pack_qkv(wq), "wk": pack_qkv(wk), "wv": pack_qkv(wv),
        "w1a": np.ascontiguousarray(inputs["L0_attn_w1"], dtype=np.float32),
        "w2a": np.ascontiguousarray(inputs["L0_attn_w2"], dtype=np.float32),
        "gns": np.ascontiguousarray(gns, dtype=np.float32),
        "gnb": np.ascontiguousarray(gnb, dtype=np.float32),
        "lng": lng,
        "rout": np.ascontiguousarray(rout, dtype=np.float32),
        "we1": np.ascontiguousarray(inputs["L0_ffn_w1"], dtype=np.float32),
        "we2": np.ascontiguousarray(inputs["L0_ffn_w2"], dtype=np.float32),
        "wgu": np.ascontiguousarray(inputs["L0_ffn_shared_gate_up"], dtype=np.float32),
        "wdn": np.ascontiguousarray(inputs["L0_ffn_shared_down"], dtype=np.float32),
        "sel16": bf(sel16), "selb": bf(selb),
        "ones1": bf(np.ones((1, 128), np.float32)),
        "onesc": bf(np.full((128, 1), 1.0 / 1024.0, np.float32)),
        "masks": bf(masks),
        "gmn4": np.ascontiguousarray(gmn4),
        "eye128": np.ascontiguousarray(np.eye(128, dtype=ml_dtypes.bfloat16)),
    }


def kernel(**inputs):
    common = _host_prep(inputs)
    x = np.asarray(inputs["L0_attn_x"])
    nc = build(nj=4)
    in_maps = []
    for i in range(8):
        m = dict(common)
        m["x"] = np.ascontiguousarray(x[i].T)
        in_maps.append(m)
    res = run_bass_kernel_spmd(nc, in_maps, core_ids=list(range(8)))
    out = np.stack([np.asarray(res.results[i]["out"], dtype=np.float32).T
                    for i in range(8)], axis=0)
    return out
